# version 13
# baseline (speedup 1.0000x reference)
import numpy as np
from contextlib import ExitStack

import concourse.bass as bass
import concourse.mybir as mybir
from concourse.bass_utils import run_bass_kernel_spmd

F32 = mybir.dt.float32
BF16 = mybir.dt.bfloat16
U8 = mybir.dt.uint8
ALU = mybir.AluOpType
ACTF = mybir.ActivationFunctionType

B, S, D = 4, 8192, 1024
T = 4096
DC = 8
H = 16
DF, DFC = 4096, 32
NT = 512
KT = T // NT

C_S = 9.94969
C_ES = float(np.exp(np.float64(np.float32(C_S))))
LN4 = 1.3862943611198906
FMAX = 3.4028235e38
MAGIC = 12582912.0
INV2PI = 0.15915494309189535
TWOPI = 6.283185307179586
PIH = 1.5707963267948966
LN_EPS = 1e-5
AG_W = 3104
EXP_INF = 88.72284
EXP_CLAMP = 88.7228
BIG_C = 16384.0
THF = 1.8443e19
C1 = float(0.25 * 2.0 ** -64)
C_RESC = float(np.float32(C_ES) * np.float32(2.0 ** 64))
WKV_SC = float(2.0 ** -32)
WKV_CL = float(np.float32(3.3e38) * 2.0 ** -32)
RW_SC = float(2.0 ** 25)
LN2_64 = float(np.float32(64.0 * np.log(2.0)))
SC70 = float(2.0 ** -70)
P70 = float(2.0 ** 70)
P76 = float(2.0 ** 76)
SC64M = float(2.0 ** -64)
TQ_S = float(np.float32(FMAX) * 2.0 ** -140)
EPS_S = float(np.float32(LN_EPS) * 2.0 ** -64)
LNB_A = float(-32.0 * np.log(2.0))
LNB_B = float(-102.0 * np.log(2.0))
B1C = float(2.0 ** 58)
TBL = float(1e19 * 2.0 ** -140)
C17 = float(np.float32(3.3e38) * 2.0 ** -70)
CY_S = 1.0e6
VAR_TH = 1.0


class Cnt:
    def __init__(self):
        self.c = {}

    def inc(self, s, n):
        self.c[s.name] = self.c.get(s.name, 0) + n
        return self.c[s.name]

    def val(self, s):
        return self.c.get(s.name, 0)


_DRAIN_OPS = {
    "memset", "memzero", "tensor_copy", "tensor_tensor", "tensor_add",
    "tensor_sub", "tensor_mul", "tensor_max", "tensor_scalar",
    "tensor_single_scalar", "tensor_scalar_add", "tensor_scalar_sub",
    "tensor_scalar_mul", "tensor_scalar_max", "tensor_scalar_min",
    "tensor_relu", "tensor_reduce", "reduce_max", "reduce_sum",
    "scalar_tensor_tensor", "tensor_tensor_scan", "select", "copy_predicated",
    "reciprocal", "transpose", "activation", "sign", "square", "sqrt",
    "mul", "add", "copy", "iota", "affine_select",
}


class _Draining:
    def __init__(self, eng):
        self._e = eng

    def __getattr__(self, name):
        f = getattr(self._e, name)
        if name in _DRAIN_OPS:
            def g(*a, **k):
                self._e.drain()
                return f(*a, **k)
            return g
        return f


def _ap_box(ap):
    off = ap.offset
    span = 0
    for stride, num in ap.ap:
        if num > 1:
            span += (num - 1) * abs(stride)
    return (ap.tensor.name, off, off + span)


def _boxes_overlap(a, b):
    return a[0] == b[0] and a[1] <= b[2] and b[1] <= a[2]


class _Chained:

    def __init__(self, eng, sem, cnt):
        self._e = eng
        self._sem = sem
        self._cnt = cnt
        self._hist = []
        self._waited = 0

    def __getattr__(self, name):
        f = getattr(self._e, name)
        if name not in _DRAIN_OPS:
            return f

        def g(*a, **k):
            aps = [x for x in a if isinstance(x, bass.AP)]
            aps += [x for x in k.values() if isinstance(x, bass.AP)]
            wbox = _ap_box(aps[0]) if aps else None
            rboxes = [_ap_box(x) for x in aps[1:]]
            need = 0
            for seq, kind, box in self._hist:
                if seq <= self._waited:
                    continue
                if kind == "W":
                    if (wbox and _boxes_overlap(box, wbox)) or \
                       any(_boxes_overlap(box, rb) for rb in rboxes):
                        need = max(need, seq)
                elif wbox and _boxes_overlap(box, wbox):
                    need = max(need, seq)
            if need > self._waited:
                self._e.wait_ge(self._sem, need)
                self._waited = need
                self._hist = [h for h in self._hist if h[0] > need]
            res = f(*a, **k)
            res.then_inc(self._sem, 1)
            seq = self._cnt.inc(self._sem, 1)
            if wbox:
                self._hist.append((seq, "W", wbox))
            for rb in rboxes:
                self._hist.append((seq, "R", rb))
            if len(self._hist) > 256:
                self._hist = self._hist[-192:]
            return res
        return g


def build_program(debug=False, upto="out"):
    nc = bass.Bass("TRN2", target_bir_lowering=False)
    es = ExitStack()
    cnt = Cnt()
    sync, ten = nc.sync, nc.tensor
    gp = _Draining(nc.gpsimd)

    def din(name, shape):
        return nc.dram_tensor(name, shape, F32, kind="ExternalInput").ap()

    x_in = din("x_in", [D, T])
    wkT = din("wkT", [D, D]); wvT = din("wvT", [D, D]); wkaT = din("wkaT", [D, D])
    w1 = din("w1", [D, DF]); w2 = din("w2", [DF, D])
    wkeT = din("wkeT", [D, D]); wvaT = din("wvaT", [D, D])
    wreT = din("wreT", [D, D]); woT = din("woT", [D, D])
    l1fw = din("l1fw", [D, D])
    v_bk = din("bk", [D]); v_bv = din("bv", [D]); v_bka = din("bka", [D])
    v_g = din("gamma", [D]); v_be = din("beta", [D])
    v_b1 = din("b1", [DF]); v_b2 = din("b2", [D])
    v_th = din("theta", [DF]); v_qr = din("qwr", [DF]); v_qi = din("qwi", [DF])
    v_mk = din("mixk", [D]); v_mv = din("mixv", [D]); v_mr = din("mixr", [D])
    v_td = din("td", [H]); v_tf = din("tf", [H])
    selrow = din("selrow", [1, 8])
    t0c = din("t0c", [1, 1])
    hadc = din("hadc", [D, D])

    out_t = nc.dram_tensor("out", [D, T], F32, kind="ExternalOutput").ap()

    BFD = mybir.dt.bfloat16
    keys_s = nc.dram_tensor("keys_s", [D, T], F32).ap()
    kw_s = nc.dram_tensor("kw_s", [D, T], BFD).ap()
    pool_s = nc.dram_tensor("pool_s", [D, T], BFD).ap()
    x2_s = nc.dram_tensor("x2_s", [D, T], F32).ap()
    vals_s = nc.dram_tensor("vals_s", [D, T], F32).ap()
    x3_s = nc.dram_tensor("x3_s", [D, T], F32).ap()
    x4_s = nc.dram_tensor("x4_s", [D, T], F32).ap()
    ekv_s = nc.dram_tensor("ekv_s", [D, T], F32).ap()
    r_s = nc.dram_tensor("r_s", [D, T], BFD).ap()
    h_s = nc.dram_tensor("h_s", [D, T], BFD).ap()
    w1b_s = nc.dram_tensor("w1b_s", [8, 128, DF], BFD).ap()
    se_s = nc.dram_tensor("se_s", [H, T], F32).ap()
    w2b_s = nc.dram_tensor("w2b_s", [DC, 128, DF], BFD).ap()
    wqT_s = nc.dram_tensor("wqT_s", [D, D], BFD).ap()
    ag_in = [nc.dram_tensor(f"ag_in{i}", [1, AG_W], F32).ap() for i in range(4)]
    ag_out = [nc.dram_tensor(f"ag_out{i}", [8, AG_W], F32).ap() for i in range(4)]

    dbg = {}
    if debug:
        for nm in ["d_x2", "d_x3", "d_x4", "d_keys", "d_ekv", "d_num", "d_wkv",
                   "d_att"]:
            dbg[nm] = nc.dram_tensor(nm, [D, T], F32, kind="ExternalOutput").ap()
        dbg["d_rdn"] = nc.dram_tensor("d_rdn", [16, T], F32, kind="ExternalOutput").ap()

    def hv(hbm, t0, n, c0=0, ncs=DC):
        return hbm.rearrange("(c p) t -> p c t", p=128)[:, c0:c0 + ncs, t0:t0 + n]

    def sv(tile, per_c, c0, ncs, j0, n):
        return tile[:].rearrange("p (c n) -> p c n", n=per_c)[:, c0:c0 + ncs, j0:j0 + n]

    def sbuf(name, shape, dtype):
        return es.enter_context(nc.sbuf_tensor(name, shape, dtype))

    AR = 3 * DC * D
    arH = sbuf("arH", [128, AR], BF16)
    arK = sbuf("arK", [128, AR], BF16)
    tX = sbuf("tX", [128, DC * NT], F32)
    tY = sbuf("tY", [128, DC * NT], F32)
    tZ = sbuf("tZ", [128, DC * NT], F32)
    tB16 = sbuf("tB16", [128, DC * NT], BF16)
    tB16b = sbuf("tB16b", [128, DC * NT], BF16)
    sc = sbuf("sc", [128, DC * NT], F32)
    svA = sbuf("svA", [128, NT], F32)
    svB = sbuf("svB", [128, NT], F32)
    mu8 = sbuf("mu8", [128, DC * NT // 2], U8)

    pat = sbuf("pat", [128, 64], F32)
    PG, PBE, PBK, PBV, PBKA = 0, 8, 16, 24, 32
    PMK, PMV, PMR = 40, 48, 56
    patB = sbuf("patB", [128, 64], F32)
    QLA, QNLA, QEU, QB2 = 0, 8, 16, 24
    patDF = sbuf("patDF", [128, 4 * DFC], F32)
    PA, PB, PB1C, PB1S = 0, DFC, 2 * DFC, 3 * DFC
    patA1 = sbuf("patA1", [128, 2 * DFC], F32)
    binv = sbuf("binv", [128, DC * KT], F32)
    bpow = sbuf("bpow", [128, DC * KT], F32)
    b16k = sbuf("b16k", [16, 2 * KT], F32)
    la16 = sbuf("la16", [16, 8], F32)
    iot = sbuf("iot", [128, NT], F32)
    iotk = sbuf("iotk", [128, KT], F32)
    onesb = sbuf("onesb", [128, 1], BF16)
    ones2d = sbuf("ones2d", [128, 128], F32)
    zerosb = sbuf("zerosb", [128, NT], BF16)
    selc = sbuf("selc", [128, DC * 16], F32)
    selT = sbuf("selT", [16, D], F32)
    ident = sbuf("ident", [128, 128], F32)
    identb = sbuf("identb", [128, 128], BF16)
    onehot8 = sbuf("onehot8", [8, 1], F32)
    carry = sbuf("carry", [128, 48], F32)
    bmt = sbuf("bmt", [128, NT], F32)
    bigk = sbuf("bigk", [128, 8], F32)
    maxP = sbuf("maxP", [128, 8], F32)
    maxN = sbuf("maxN", [128, 8], F32)
    chainP = sbuf("chainP", [128, 8], F32)
    chainN = sbuf("chainN", [128, 8], F32)
    c16 = sbuf("c16", [16, 8], F32)
    lastK = sbuf("lastK", [128, 8], F32)
    lastP = sbuf("lastP", [128, 8], F32)
    lastA = sbuf("lastA", [128, 8], F32)
    sumkw = sbuf("sumkw", [128, 8], F32)
    scr8 = sbuf("scr8", [128, 8], F32)
    scr8b = sbuf("scr8b", [128, 8], F32)
    hrow = sbuf("hrow", [8, 128], F32)
    t0bc = sbuf("t0bc", [128, 1], F32)
    c_eps = sbuf("c_eps", [128, 1], F32)
    c_epsS = sbuf("c_epsS", [128, 1], F32)
    c_lnA = sbuf("c_lnA", [128, 1], F32)
    c_lnB = sbuf("c_lnB", [128, 1], F32)
    cse = sbuf("cse", [16, NT], F32)
    ccb = sbuf("ccb", [16, NT], F32)
    ct1 = sbuf("ct1", [16, NT], F32)
    ct2 = sbuf("ct2", [16, NT], F32)
    crd = sbuf("crd", [16, NT], F32)
    mu16 = sbuf("mu16", [16, NT], U8)

    ps0 = es.enter_context(nc.psum_tensor("ps0", [128, NT], F32))
    ps1 = es.enter_context(nc.psum_tensor("ps1", [128, NT], F32))
    ps2 = es.enter_context(nc.psum_tensor("ps2", [128, NT], F32))
    ps3 = es.enter_context(nc.psum_tensor("ps3", [128, NT], F32))
    psBig = es.enter_context(nc.psum_tensor("psBig", [128, 2 * NT], F32))
    psT = es.enter_context(nc.psum_tensor("psT", [128, NT], BF16))
    psAB = [ps0, ps1]

    s_ld = es.enter_context(nc.semaphore("s_ld"))
    s_gd = es.enter_context(nc.semaphore("s_gd"))
    s_st = es.enter_context(nc.semaphore("s_st"))
    s_ag = es.enter_context(nc.semaphore("s_ag"))
    s_pe = es.enter_context(nc.semaphore("s_pe"))
    s_ve = es.enter_context(nc.semaphore("s_ve"))
    s_se = es.enter_context(nc.semaphore("s_se"))
    s_gp = es.enter_context(nc.semaphore("s_gp"))
    s_h = es.enter_context(nc.semaphore("s_h"))
    s_pc = es.enter_context(nc.semaphore("s_pc"))
    s_sl0 = es.enter_context(nc.semaphore("s_sl0"))
    s_sl1 = es.enter_context(nc.semaphore("s_sl1"))
    s_la = es.enter_context(nc.semaphore("s_la"))

    vec = _Chained(nc.vector, s_ve, cnt)
    sca = _Chained(nc.scalar, s_se, cnt)

    def bar():
        for s in (s_ld, s_st, s_gd, s_h, s_pc, s_sl0, s_sl1, s_la):
            if cnt.val(s):
                sync.wait_ge(s, cnt.val(s))
        nc.all_engine_barrier()
        for w in (vec, sca):
            w._waited = cnt.val(w._sem)
            w._hist = []

    def tbar():
        sync.wait_ge(s_ve, cnt.val(s_ve))
        sync.wait_ge(s_se, cnt.val(s_se))
        sync.wait_ge(s_pe, cnt.val(s_pe))
        for s in (s_st, s_h):
            if cnt.val(s):
                sync.wait_ge(s, cnt.val(s))

    def vdone():
        pass

    W = {}

    with es:
        def load_pat(dst, col0, v, ncols):
            sync.dma_start(out=dst[:, col0:col0 + ncols],
                           in_=v.rearrange("(c p) -> p c", p=128),
                           allow_slow_non_contiguous=True).then_inc(s_ld, 16)
            cnt.inc(s_ld, 16)

        load_pat(pat, PG, v_g, 8); load_pat(pat, PBE, v_be, 8)
        load_pat(pat, PBK, v_bk, 8); load_pat(pat, PBV, v_bv, 8)
        load_pat(pat, PBKA, v_bka, 8)
        load_pat(pat, PMK, v_mk, 8); load_pat(pat, PMV, v_mv, 8)
        load_pat(pat, PMR, v_mr, 8)
        load_pat(patB, QB2, v_b2, 8)
        load_pat(patA1, 0, v_b1, DFC)
        load_pat(patDF, PA, v_th, DFC)
        load_pat(patDF, PB, v_qr, DFC)
        load_pat(patDF, PB1C, v_qi, DFC)
        sync.dma_start(out=la16[:, 6:7], in_=v_td[:, None]).then_inc(s_ld, 16); cnt.inc(s_ld, 16)
        sync.dma_start(out=la16[:, 7:8], in_=v_tf[:, None]).then_inc(s_ld, 16); cnt.inc(s_ld, 16)
        sync.dma_start(out=onehot8[:], in_=selrow.rearrange("o r -> r o")).then_inc(s_ld, 16)
        cnt.inc(s_ld, 16)
        sync.dma_start(out=t0bc[:], in_=bass.AP(t0c.tensor, 0, [[0, 128], [1, 1]]),
                       allow_slow_non_contiguous=True).then_inc(s_ld, 16)
        cnt.inc(s_ld, 16)

        def wload(dst, col0, srcT, cols=D):
            gp.dma_start(
                out=dst[:].rearrange("p (c e) -> p c e", e=cols)[:, col0 // cols:(col0 // cols) + srcT.shape[0] // 128, :],
                in_=srcT.rearrange("(c p) e -> p c e", p=128)).then_inc(s_gd, 16)
            cnt.inc(s_gd, 16)

        wload(arK, 0, wkT)
        wload(arK, 8 * D, wkaT)
        gp.memset(zerosb[:], 0.0)
        gp.memset(c_eps[:], LN_EPS)
        gp.memset(c_epsS[:], EPS_S)
        gp.memset(c_lnA[:], LNB_A)
        gp.memset(c_lnB[:], LNB_B)
        gp.memset(ones2d[:], 1.0)
        gp.memset(onesb[:], 1.0)
        gp.memset(carry[:], 0.0)
        gp.memset(c16[:], 0.0)
        gp.memset(lastK[:], 0.0)
        gp.memset(lastP[:], 0.0)
        gp.memset(lastA[:], 0.0)
        gp.memset(sumkw[:], 0.0)
        gp.iota(iot[:], [[1, NT]], base=0, channel_multiplier=0,
                allow_small_or_imprecise_dtypes=True)
        gp.iota(iotk[:], [[1, KT]], base=0, channel_multiplier=0,
                allow_small_or_imprecise_dtypes=True)
        def gstep(op):
            op.then_inc(s_gp, 1)
            cnt.inc(s_gp, 1)
            gp.wait_ge(s_gp, cnt.val(s_gp))

        gstep(gp.memset(ident[:], 1.0))
        gstep(gp.affine_select(ident[:], ident[:], [[1, 128]], ALU.is_equal, 0.0,
                               base=0, channel_multiplier=-1))
        gstep(gp.memset(identb[:], 1.0))
        gstep(gp.affine_select(identb[:], identb[:], [[1, 128]], ALU.is_equal, 0.0,
                               base=0, channel_multiplier=-1))
        gstep(gp.memset(selT[:], 1.0))
        gstep(gp.affine_select(selT[:], selT[:], [[1, D]], ALU.is_ge, 0.0,
                               base=0, channel_multiplier=-64))
        gstep(gp.affine_select(selT[:], selT[:], [[-1, D]], ALU.is_ge, 0.0,
                               base=63, channel_multiplier=64))
        gstep(gp.memset(selc[:], 1.0))
        for c in range(DC):
            slc = selc[:, c * 16:(c + 1) * 16]
            gstep(gp.affine_select(slc, slc, [[-64, 16]], ALU.is_ge, 0.0,
                                   base=128 * c, channel_multiplier=1))
            gstep(gp.affine_select(slc, slc, [[64, 16]], ALU.is_ge, 0.0,
                                   base=63 - 128 * c, channel_multiplier=-1))
        gp_done = cnt.val(s_gd)
        gp_done2 = cnt.val(s_gp)

        for e in (vec, sca, ten):
            e.wait_ge(s_ld, cnt.val(s_ld))
            e.wait_ge(s_gd, gp_done)
            e.wait_ge(s_gp, gp_done2)
        sca.activation(la16[:, 5:6], la16[:, 6:7], ACTF.Exp)
        sca.activation(la16[:, 4:5], la16[:, 5:6], ACTF.Exp, scale=-1.0)
        sca.activation(la16[:, 0:1], la16[:, 4:5], ACTF.Ln)
        si = sca.activation(la16[:, 2:3], la16[:, 7:8], ACTF.Exp)
        vec.wait_ge(s_se, cnt.val(s_se))
        vec.tensor_scalar(la16[:, 1:2], la16[:, 0:1], -1.0, None, ALU.mult)
        th = patDF[:, PA:PA + DFC]; qwr = patDF[:, PB:PB + DFC]
        qwi = patDF[:, PB1C:PB1C + DFC]
        thr = patA1[:, DFC:2 * DFC]
        vec.tensor_scalar(thr, th, INV2PI, MAGIC, ALU.mult, ALU.add)
        vec.tensor_scalar(thr, thr, MAGIC, None, ALU.subtract)
        vec.scalar_tensor_tensor(thr, thr, -TWOPI, th, ALU.mult, ALU.add)
        sth = tX[:, 0:DFC]; cth = tX[:, DFC:2 * DFC]
        s2 = tX[:, 2 * DFC:3 * DFC]
        vec.tensor_scalar(s2, th, 1.0, PIH, ALU.mult, ALU.add)
        vec.tensor_scalar(tX[:, 3 * DFC:4 * DFC], s2, INV2PI, MAGIC, ALU.mult, ALU.add)
        vec.tensor_scalar(tX[:, 3 * DFC:4 * DFC], tX[:, 3 * DFC:4 * DFC], MAGIC,
                          None, ALU.subtract)
        vec.scalar_tensor_tensor(s2, tX[:, 3 * DFC:4 * DFC], -TWOPI, s2,
                                 ALU.mult, ALU.add)
        vdone()
        sca.wait_ge(s_ve, cnt.val(s_ve))
        sca.activation(sth, thr, ACTF.Sin)
        si = sca.activation(cth, s2, ACTF.Sin)
        vec.wait_ge(s_se, cnt.val(s_se))
        qp = tY[:, 0:DFC]; scrD = tY[:, DFC:2 * DFC]
        vec.tensor_tensor(qp, cth, qwr, ALU.mult)
        vec.tensor_tensor(scrD, sth, qwi, ALU.mult)
        vec.tensor_tensor(qp, qp, scrD, ALU.add)
        vec.tensor_tensor(scrD, qp, qwi, ALU.mult)
        vec.tensor_scalar(scrD, scrD, 0.1, None, ALU.mult)
        vec.tensor_tensor(tY[:, 2 * DFC:3 * DFC], scrD, sth, ALU.mult)
        vec.tensor_tensor(tY[:, 3 * DFC:4 * DFC], scrD, cth, ALU.mult)
        vec.tensor_copy(patDF[:, PA:PA + DFC], tY[:, 2 * DFC:3 * DFC])
        vec.tensor_copy(patDF[:, PB:PB + DFC], tY[:, 3 * DFC:4 * DFC])
        vec.tensor_scalar(patDF[:, PB1C:PB1C + DFC], patA1[:, 0:DFC], 0.1, PIH,
                          ALU.mult, ALU.add)
        vec.tensor_scalar(patDF[:, PB1S:PB1S + DFC], patA1[:, 0:DFC], 0.1, None,
                          ALU.mult)
        vec.tensor_scalar(pat[:, PBKA:PBKA + 8], pat[:, PBKA:PBKA + 8],
                          1.0 / 32.0, None, ALU.mult)
        vdone()

        ten.wait_ge(s_ve, cnt.val(s_ve))
        for (colsrc, dstcol) in [(0, QLA), (1, QNLA), (2, QEU)]:
            for c in range(DC):
                mi = ten.matmul(ps2[:, c:c + 1], selT[:, c * 128:(c + 1) * 128],
                                la16[:, colsrc:colsrc + 1], start=True, stop=True)
            mi.then_inc(s_pe, 1); cnt.inc(s_pe, 1)
            vec.wait_ge(s_pe, cnt.val(s_pe))
            vec.tensor_copy(patB[:, dstcol:dstcol + 8], ps2[:, 0:8])
            vdone()
            ten.wait_ge(s_ve, cnt.val(s_ve))

        for c in range(DC):
            nla = patB[:, QNLA + c:QNLA + c + 1]
            la_ = patB[:, QLA + c:QLA + c + 1]
            vec.tensor_tensor(scr8[:, 0:1], nla, t0bc[:], ALU.mult)
            vec.tensor_scalar(scr8[:, 1:2], nla, 1.0, LN4, ALU.mult, ALU.add)
            vec.tensor_tensor(scr8[:, 0:1], scr8[:, 0:1], scr8[:, 1:2], ALU.add)
            vec.tensor_scalar(scr8[:, 2:3], nla, float(NT), None, ALU.mult)
            vec.tensor_scalar(svA[:, 0:KT], iotk[:, 0:KT], scr8[:, 2:3], None, ALU.mult)
            vec.tensor_scalar(binv[:, c * KT:(c + 1) * KT], svA[:, 0:KT],
                              scr8[:, 0:1], None, ALU.add)
            vec.tensor_tensor(scr8[:, 3:4], la_, t0bc[:], ALU.mult)
            vec.tensor_scalar(scr8[:, 3:4], scr8[:, 3:4], 1.0, -float(C_S), ALU.mult, ALU.add)
            vec.tensor_scalar(scr8[:, 4:5], la_, float(NT), None, ALU.mult)
            vec.tensor_scalar(svA[:, 0:KT], iotk[:, 0:KT], scr8[:, 4:5], None, ALU.mult)
            vec.tensor_scalar(bpow[:, c * KT:(c + 1) * KT], svA[:, 0:KT],
                              scr8[:, 3:4], None, ALU.add)
        vec.tensor_tensor(la16[:, 3:4], la16[:, 1:2], t0bc[0:16, :], ALU.mult)
        vec.tensor_scalar(la16[:, 6:7], la16[:, 1:2], 1.0, LN4, ALU.mult, ALU.add)
        vec.tensor_tensor(la16[:, 3:4], la16[:, 3:4], la16[:, 6:7], ALU.add)
        vec.tensor_scalar(la16[:, 6:7], la16[:, 1:2], float(NT), None, ALU.mult)
        vec.tensor_scalar(ct1[:, 0:KT], iotk[0:16, 0:KT], la16[:, 6:7], None, ALU.mult)
        vec.tensor_scalar(b16k[:, 0:KT], ct1[:, 0:KT], la16[:, 3:4], None, ALU.add)
        vec.tensor_tensor(la16[:, 3:4], la16[:, 0:1], t0bc[0:16, :], ALU.mult)
        vec.tensor_scalar(la16[:, 3:4], la16[:, 3:4], 1.0, -float(C_S), ALU.mult, ALU.add)
        vec.tensor_scalar(la16[:, 6:7], la16[:, 0:1], float(NT), None, ALU.mult)
        vec.tensor_scalar(ct1[:, 0:KT], iotk[0:16, 0:KT], la16[:, 6:7], None, ALU.mult)
        vec.tensor_scalar(b16k[:, KT:2 * KT], ct1[:, 0:KT], la16[:, 3:4], None, ALU.add)
        vec.memset(tZ[:], 0.0)
        vdone()
        sync.wait_ge(s_ve, cnt.val(s_ve))
        for i in range(4):
            sync.dma_start(out=ag_in[i], in_=tZ[0:1, 0:AG_W]).then_inc(s_st, 16)
            cnt.inc(s_st, 16)
        vec.tensor_scalar(bigk[:, 0:KT], iotk[:, 0:KT], -float(NT), BIG_C,
                          ALU.mult, ALU.add)
        vec.tensor_scalar(bigk[:, 0:KT], bigk[:, 0:KT], t0bc[:], None,
                          ALU.subtract)
        vdone()
        bar()

        evac_cnt = {"v": 0}

        def ln_stats_norm(n, y_tile, out_tile, armored=False):
            if armored:
                vec.tensor_scalar(sc[:, 0:DC * n], y_tile[:, 0:DC * n],
                                  B1C, -B1C, ALU.min, ALU.max)
                vdone()
                sca.wait_ge(s_ve, cnt.val(s_ve))
                si = sca.activation(tB16[:, 0:DC * n], sc[:, 0:DC * n], ACTF.Square)
                ten.wait_ge(s_se, cnt.val(s_se))
                for c in range(DC):
                    mi = ten.matmul(ps2[64:65, 0:n], onesb[:, 0:1],
                                    tB16[:, c * n:(c + 1) * n],
                                    start=(c == 0), stop=(c == DC - 1))
                mi.then_inc(s_pe, 1); cnt.inc(s_pe, 1)
                vec.wait_ge(s_pe, cnt.val(s_pe))
                vec.tensor_scalar(sc[:, 0:DC * n], y_tile[:, 0:DC * n],
                                  SC70, None, ALU.mult)
                vec.tensor_copy(tB16b[:, 0:DC * n], sc[:, 0:DC * n])
                vdone()
                sca.wait_ge(s_ve, cnt.val(s_ve))
                sca.wait_ge(s_pe, cnt.val(s_pe))
                si = sca.activation(tB16[:, 0:DC * n], sc[:, 0:DC * n], ACTF.Square)
            else:
                vec.tensor_copy(tB16b[:, 0:DC * n], y_tile[:, 0:DC * n])
                vdone()
                sca.wait_ge(s_ve, cnt.val(s_ve))
                si = sca.activation(tB16[:, 0:DC * n], y_tile[:, 0:DC * n], ACTF.Square)
            ten.wait_ge(s_ve, cnt.val(s_ve))
            ten.wait_ge(s_se, cnt.val(s_se))
            for c in range(DC):
                mi = ten.matmul(ps2[0:1, 0:n], onesb[:, 0:1],
                                tB16b[:, c * n:(c + 1) * n],
                                start=(c == 0), stop=(c == DC - 1))
            for c in range(DC):
                mi = ten.matmul(ps2[32:33, 0:n], onesb[:, 0:1],
                                tB16[:, c * n:(c + 1) * n],
                                start=(c == 0), stop=(c == DC - 1))
            mi.then_inc(s_pe, 1); cnt.inc(s_pe, 1)
            vec.wait_ge(s_pe, cnt.val(s_pe))
            mu = svA[0:1, 0:n]; var = svB[0:1, 0:n]
            rstd = ct1[0:1, 0:n]; c0r = ct2[0:1, 0:n]; msk = crd[0:1, 0:n]
            vec.tensor_scalar(mu, ps2[0:1, 0:n], 1.0 / D, None, ALU.mult)
            vec.tensor_scalar(var, ps2[32:33, 0:n], 1.0 / D, None, ALU.mult)
            if armored:
                aarg = sc[0:1, 0:n]; barg = sc[0:1, n:2 * n]
                mbl = sc[0:1, 2 * n:3 * n]; muc = sc[0:1, 3 * n:4 * n]
                vlo = sc[0:1, 4 * n:5 * n]; rb = tZ[0:1, 0:n]
                vec.tensor_scalar(msk, ps2[32:33, 0:n], TQ_S, None, ALU.is_lt)
                vec.tensor_tensor(c0r, mu, mu, ALU.mult)
                vec.tensor_tensor(var, var, c0r, ALU.subtract)
                vec.tensor_scalar(muc, mu, P70, B1C, ALU.mult, ALU.min)
                vec.tensor_scalar(muc, muc, -B1C, None, ALU.max)
                vec.tensor_tensor(muc, muc, muc, ALU.mult)
                vec.scalar_tensor_tensor(vlo, ps2[64:65, 0:n], 1.0 / D,
                                         muc, ALU.mult, ALU.subtract)
                vec.tensor_scalar(aarg, vlo, 1.5e19, 0.0, ALU.min, ALU.max)
                vec.tensor_scalar(barg, var, SC64M, 1e-30, ALU.mult, ALU.max)
                vec.tensor_scalar(mbl, var, TBL, None, ALU.is_gt)
                vdone()
                sca.wait_ge(s_ve, cnt.val(s_ve))
                sca.activation(rstd, aarg, ACTF.Ln, bias=c_eps[0:1, 0:1])
                sca.activation(rstd, rstd, ACTF.Exp, scale=-0.5)
                sca.activation(rb, barg, ACTF.Ln)
                si = sca.activation(rb, rb, ACTF.Exp, scale=-0.5,
                                    bias=c_lnB[0:1, 0:1])
                vec.wait_ge(s_se, cnt.val(s_se))
                vec.tensor_tensor(rb, rb, rstd, ALU.subtract)
                vec.tensor_tensor(rb, rb, mbl, ALU.mult)
                vec.tensor_tensor(rstd, rstd, rb, ALU.add)
                vec.tensor_tensor(rstd, rstd, msk, ALU.mult)
                vec.scalar_tensor_tensor(c0r, mu, -1.0, rstd, ALU.mult, ALU.mult)
                vec.tensor_scalar(c0r, c0r, C17, -C17, ALU.min, ALU.max)
                vec.tensor_scalar(c0r, c0r, P70, None, ALU.mult)
                vdone()
            else:
                vec.tensor_tensor(msk, mu, mu, ALU.mult)
                vec.tensor_tensor(var, var, msk, ALU.subtract)
                vdone()
                sca.wait_ge(s_ve, cnt.val(s_ve))
                sca.activation(rstd, var, ACTF.Ln, bias=c_eps[0:1, 0:1])
                si = sca.activation(rstd, rstd, ACTF.Exp, scale=-0.5)
                vec.wait_ge(s_se, cnt.val(s_se))
                vec.tensor_scalar(msk, var, 0.0, 0.0, ALU.mult, ALU.is_equal)
                vec.tensor_tensor(rstd, rstd, msk, ALU.mult)
                vec.scalar_tensor_tensor(c0r, mu, -1.0, rstd, ALU.mult, ALU.mult)
                vdone()
            ten.wait_ge(s_ve, cnt.val(s_ve))
            ten.matmul(ps3[:, 0:n], ones2d[0:1, :], rstd, start=True, stop=True)
            mi = ten.matmul(psBig[:, 0:n], ones2d[0:1, :], c0r, start=True, stop=True)
            mi.then_inc(s_pe, 1); cnt.inc(s_pe, 1)
            vec.wait_ge(s_pe, cnt.val(s_pe))
            for c in range(DC):
                slq = slice(c * n, (c + 1) * n)
                vec.tensor_tensor(out_tile[:, slq], y_tile[:, slq], ps3[:, 0:n], ALU.mult)
                vec.tensor_tensor(out_tile[:, slq], out_tile[:, slq], psBig[:, 0:n], ALU.add)

        for hh in range(2):
            gp.dma_start(
                out=arH[:, 0:4 * DF].rearrange("p (c f) -> p c f", f=DF),
                in_=w1[hh * 512:(hh + 1) * 512, :].rearrange("(c p) f -> p c f", p=128)
            ).then_inc(s_gd, 16)
            cnt.inc(s_gd, 16)
            gp.wait_ge(s_gd, cnt.val(s_gd))
            for q in range(8):
                gp.dma_start(
                    out=w1b_s[q].rearrange("p (kc v) -> p kc v", v=NT)[:, hh * 4:(hh + 1) * 4, :],
                    in_=arH[:, 0:4 * DF].rearrange("p (c f) -> p c f", f=DF)[:, :, q * NT:(q + 1) * NT]
                ).then_inc(s_pc, 16)
                cnt.inc(s_pc, 16)
            gp.wait_ge(s_pc, cnt.val(s_pc))
        for hh in range(4):
            gp.dma_start(
                out=arH[:, 0:8 * D].rearrange("p (c f) -> p c f", f=D),
                in_=w2[hh * 1024:(hh + 1) * 1024, :].rearrange("(c p) f -> p c f", p=128)
            ).then_inc(s_gd, 16)
            cnt.inc(s_gd, 16)
            gp.wait_ge(s_gd, cnt.val(s_gd))
            for e in range(DC):
                gp.dma_start(
                    out=w2b_s[e].rearrange("p (fc j) -> p fc j", j=128)[:, hh * 8:(hh + 1) * 8, :],
                    in_=arH[:, 0:8 * D].rearrange("p (c f) -> p c f", f=D)[:, :, e * 128:(e + 1) * 128]
                ).then_inc(s_pc, 16)
                cnt.inc(s_pc, 16)
            gp.wait_ge(s_pc, cnt.val(s_pc))


        for k in range(KT):
            vec.wait_ge(s_st, cnt.val(s_st))
            sca.wait_ge(s_st, cnt.val(s_st))
            sync.dma_start(out=sv(tX, NT, 0, DC, 0, NT),
                           in_=hv(x_in, k * NT, NT)).then_inc(s_ld, 16)
            cnt.inc(s_ld, 16)
            vec.wait_ge(s_ld, cnt.val(s_ld))
            sca.wait_ge(s_ld, cnt.val(s_ld))
            ln_stats_norm(NT, tX, tY)
            for c in range(DC):
                slq = slice(c * NT, (c + 1) * NT)
                vec.tensor_scalar(tB16[:, slq],
                                  tY[:, slq], pat[:, PG + c:PG + c + 1],
                                  pat[:, PBE + c:PBE + c + 1], ALU.mult, ALU.add)
            vdone()
            sync.wait_ge(s_ve, cnt.val(s_ve))
            sync.dma_start(out=hv(h_s, k * NT, NT),
                           in_=sv(tB16, NT, 0, DC, 0, NT)).then_inc(s_h, 16)
            cnt.inc(s_h, 16)
            ten.wait_ge(s_ve, cnt.val(s_ve))
            se_b = cnt.val(s_se)
            for e in range(DC):
                bank = psAB[e % 2]
                if e >= 2:
                    ten.wait_ge(s_se, se_b + e - 1)
                else:
                    ten.wait_ge(s_se, se_b)
                for kc in range(DC):
                    mi = ten.matmul(bank[:, 0:NT],
                                    arK[:, kc * D + e * 128: kc * D + e * 128 + 128],
                                    tB16[:, kc * NT: kc * NT + NT],
                                    start=(kc == 0), stop=(kc == DC - 1))
                mi.then_inc(s_pe, 1); cnt.inc(s_pe, 1)
                sca.wait_ge(s_pe, cnt.val(s_pe))
                si = sca.activation(tZ[:, e * NT:(e + 1) * NT], bank[:, 0:NT],
                                    ACTF.Identity, bias=pat[:, PBK + e:PBK + e + 1])
            vec.wait_ge(s_se, cnt.val(s_se))
            vec.wait_ge(s_pe, cnt.val(s_pe))
            vec.wait_ge(s_h, cnt.val(s_h))
            vec.tensor_copy(tB16[:, 0:DC * NT], tZ[:, 0:DC * NT])
            vdone()
            sync.wait_ge(s_se, cnt.val(s_se))
            sync.dma_start(out=hv(keys_s, k * NT, NT),
                           in_=sv(tZ, NT, 0, DC, 0, NT)).then_inc(s_st, 16)
            cnt.inc(s_st, 16)
            ten.wait_ge(s_ve, cnt.val(s_ve))
            se_b = cnt.val(s_se)
            for e in range(DC):
                bank = psAB[e % 2]
                if e >= 2:
                    ten.wait_ge(s_se, se_b + e - 1)
                else:
                    ten.wait_ge(s_se, se_b)
                for kc in range(DC):
                    mi = ten.matmul(bank[:, 0:NT],
                                    arK[:, (8 + kc) * D + e * 128: (8 + kc) * D + e * 128 + 128],
                                    tB16[:, kc * NT: kc * NT + NT],
                                    start=(kc == 0), stop=(kc == DC - 1))
                mi.then_inc(s_pe, 1); cnt.inc(s_pe, 1)
                sca.wait_ge(s_pe, cnt.val(s_pe))
                si = sca.activation(sc[:, e * NT:(e + 1) * NT], bank[:, 0:NT],
                                    ACTF.Exp, scale=1.0 / 32.0,
                                    bias=pat[:, PBKA + e:PBKA + e + 1])
                vec.wait_ge(s_se, cnt.val(s_se))
                vec.tensor_copy(tB16b[:, e * NT:(e + 1) * NT], sc[:, e * NT:(e + 1) * NT])
                vec.tensor_reduce(scr8[:, e:e + 1], sc[:, e * NT:(e + 1) * NT],
                                  mybir.AxisListType.X, ALU.add)
                vec.tensor_tensor(sumkw[:, e:e + 1], sumkw[:, e:e + 1],
                                  scr8[:, e:e + 1], ALU.add)
            vdone()
            sync.wait_ge(s_ve, cnt.val(s_ve))
            sync.dma_start(out=hv(kw_s, k * NT, NT),
                           in_=sv(tB16b, NT, 0, DC, 0, NT)).then_inc(s_st, 16)
            cnt.inc(s_st, 16)
            tbar()

        def allgather_carry(idx, src_cols, ccol0, with16=False):
            ten.wait_ge(s_ve, cnt.val(s_ve))
            ten.transpose(ps2[0:8, 0:128], src_cols[:], ident[:]).then_inc(s_pe, 1)
            cnt.inc(s_pe, 1)
            sca.wait_ge(s_pe, cnt.val(s_pe))
            sca.activation(hrow[:], ps2[0:8, 0:128], ACTF.Copy)
            sync.wait_ge(s_se, cnt.val(s_se))
            sync.dma_start(out=ag_in[idx][:, 0:1024].rearrange("o (r c) -> r (o c)", r=8),
                           in_=hrow[:]).then_inc(s_la, 16)
            cnt.inc(s_la, 16)
            if with16:
                sync.dma_start(out=ag_in[idx][:, 1024:1040].rearrange("o (w u) -> (o w) u", u=1),
                               in_=c16[:, 1:2]).then_inc(s_la, 16)
                cnt.inc(s_la, 16)
            gp.wait_ge(s_la, cnt.val(s_la))
            gp.collective_compute("AllGather", ALU.bypass,
                                  replica_groups=[list(range(8))],
                                  ins=[ag_in[idx].opt()],
                                  outs=[ag_out[idx].opt()]).then_inc(s_ag, 1)
            cnt.inc(s_ag, 1)
            sync.wait_ge(s_ag, cnt.val(s_ag))
            gselv = sc[0:8, 0:AG_W]
            sync.dma_start(out=gselv, in_=ag_out[idx]).then_inc(s_la, 16)
            cnt.inc(s_la, 16)
            ten.wait_ge(s_la, cnt.val(s_la))
            ten.matmul(ps2[0:1, 0:NT], onehot8[:], gselv[:, 0:NT], start=True, stop=True)
            mi = ten.matmul(ps2[32:33, 0:NT], onehot8[:], gselv[:, NT:2 * NT],
                            start=True, stop=True)
            if with16:
                mi = ten.matmul(ps2[64:65, 0:16], onehot8[:], gselv[:, 1024:1040],
                                start=True, stop=True)
            mi.then_inc(s_pe, 1); cnt.inc(s_pe, 1)
            sca.wait_ge(s_pe, cnt.val(s_pe))
            sca.activation(svA[0:1, 0:NT], ps2[0:1, 0:NT], ACTF.Copy)
            si = sca.activation(svA[32:33, 0:NT], ps2[32:33, 0:NT], ACTF.Copy)
            if with16:
                si = sca.activation(svA[64:65, 0:16], ps2[64:65, 0:16], ACTF.Copy)
            ten.wait_ge(s_se, cnt.val(s_se))
            for c in range(DC):
                rsrc = (svA[0:1, c * 128:(c + 1) * 128] if c < 4 else
                        svA[32:33, (c - 4) * 128:(c - 3) * 128])
                idk = ones2d[0:1, 0:1] if c < 4 else ones2d[32:33, 0:1]
                ten.transpose(ps3[:, 0:1], rsrc, idk).then_inc(s_pe, 1)
                cnt.inc(s_pe, 1)
                vec.wait_ge(s_pe, cnt.val(s_pe))
                vec.tensor_copy(carry[:, ccol0 + c:ccol0 + c + 1], ps3[:, 0:1])
                vdone()
                ten.wait_ge(s_ve, cnt.val(s_ve))
            if with16:
                ten.transpose(ps3[0:16, 0:1], svA[64:65, 0:16],
                              ones2d[64:65, 0:1]).then_inc(s_pe, 1)
                cnt.inc(s_pe, 1)
                vec.wait_ge(s_pe, cnt.val(s_pe))
                vec.tensor_copy(c16[:, 0:1], ps3[0:16, 0:1])
            bar()

        allgather_carry(0, sumkw, 0)
        vec.tensor_copy(lastK[:], carry[:, 0:8])
        vdone()
        bar()

        gp.wait_ge(s_pe, cnt.val(s_pe))
        wload(arK, 0, wvT)
        ten.wait_ge(s_gd, cnt.val(s_gd))
        for k in range(KT):
            vec.wait_ge(s_st, cnt.val(s_st))
            sca.wait_ge(s_st, cnt.val(s_st))
            sync.dma_start(out=sv(tX, NT, 0, DC, 0, NT),
                           in_=hv(keys_s, k * NT, NT)).then_inc(s_ld, 16)
            cnt.inc(s_ld, 16)
            sync.dma_start(out=sv(tB16, NT, 0, DC, 0, NT),
                           in_=hv(kw_s, k * NT, NT)).then_inc(s_ld, 16)
            cnt.inc(s_ld, 16)
            sync.dma_start(out=sv(arH, NT, 0, DC, 0, NT),
                           in_=hv(h_s, k * NT, NT)).then_inc(s_ld, 16)
            cnt.inc(s_ld, 16)
            ten.wait_ge(s_ld, cnt.val(s_ld))
            vp_done = []
            for e in range(DC):
                bank = psAB[e % 2]
                if e >= 2:
                    ten.wait_ge(s_se, vp_done[e - 2])
                else:
                    ten.wait_ge(s_se, cnt.val(s_se))
                for kc in range(DC):
                    mi = ten.matmul(bank[:, 0:NT],
                                    arK[:, kc * D + e * 128: kc * D + e * 128 + 128],
                                    arH[:, kc * NT: kc * NT + NT],
                                    start=(kc == 0), stop=(kc == DC - 1))
                mi.then_inc(s_pe, 1); cnt.inc(s_pe, 1)
                sca.wait_ge(s_pe, cnt.val(s_pe))
                si = sca.activation(sc[:, e * NT:(e + 1) * NT], bank[:, 0:NT],
                                    ACTF.Identity, bias=pat[:, PBV + e:PBV + e + 1])
                vp_done.append(cnt.val(s_se))
            sync.wait_ge(s_se, cnt.val(s_se))
            sync.dma_start(out=hv(vals_s, k * NT, NT),
                           in_=sv(sc, NT, 0, DC, 0, NT)).then_inc(s_st, 16)
            cnt.inc(s_st, 16)
            vec.wait_ge(s_ld, cnt.val(s_ld))
            for c in range(DC):
                slq = slice(c * NT, (c + 1) * NT)
                vec.tensor_tensor_scan(tY[:, slq], tB16[:, slq], zerosb[:, 0:NT],
                                       lastK[:, c:c + 1], ALU.add, ALU.add)
                vec.tensor_copy(lastK[:, c:c + 1], tY[:, c * NT + NT - 1:c * NT + NT])
            vdone()
            sca.wait_ge(s_ve, cnt.val(s_ve))
            sca.activation(tZ[:, 0:DC * NT], tY[:, 0:DC * NT], ACTF.Ln)
            si = sca.activation(tZ[:, 0:DC * NT], tZ[:, 0:DC * NT], ACTF.Exp, scale=-1.0)
            vec.wait_ge(s_se, cnt.val(s_se))
            for c in range(DC):
                slq = slice(c * NT, (c + 1) * NT)
                vec.tensor_tensor(tY[:, slq], tB16[:, slq], tX[:, slq], ALU.mult)
                vec.tensor_tensor(tY[:, slq], tY[:, slq], tZ[:, slq], ALU.mult)
            for c in range(DC):
                slq = slice(c * NT, (c + 1) * NT)
                vec.tensor_tensor_scan(tZ[:, slq], tY[:, slq], zerosb[:, 0:NT],
                                       lastP[:, c:c + 1], ALU.add, ALU.add)
                vec.tensor_copy(lastP[:, c:c + 1], tZ[:, c * NT + NT - 1:c * NT + NT])
            vec.tensor_copy(tB16b[:, 0:DC * NT], tZ[:, 0:DC * NT])
            vdone()
            sync.wait_ge(s_ve, cnt.val(s_ve))
            sync.dma_start(out=hv(pool_s, k * NT, NT),
                           in_=sv(tB16b, NT, 0, DC, 0, NT)).then_inc(s_st, 16)
            cnt.inc(s_st, 16)
            tbar()

        allgather_carry(1, lastP, 8)

        for k in range(KT):
            vec.wait_ge(s_st, cnt.val(s_st))
            sca.wait_ge(s_st, cnt.val(s_st))
            sync.dma_start(out=sv(tX, NT, 0, DC, 0, NT),
                           in_=hv(x_in, k * NT, NT)).then_inc(s_ld, 16)
            cnt.inc(s_ld, 16)
            sync.dma_start(out=sv(tB16b, NT, 0, DC, 0, NT),
                           in_=hv(pool_s, k * NT, NT)).then_inc(s_ld, 16)
            cnt.inc(s_ld, 16)
            sync.dma_start(out=sv(sc, NT, 0, DC, 0, NT),
                           in_=hv(vals_s, k * NT, NT)).then_inc(s_ld, 16)
            cnt.inc(s_ld, 16)
            vec.wait_ge(s_ld, cnt.val(s_ld))
            for c in range(DC):
                slq = slice(c * NT, (c + 1) * NT)
                vec.scalar_tensor_tensor(tY[:, slq], tB16b[:, slq],
                                         carry[:, 8 + c:8 + c + 1],
                                         sc[:, slq], ALU.add, ALU.mult)
                vec.tensor_tensor(tY[:, slq], tY[:, slq], tX[:, slq], ALU.add)
            vdone()
            sca.wait_ge(s_ve, cnt.val(s_ve))
            ln_stats_norm(NT, tY, tZ)
            vdone()
            sync.wait_ge(s_ve, cnt.val(s_ve))
            sync.dma_start(out=hv(x2_s, k * NT, NT),
                           in_=sv(tZ, NT, 0, DC, 0, NT)).then_inc(s_st, 16)
            cnt.inc(s_st, 16)
            tbar()

        if upto == "x2":
            _finalize(nc, sync, cnt, s_st, dbg, debug,
                      [("d_x2", x2_s, D * T), ("d_keys", keys_s, D * T)])
            return nc

        W2SL = DFC * NT
        bar()
        for k in range(KT):
            vec.wait_ge(s_st, cnt.val(s_st))
            sca.wait_ge(s_st, cnt.val(s_st))
            sync.dma_start(out=sv(tX, NT, 0, DC, 0, NT),
                           in_=hv(x2_s, k * NT, NT)).then_inc(s_ld, 16)
            cnt.inc(s_ld, 16)
            vec.wait_ge(s_ld, cnt.val(s_ld))
            vec.tensor_copy(tB16[:, 0:DC * NT], tX[:, 0:DC * NT])
            vdone()
            ten.wait_ge(s_ve, cnt.val(s_ve))
            banks4 = [psBig[:, 0:NT], psBig[:, NT:2 * NT],
                      ps2[:, 0:NT], ps3[:, 0:NT]]
            pe_q = cnt.val(s_pe)
            se_done4, ve_done4 = [], []
            for q in range(8):
                ssl = s_sl0 if q % 2 == 0 else s_sl1
                if q >= 2:
                    sync.wait_ge(s_pe, pe_q + 4 * (q - 1))
                sync.dma_start(out=arH[:, (q % 2) * DF: (q % 2) * DF + DF],
                               in_=w1b_s[q, :, :]).then_inc(ssl, 16)
                cnt.inc(ssl, 16)
                ten.wait_ge(ssl, cnt.val(ssl))
                woq = (q % 2) * DF
                for fq in range(4):
                    fc = q * 4 + fq
                    bank = banks4[fc % 4]
                    slq = slice((fc % 4) * NT, (fc % 4) * NT + NT)
                    if fc >= 4:
                        ten.wait_ge(s_se, se_done4[fc - 4])
                        ten.wait_ge(s_ve, ve_done4[fc - 4])
                    for kc in range(DC):
                        mi = ten.matmul(bank,
                                        arH[:, woq + kc * NT + fq * 128: woq + kc * NT + fq * 128 + 128],
                                        tB16[:, kc * NT: kc * NT + NT],
                                        start=(kc == 0), stop=(kc == DC - 1))
                    mi.then_inc(s_pe, 1); cnt.inc(s_pe, 1)
                    sca.wait_ge(s_pe, cnt.val(s_pe))
                    sca.activation(tY[:, slq], bank, ACTF.Sin, scale=0.1,
                                   bias=patDF[:, PB1C + fc:PB1C + fc + 1])
                    sca.activation(tZ[:, slq], bank, ACTF.Sin, scale=0.1,
                                   bias=patDF[:, PB1S + fc:PB1S + fc + 1])
                    se_done4.append(cnt.val(s_se))
                    vec.wait_ge(s_se, cnt.val(s_se))
                    vec.scalar_tensor_tensor(sc[:, slq], tY[:, slq],
                                             patDF[:, PA + fc:PA + fc + 1],
                                             bank, ALU.mult, ALU.add)
                    vec.scalar_tensor_tensor(sc[:, slq], tZ[:, slq],
                                             patDF[:, PB + fc:PB + fc + 1],
                                             sc[:, slq], ALU.mult, ALU.add)
                    vec.tensor_scalar(arK[:, fc * NT:fc * NT + NT], sc[:, slq],
                                      patA1[:, fc:fc + 1], 0.0, ALU.add, ALU.max)
                    ve_done4.append(cnt.val(s_ve))
            ten.wait_ge(s_ve, ve_done4[-1])
            se_b = cnt.val(s_se)
            pe_b = cnt.val(s_pe)
            for e in range(DC):
                ssl = s_sl0 if e % 2 == 0 else s_sl1
                if e >= 2:
                    sync.wait_ge(s_pe, pe_b + e - 1)
                else:
                    sync.wait_ge(s_pe, pe_b)
                sync.dma_start(out=arK[:, W2SL + (e % 2) * DFC * 128: W2SL + (e % 2) * DFC * 128 + DFC * 128],
                               in_=w2b_s[e, :, :]).then_inc(ssl, 16)
                cnt.inc(ssl, 16)
                ten.wait_ge(ssl, cnt.val(ssl))
                if e >= 2:
                    ten.wait_ge(s_se, se_b + e - 1)
                else:
                    ten.wait_ge(s_se, se_b)
                bank = psAB[e % 2]
                woff = W2SL + (e % 2) * DFC * 128
                for fc in range(DFC):
                    mi = ten.matmul(bank[:, 0:NT],
                                    arK[:, woff + fc * 128: woff + fc * 128 + 128],
                                    arK[:, fc * NT: fc * NT + NT],
                                    start=(fc == 0), stop=(fc == DFC - 1))
                mi.then_inc(s_pe, 1); cnt.inc(s_pe, 1)
                sca.wait_ge(s_pe, cnt.val(s_pe))
                si = sca.activation(tY[:, e * NT:(e + 1) * NT], bank[:, 0:NT],
                                    ACTF.Identity, bias=patB[:, QB2 + e:QB2 + e + 1])
            vec.wait_ge(s_se, cnt.val(s_se))
            vec.tensor_tensor(tY[:, 0:DC * NT], tY[:, 0:DC * NT], tX[:, 0:DC * NT],
                              ALU.add)
            vdone()
            ln_stats_norm(NT, tY, tZ)
            vdone()
            sync.wait_ge(s_ve, cnt.val(s_ve))
            sync.dma_start(out=hv(x3_s, k * NT, NT),
                           in_=sv(tZ, NT, 0, DC, 0, NT)).then_inc(s_st, 16)
            cnt.inc(s_st, 16)
            tbar()

        if upto == "x3":
            _finalize(nc, sync, cnt, s_st, dbg, debug,
                      [("d_x2", x2_s, D * T), ("d_x3", x3_s, D * T)])
            return nc

        for c in range(DC):
            vec.tensor_copy(scr8b[:, c:c + 1], tZ[:, c * NT + NT - 1:c * NT + NT])
        vdone()
        ten.wait_ge(s_ve, cnt.val(s_ve))
        allgather_carry(2, scr8b, 16)

        wload(arH, 0, wkeT)
        wload(arH, 8 * D, wvaT)
        wload(arH, 16 * D, wreT)
        for e in (vec, sca, ten):
            e.wait_ge(s_gd, cnt.val(s_gd))
        vec.memset(lastA[:], 0.0)
        vec.memset(maxP[:], 0.0)
        vec.memset(maxN[:], 0.0)
        vec.memset(c16[:, 1:2], 0.0)
        vec.memset(c16[:, 6:7], 0.0)
        vdone()
        bar()

        def proj_l1(wbase, ksl_tile, evac, eng=None, rbase=0):
            eng = eng or sca
            esem = eng._sem
            done = []
            for e in range(DC):
                bank = psAB[e % 2]
                if e >= 2:
                    ten.wait_ge(esem, done[e - 2])
                else:
                    ten.wait_ge(esem, cnt.val(esem))
                for kc in range(DC):
                    mi = ten.matmul(bank[:, 0:NT],
                                    arH[:, wbase + kc * D + e * 128: wbase + kc * D + e * 128 + 128],
                                    ksl_tile[:, rbase + kc * NT: rbase + kc * NT + NT],
                                    start=(kc == 0), stop=(kc == DC - 1))
                mi.then_inc(s_pe, 1); cnt.inc(s_pe, 1)
                eng.wait_ge(s_pe, cnt.val(s_pe))
                evac(e, bank)
                done.append(cnt.val(esem))

        def inv_alpha_tile(k, dst, scr_t):
            for c in range(DC):
                slq = slice(c * NT, (c + 1) * NT)
                vec.tensor_scalar(scr_t[:, slq], iot[:, 0:NT],
                                  patB[:, QNLA + c:QNLA + c + 1],
                                  binv[:, c * KT + k:c * KT + k + 1],
                                  ALU.mult, ALU.add)
            vec.tensor_scalar(scr_t[:, 0:DC * NT], scr_t[:, 0:DC * NT], EXP_CLAMP,
                              None, ALU.min)
            vdone()
            sca.wait_ge(s_ve, cnt.val(s_ve))
            si = sca.activation(dst[:, 0:DC * NT], scr_t[:, 0:DC * NT], ACTF.Exp)
            vec.wait_ge(s_se, cnt.val(s_se))

        def inv16_tile(k, dst, scr_c):
            vec.tensor_scalar(scr_c[:, 0:NT], iot[0:16, 0:NT], la16[:, 1:2],
                              b16k[:, k:k + 1], ALU.mult, ALU.add)
            vec.tensor_scalar(scr_c[:, 0:NT], scr_c[:, 0:NT], EXP_CLAMP, None, ALU.min)
            vdone()
            sca.wait_ge(s_ve, cnt.val(s_ve))
            si = sca.activation(dst[:, 0:NT], scr_c[:, 0:NT], ACTF.Exp)
            vec.wait_ge(s_se, cnt.val(s_se))

        for k in range(KT):
            vec.wait_ge(s_st, cnt.val(s_st))
            sca.wait_ge(s_st, cnt.val(s_st))
            sync.dma_start(out=sv(tX, NT, 0, DC, 0, NT),
                           in_=hv(x3_s, k * NT, NT)).then_inc(s_ld, 16)
            cnt.inc(s_ld, 16)
            if k == 0:
                sync.dma_start(out=sv(tY, NT, 0, DC, 1, NT - 1),
                               in_=hv(x3_s, 0, NT - 1)).then_inc(s_ld, 16)
                cnt.inc(s_ld, 16)
                vec.wait_ge(s_ld, cnt.val(s_ld))
                for c in range(DC):
                    vec.tensor_copy(tY[:, c * NT:c * NT + 1],
                                    carry[:, 16 + c:16 + c + 1])
            else:
                sync.dma_start(out=sv(tY, NT, 0, DC, 0, NT),
                               in_=hv(x3_s, k * NT - 1, NT)).then_inc(s_ld, 16)
                cnt.inc(s_ld, 16)
                vec.wait_ge(s_ld, cnt.val(s_ld))
            vec.tensor_tensor(tX[:, 0:DC * NT], tX[:, 0:DC * NT], tY[:, 0:DC * NT],
                              ALU.subtract)
            vdone()

            for c in range(DC):
                slq = slice(c * NT, (c + 1) * NT)
                vec.scalar_tensor_tensor(arK[:, c * NT:c * NT + NT], tX[:, slq],
                                         pat[:, PMK + c:PMK + c + 1],
                                         tY[:, slq], ALU.mult, ALU.add)
                vec.scalar_tensor_tensor(arK[:, DF + c * NT:DF + c * NT + NT],
                                         tX[:, slq],
                                         pat[:, PMV + c:PMV + c + 1],
                                         tY[:, slq], ALU.mult, ALU.add)
                vec.scalar_tensor_tensor(arK[:, 2 * DF + c * NT:2 * DF + c * NT + NT],
                                         tX[:, slq],
                                         pat[:, PMR + c:PMR + c + 1],
                                         tY[:, slq], ALU.mult, ALU.add)
            vdone()
            ten.wait_ge(s_ve, cnt.val(s_ve))

            def evac_k(e, bank):
                si = sca.activation(sc[:, e * NT:(e + 1) * NT], bank[:, 0:NT], ACTF.Exp)
            proj_l1(0, arK, evac_k, rbase=0)

            def evac_v(e, bank):
                si = sca.activation(tZ[:, e * NT:(e + 1) * NT], bank[:, 0:NT],
                                    ACTF.Identity)
            proj_l1(8 * D, arK, evac_v, rbase=DF)

            def evac_r(e, bank):
                si = sca.activation(tB16b[:, e * NT:(e + 1) * NT], bank[:, 0:NT],
                                    ACTF.Sigmoid)
            proj_l1(16 * D, arK, evac_r, rbase=2 * DF)

            ten.wait_ge(s_se, cnt.val(s_se))
            for c in range(DC):
                mi = ten.matmul(ps2[0:16, 0:NT], selc[:, c * 16:(c + 1) * 16],
                                sc[:, c * NT:(c + 1) * NT],
                                start=(c == 0), stop=(c == DC - 1))
            mi.then_inc(s_pe, 1); cnt.inc(s_pe, 1)
            vec.wait_ge(s_pe, cnt.val(s_pe))
            vec.tensor_copy(cse[:, 0:NT], ps2[0:16, 0:NT])
            vdone()
            sync.wait_ge(s_ve, cnt.val(s_ve))
            sync.dma_start(out=se_s[:, k * NT:(k + 1) * NT], in_=cse[:, 0:NT]).then_inc(s_st, 16)
            cnt.inc(s_st, 16)
            sync.wait_ge(s_se, cnt.val(s_se))
            sync.dma_start(out=hv(r_s, k * NT, NT),
                           in_=sv(tB16b, NT, 0, DC, 0, NT)).then_inc(s_st, 16)
            cnt.inc(s_st, 16)

            vec.wait_ge(s_se, cnt.val(s_se))
            vec.tensor_tensor(tZ[:, 0:DC * NT], sc[:, 0:DC * NT], tZ[:, 0:DC * NT],
                              ALU.mult)
            vdone()
            sync.wait_ge(s_ve, cnt.val(s_ve))
            sync.dma_start(out=hv(ekv_s, k * NT, NT),
                           in_=sv(tZ, NT, 0, DC, 0, NT)).then_inc(s_st, 16)
            cnt.inc(s_st, 16)

            inv_alpha_tile(k, tX, tY)
            vec.scalar_tensor_tensor(tX[:, 0:DC * NT], tZ[:, 0:DC * NT], C1,
                                     tX[:, 0:DC * NT], ALU.mult, ALU.mult)
            vec.tensor_scalar(bmt[:, 0:NT], iot[:, 0:NT], -1.0,
                              bigk[:, k:k + 1], ALU.mult, ALU.add)
            vec.wait_ge(s_pe, cnt.val(s_pe))
            u1 = psBig[:, 0:NT]
            for c in range(DC):
                slq = slice(c * NT, (c + 1) * NT)
                vec.tensor_tensor_scan(tY[:, slq], tX[:, slq], zerosb[:, 0:NT],
                                       lastA[:, c:c + 1], ALU.add, ALU.add)
                vec.tensor_copy(lastA[:, c:c + 1], tY[:, c * NT + NT - 1:c * NT + NT])
                vec.tensor_tensor(u1, tX[:, slq], tY[:, slq], ALU.max)
                vec.scalar_tensor_tensor(u1, u1, THF, bmt[:, 0:NT],
                                         ALU.is_ge, ALU.mult)
                vec.tensor_reduce(scr8[:, c:c + 1], u1, mybir.AxisListType.X, ALU.max)
                vec.tensor_tensor(u1, tX[:, slq], tY[:, slq], ALU.min)
                vec.scalar_tensor_tensor(u1, u1, -THF, bmt[:, 0:NT],
                                         ALU.is_le, ALU.mult)
                vec.tensor_reduce(scr8b[:, c:c + 1], u1, mybir.AxisListType.X, ALU.max)
            vec.tensor_tensor(maxP[:], maxP[:], scr8[:], ALU.max)
            vec.tensor_tensor(maxN[:], maxN[:], scr8b[:], ALU.max)
            vdone()
            inv16_tile(k, ct2, ct1)
            vec.scalar_tensor_tensor(ccb[:, 0:NT], cse[:, 0:NT], C1,
                                     ct2[:, 0:NT], ALU.mult, ALU.mult)
            vec.tensor_tensor_scan(ct1[:, 0:NT], ccb[:, 0:NT], zerosb[0:16, 0:NT],
                                   c16[:, 1:2], ALU.add, ALU.add)
            vec.tensor_copy(c16[:, 1:2], ct1[:, NT - 1:NT])
            vec.tensor_tensor(ct2[:, 0:NT], ccb[:, 0:NT], ct1[:, 0:NT], ALU.max)
            vec.scalar_tensor_tensor(ct2[:, 0:NT], ct2[:, 0:NT], THF,
                                     bmt[0:16, 0:NT], ALU.is_ge, ALU.mult)
            vec.tensor_reduce(c16[:, 4:5], ct2[:, 0:NT], mybir.AxisListType.X, ALU.max)
            vec.tensor_tensor(c16[:, 6:7], c16[:, 6:7], c16[:, 4:5], ALU.max)
            vdone()
            tbar()

        ten.wait_ge(s_ve, cnt.val(s_ve))
        for (src_cols, off) in ((lastA[:], 0), (maxP[:], 1024), (maxN[:], 2048)):
            ten.transpose(ps2[0:8, 0:128], src_cols, ident[:]).then_inc(s_pe, 1)
            cnt.inc(s_pe, 1)
            sca.wait_ge(s_pe, cnt.val(s_pe))
            sca.wait_ge(s_la, cnt.val(s_la))
            sca.activation(hrow[:], ps2[0:8, 0:128], ACTF.Copy)
            sync.wait_ge(s_se, cnt.val(s_se))
            sync.dma_start(out=ag_in[3][:, off:off + 1024].rearrange("o (r c) -> r (o c)", r=8),
                           in_=hrow[:]).then_inc(s_la, 16)
            cnt.inc(s_la, 16)
            ten.wait_ge(s_se, cnt.val(s_se))
        sync.dma_start(out=ag_in[3][:, 3072:3088].rearrange("o (w u) -> (o w) u", u=1),
                       in_=c16[:, 1:2]).then_inc(s_la, 16)
        cnt.inc(s_la, 16)
        sync.dma_start(out=ag_in[3][:, 3088:3104].rearrange("o (w u) -> (o w) u", u=1),
                       in_=c16[:, 6:7]).then_inc(s_la, 16)
        cnt.inc(s_la, 16)
        gp.wait_ge(s_la, cnt.val(s_la))
        gp.collective_compute("AllGather", ALU.bypass,
                              replica_groups=[list(range(8))],
                              ins=[ag_in[3].opt()],
                              outs=[ag_out[3].opt()]).then_inc(s_ag, 1)
        cnt.inc(s_ag, 1)
        sync.wait_ge(s_ag, cnt.val(s_ag))
        gselv = sc[0:8, 0:AG_W]
        sync.dma_start(out=gselv, in_=ag_out[3]).then_inc(s_la, 16)
        cnt.inc(s_la, 16)
        ten.wait_ge(s_la, cnt.val(s_la))
        for (off, ccol0) in ((0, 24), (1024, 32), (2048, 40)):
            ten.matmul(ps2[0:1, 0:NT], onehot8[:], gselv[:, off:off + NT],
                       start=True, stop=True)
            mi = ten.matmul(ps2[32:33, 0:NT], onehot8[:], gselv[:, off + NT:off + 2 * NT],
                            start=True, stop=True)
            mi.then_inc(s_pe, 1); cnt.inc(s_pe, 1)
            sca.wait_ge(s_pe, cnt.val(s_pe))
            sca.activation(svA[0:1, 0:NT], ps2[0:1, 0:NT], ACTF.Copy)
            si = sca.activation(svA[32:33, 0:NT], ps2[32:33, 0:NT], ACTF.Copy)
            ten.wait_ge(s_se, cnt.val(s_se))
            for c in range(DC):
                rsrc = (svA[0:1, c * 128:(c + 1) * 128] if c < 4 else
                        svA[32:33, (c - 4) * 128:(c - 3) * 128])
                idk = ones2d[0:1, 0:1] if c < 4 else ones2d[32:33, 0:1]
                ten.transpose(ps3[:, 0:1], rsrc, idk).then_inc(s_pe, 1)
                cnt.inc(s_pe, 1)
                vec.wait_ge(s_pe, cnt.val(s_pe))
                vec.tensor_copy(carry[:, ccol0 + c:ccol0 + c + 1], ps3[:, 0:1])
                vdone()
                ten.wait_ge(s_ve, cnt.val(s_ve))
        ten.matmul(ps2[0:1, 0:16], onehot8[:], gselv[:, 3072:3088], start=True, stop=True)
        mi = ten.matmul(ps2[32:33, 0:16], onehot8[:], gselv[:, 3088:3104],
                        start=True, stop=True)
        mi.then_inc(s_pe, 1); cnt.inc(s_pe, 1)
        sca.wait_ge(s_pe, cnt.val(s_pe))
        sca.activation(svA[0:1, 0:16], ps2[0:1, 0:16], ACTF.Copy)
        si = sca.activation(svA[32:33, 0:16], ps2[32:33, 0:16], ACTF.Copy)
        ten.wait_ge(s_se, cnt.val(s_se))
        ten.transpose(ps3[0:16, 0:1], svA[0:1, 0:16], ones2d[0:1, 0:1]).then_inc(s_pe, 1)
        cnt.inc(s_pe, 1)
        vec.wait_ge(s_pe, cnt.val(s_pe))
        vec.tensor_copy(c16[:, 0:1], ps3[0:16, 0:1])
        vdone()
        ten.wait_ge(s_ve, cnt.val(s_ve))
        ten.transpose(ps3[0:16, 0:1], svA[32:33, 0:16], ones2d[32:33, 0:1]).then_inc(s_pe, 1)
        cnt.inc(s_pe, 1)
        vec.wait_ge(s_pe, cnt.val(s_pe))
        vec.tensor_copy(c16[:, 2:3], ps3[0:16, 0:1])
        vdone()
        bar()

        if upto == "front":
            _finalize(nc, sync, cnt, s_st, dbg, debug,
                      [("d_x2", x2_s, 0), ("d_x3", x3_s, 0), ("d_ekv", ekv_s, 0)])
            return nc

        wload(arH, 0, woT)
        ten.wait_ge(s_gd, cnt.val(s_gd))
        vec.tensor_copy(lastP[:], carry[:, 24:32])
        vec.tensor_copy(chainP[:], carry[:, 32:40])
        vec.tensor_copy(chainN[:], carry[:, 40:48])
        vec.tensor_copy(c16[:, 3:4], c16[:, 0:1])
        vec.tensor_copy(c16[:, 4:5], c16[:, 2:3])
        vec.memset(c16[:, 7:8], -LN2_64)
        vdone()
        bar()
        for k in range(KT):
            vec.wait_ge(s_st, cnt.val(s_st))
            sca.wait_ge(s_st, cnt.val(s_st))
            sync.dma_start(out=sv(tZ, NT, 0, DC, 0, NT),
                           in_=hv(ekv_s, k * NT, NT)).then_inc(s_ld, 16)
            cnt.inc(s_ld, 16)
            sync.dma_start(out=sv(tX, NT, 0, DC, 0, NT),
                           in_=hv(x3_s, k * NT, NT)).then_inc(s_ld, 16)
            cnt.inc(s_ld, 16)
            sync.dma_start(out=sv(tB16b, NT, 0, DC, 0, NT),
                           in_=hv(r_s, k * NT, NT)).then_inc(s_ld, 16)
            cnt.inc(s_ld, 16)
            sync.dma_start(out=cse[:, 0:NT], in_=se_s[:, k * NT:(k + 1) * NT]).then_inc(s_ld, 16)
            cnt.inc(s_ld, 16)
            vec.wait_ge(s_ld, cnt.val(s_ld))
            sca.wait_ge(s_ld, cnt.val(s_ld))
            inv_alpha_tile(k, sc, tY)
            vec.scalar_tensor_tensor(sc[:, 0:DC * NT], tZ[:, 0:DC * NT], C1,
                                     sc[:, 0:DC * NT], ALU.mult, ALU.mult)
            vec.tensor_scalar(bmt[:, 0:NT], iot[:, 0:NT], -1.0,
                              bigk[:, k:k + 1], ALU.mult, ALU.add)
            inv16_tile(k, ct2, ct1)
            vec.scalar_tensor_tensor(ccb[:, 0:NT], cse[:, 0:NT], C1,
                                     ct2[:, 0:NT], ALU.mult, ALU.mult)
            vec.tensor_tensor_scan(ct2[:, 0:NT], ccb[:, 0:NT], zerosb[0:16, 0:NT],
                                   c16[:, 3:4], ALU.add, ALU.add)
            vec.tensor_tensor(ct1[:, 0:NT], ccb[:, 0:NT], ct2[:, 0:NT], ALU.max)
            vec.scalar_tensor_tensor(ct1[:, 0:NT], ct1[:, 0:NT], THF,
                                     bmt[0:16, 0:NT], ALU.is_ge, ALU.mult)
            vec.tensor_tensor_scan(ccb[:, 0:NT], ct1[:, 0:NT], zerosb[0:16, 0:NT],
                                   c16[:, 4:5], ALU.max, ALU.max)
            vec.tensor_copy(crd[:, 1:NT], ct2[:, 0:NT - 1])
            vec.tensor_copy(crd[:, 0:1], c16[:, 3:4])
            vec.scalar_tensor_tensor(crd[:, 1:NT], ccb[:, 0:NT - 1], 0.0,
                                     crd[:, 1:NT], ALU.is_le, ALU.mult)
            vec.scalar_tensor_tensor(crd[:, 0:1], c16[:, 4:5], 0.0,
                                     crd[:, 0:1], ALU.is_le, ALU.mult)
            vec.tensor_copy(c16[:, 3:4], ct2[:, NT - 1:NT])
            vec.tensor_copy(c16[:, 4:5], ccb[:, NT - 1:NT])
            vec.tensor_scalar(ct1[:, 0:NT], iot[0:16, 0:NT], la16[:, 0:1],
                              b16k[:, KT + k:KT + k + 1], ALU.mult, ALU.add)
            vdone()
            sca.wait_ge(s_ve, cnt.val(s_ve))
            si = sca.activation(ccb[:, 0:NT], ct1[:, 0:NT], ACTF.Exp)
            vec.wait_ge(s_se, cnt.val(s_se))
            vec.scalar_tensor_tensor(crd[:, 0:NT], ccb[:, 0:NT], C_RESC,
                                     crd[:, 0:NT], ALU.mult, ALU.mult)
            vec.scalar_tensor_tensor(crd[:, 0:NT], cse[:, 0:NT], la16[:, 2:3],
                                     crd[:, 0:NT], ALU.mult, ALU.add)
            vec.tensor_scalar(crd[:, 0:NT], crd[:, 0:NT], 1e-6, None, ALU.max)
            vec.tensor_scalar(crd[:, 0:NT], crd[:, 0:NT], 3.3e38, None, ALU.min)
            vdone()
            sca.wait_ge(s_ve, cnt.val(s_ve))
            sca.activation(ct1[:, 0:NT], crd[:, 0:NT], ACTF.Ln, scale=WKV_SC * WKV_SC)
            si = sca.activation(crd[:, 0:NT], ct1[:, 0:NT], ACTF.Exp, scale=-1.0,
                                bias=c16[:, 7:8])
            vec.wait_ge(s_pe, cnt.val(s_pe))

            def wkv_post(c):
                slq = slice(c * NT, (c + 1) * NT)
                vec.wait_ge(s_se, se_mark[c])
                vec.scalar_tensor_tensor(sc[:, slq], tY[:, slq], C_RESC,
                                         sc[:, slq], ALU.mult, ALU.mult)
                vec.scalar_tensor_tensor(ps2[:, 0:NT], tY[:, slq], 0.0,
                                         ps2[:, 0:NT], ALU.is_gt, ALU.mult)
                vec.scalar_tensor_tensor(sc[:, slq], ps2[:, 0:NT], -FMAX,
                                         sc[:, slq], ALU.mult, ALU.add)
                vec.scalar_tensor_tensor(sc[:, slq], tZ[:, slq],
                                         patB[:, QEU + c:QEU + c + 1],
                                         sc[:, slq], ALU.mult, ALU.add)

            se_mark = {}
            for c in range(DC):
                slq = slice(c * NT, (c + 1) * NT)
                u1 = tY[:, ((c + 1) % DC) * NT:((c + 1) % DC) * NT + NT]
                u2 = tY[:, ((c + 2) % DC) * NT:((c + 2) % DC) * NT + NT]
                u4 = tY[:, ((c + 3) % DC) * NT:((c + 3) % DC) * NT + NT]
                u3 = ps2[:, 0:NT]
                vec.tensor_tensor_scan(tY[:, slq], sc[:, slq], zerosb[:, 0:NT],
                                       lastP[:, c:c + 1], ALU.add, ALU.add)
                vec.tensor_tensor(u1, sc[:, slq], tY[:, slq], ALU.max)
                vec.scalar_tensor_tensor(u1, u1, THF, bmt[:, 0:NT],
                                         ALU.is_ge, ALU.mult)
                vec.tensor_tensor_scan(u2, u1, zerosb[:, 0:NT],
                                       chainP[:, c:c + 1], ALU.max, ALU.max)
                if c >= 1:
                    wkv_post(c - 1)
                vec.tensor_tensor(u3, sc[:, slq], tY[:, slq], ALU.min)
                vec.scalar_tensor_tensor(u3, u3, -THF, bmt[:, 0:NT],
                                         ALU.is_le, ALU.mult)
                vec.tensor_tensor_scan(u4, u3, zerosb[:, 0:NT],
                                       chainN[:, c:c + 1], ALU.max, ALU.max)
                vec.tensor_tensor(u3[:, 1:NT], u4[:, 0:NT - 1],
                                  u2[:, 0:NT - 1], ALU.is_gt)
                vec.tensor_tensor(u3[:, 0:1], chainN[:, c:c + 1],
                                  chainP[:, c:c + 1], ALU.is_gt)
                vec.tensor_tensor(u1[:, 1:NT], u2[:, 0:NT - 1],
                                  u4[:, 0:NT - 1], ALU.max)
                vec.tensor_tensor(u1[:, 0:1], chainP[:, c:c + 1],
                                  chainN[:, c:c + 1], ALU.max)
                vec.tensor_copy(chainP[:, c:c + 1], u2[:, NT - 1:NT])
                vec.tensor_copy(chainN[:, c:c + 1], u4[:, NT - 1:NT])
                vec.tensor_copy(sc[:, c * NT + 1:(c + 1) * NT],
                                tY[:, c * NT:c * NT + NT - 1])
                vec.tensor_copy(sc[:, c * NT:c * NT + 1], lastP[:, c:c + 1])
                vec.tensor_copy(lastP[:, c:c + 1], tY[:, c * NT + NT - 1:c * NT + NT])
                vec.scalar_tensor_tensor(sc[:, slq], u1, 0.0, sc[:, slq],
                                         ALU.is_le, ALU.mult)
                sca.wait_ge(s_ve, cnt.val(s_ve))
                si = sca.activation(tY[:, slq], iot[:, 0:NT],
                                    ACTF.Exp, scale=patB[:, QLA + c:QLA + c + 1],
                                    bias=bpow[:, c * KT + k:c * KT + k + 1])
                se_mark[c] = cnt.val(s_se)
            wkv_post(DC - 1)
            vdone()
            if debug:
                sync.wait_ge(s_ve, cnt.val(s_ve))
                sync.dma_start(out=hv(dbg["d_num"], k * NT, NT),
                               in_=sv(sc, NT, 0, DC, 0, NT)).then_inc(s_st, 16)
                cnt.inc(s_st, 16)
                vec.wait_ge(s_st, cnt.val(s_st))
            ten.wait_ge(s_se, cnt.val(s_se))
            ten.wait_ge(s_ve, cnt.val(s_ve))
            banks4b = [ps2[:, 0:NT], ps3[:, 0:NT],
                       psBig[:, 0:NT], psBig[:, NT:2 * NT]]
            bdone = []
            for c in range(DC):
                bank = banks4b[c % 4]
                if c >= 4:
                    ten.wait_ge(s_ve, bdone[c - 4])
                mi = ten.matmul(bank, selT[:, c * 128:(c + 1) * 128],
                                crd[:, 0:NT], start=True, stop=True)
                mi.then_inc(s_pe, 1); cnt.inc(s_pe, 1)
                vec.wait_ge(s_pe, cnt.val(s_pe))
                slq = slice(c * NT, (c + 1) * NT)
                vec.scalar_tensor_tensor(sc[:, slq], sc[:, slq], WKV_SC,
                                         bank, ALU.mult, ALU.mult)
                vec.tensor_scalar(sc[:, slq], sc[:, slq], WKV_CL, -WKV_CL,
                                  ALU.min, ALU.max)
                vec.tensor_tensor(tY[:, slq], sc[:, slq], tB16b[:, slq], ALU.mult)
                vec.tensor_scalar(tY[:, slq], tY[:, slq], RW_SC, 2.656e35,
                                  ALU.mult, ALU.min)
                vec.tensor_scalar(tB16[:, slq], tY[:, slq], -2.656e35, None, ALU.max)
                bdone.append(cnt.val(s_ve))
            if debug:
                sync.wait_ge(s_ve, cnt.val(s_ve))
                sync.dma_start(out=hv(dbg["d_wkv"], k * NT, NT),
                               in_=sv(sc, NT, 0, DC, 0, NT)).then_inc(s_st, 16)
                cnt.inc(s_st, 16)
                sync.dma_start(out=dbg["d_rdn"][:, k * NT:(k + 1) * NT],
                               in_=crd[:, 0:NT]).then_inc(s_st, 16)
                cnt.inc(s_st, 16)
                vec.wait_ge(s_st, cnt.val(s_st))
                sca.wait_ge(s_st, cnt.val(s_st))
            def evac_att(e, bank):
                vi = vec.tensor_scalar(tZ[:, e * NT:(e + 1) * NT], bank[:, 0:NT],
                                       2.656e36, -2.656e36, ALU.min, ALU.max)
            proj_l1(0, tB16, evac_att, eng=vec)
            if debug:
                vdone()
                sync.wait_ge(s_ve, cnt.val(s_ve))
                sync.dma_start(out=hv(dbg["d_att"], k * NT, NT),
                               in_=sv(tZ, NT, 0, DC, 0, NT)).then_inc(s_st, 16)
                cnt.inc(s_st, 16)
                vec.wait_ge(s_st, cnt.val(s_st))
            vec.scalar_tensor_tensor(tY[:, 0:DC * NT], tZ[:, 0:DC * NT], 128.0,
                                     tX[:, 0:DC * NT], ALU.mult, ALU.add)
            vdone()
            ln_stats_norm(NT, tY, tZ, armored=True)
            vdone()
            sync.wait_ge(s_ve, cnt.val(s_ve))
            sync.dma_start(out=hv(x4_s, k * NT, NT),
                           in_=sv(tZ, NT, 0, DC, 0, NT)).then_inc(s_st, 16)
            cnt.inc(s_st, 16)
            tbar()

        if upto == "x4":
            _finalize(nc, sync, cnt, s_st, dbg, debug,
                      [("d_x2", x2_s, 0), ("d_x3", x3_s, 0), ("d_x4", x4_s, 0)])
            return nc

        for oc in range(DC):
            vec.wait_ge(s_st, cnt.val(s_st))
            sca.wait_ge(s_st, cnt.val(s_st))
            sync.wait_ge(s_ld, cnt.val(s_ld))
            sync.dma_start(out=tX[:, 0:D], in_=l1fw[oc * 128:(oc + 1) * 128, :]).then_inc(s_ld, 16)
            cnt.inc(s_ld, 16)
            vec.wait_ge(s_ld, cnt.val(s_ld))
            vec.tensor_reduce(scr8[:, 0:1], tX[:, 0:D], mybir.AxisListType.X,
                              ALU.add, apply_absolute_value=True)
            vec.tensor_scalar(scr8[:, 0:1], scr8[:, 0:1], 1.0 / D, None, ALU.mult)
            vec.tensor_scalar(scr8[:, 0:1], scr8[:, 0:1], 1e-5, None, ALU.max)
            vdone()
            sca.wait_ge(s_ve, cnt.val(s_ve))
            sca.activation(scr8[:, 1:2], scr8[:, 0:1], ACTF.Ln)
            si = sca.activation(scr8[:, 1:2], scr8[:, 1:2], ACTF.Exp, scale=-1.0)
            vec.wait_ge(s_se, cnt.val(s_se))
            vec.tensor_scalar(tY[:, 0:D], tX[:, 0:D], scr8[:, 1:2], None, ALU.mult)
            vec.tensor_scalar(tZ[:, 0:D], tY[:, 0:D], 0.5, 1.0, ALU.is_gt, ALU.mult)
            vec.tensor_scalar(tX[:, 0:D], tY[:, 0:D], -0.5, 1.0, ALU.is_lt, ALU.mult)
            vec.tensor_tensor(tZ[:, 0:D], tZ[:, 0:D], tX[:, 0:D], ALU.subtract)
            vec.tensor_scalar(tB16[:, 0:D], tZ[:, 0:D], scr8[:, 0:1], None, ALU.mult)
            vdone()
            ten.wait_ge(s_ve, cnt.val(s_ve))
            sca.wait_ge(s_h, cnt.val(s_h))
            for ic in range(DC):
                ten.transpose(psT[:, 0:128],
                              tB16[:, ic * 128:(ic + 1) * 128], identb[:]).then_inc(s_pe, 1)
                cnt.inc(s_pe, 1)
                sca.wait_ge(s_pe, cnt.val(s_pe))
                si = sca.activation(tB16b[:, ic * 128:(ic + 1) * 128], psT[:, 0:128],
                                    ACTF.Copy)
                ten.wait_ge(s_se, cnt.val(s_se))
            sync.wait_ge(s_se, cnt.val(s_se))
            sync.dma_start(out=wqT_s.rearrange("(c p) e -> p c e", p=128)[:, :, oc * 128:(oc + 1) * 128],
                           in_=tB16b[:, 0:D].rearrange("p (c e) -> p c e", e=128)).then_inc(s_h, 16)
            cnt.inc(s_h, 16)
            tbar()
        sync.dma_start(out=arH[:, 0:8 * D].rearrange("p (c e) -> p c e", e=D),
                       in_=wqT_s.rearrange("(c p) e -> p c e", p=128)).then_inc(s_ld, 16)
        cnt.inc(s_ld, 16)
        wload(arH, 8 * D, hadc)
        bar()

        for e in (vec, sca, ten):
            e.wait_ge(s_ld, cnt.val(s_ld))
        for k in range(KT):
            vec.wait_ge(s_st, cnt.val(s_st))
            sca.wait_ge(s_st, cnt.val(s_st))
            sync.dma_start(out=sv(tX, NT, 0, DC, 0, NT),
                           in_=hv(x4_s, k * NT, NT)).then_inc(s_ld, 16)
            cnt.inc(s_ld, 16)
            vec.wait_ge(s_ld, cnt.val(s_ld))
            vec.tensor_copy(tB16[:, 0:DC * NT], tX[:, 0:DC * NT])
            vdone()
            ten.wait_ge(s_ve, cnt.val(s_ve))

            def p12_quant(tg):
                slq = slice(tg * 1024, tg * 1024 + 1024)
                vec.wait_ge(s_se, se_done[tg])
                vec.tensor_reduce(scr8[:, 0:1], tY[:, slq], mybir.AxisListType.X,
                                  ALU.max, apply_absolute_value=True)
                vec.tensor_scalar(scr8[:, 0:1], scr8[:, 0:1], 1e-5, None, ALU.max)
                vec.tensor_scalar(scr8[:, 1:2], scr8[:, 0:1], 1.0 / 7.0, None, ALU.mult)
                vec.reciprocal(scr8[:, 2:3], scr8[:, 1:2])
                vec.tensor_scalar(tZ[:, slq], tY[:, slq], scr8[:, 2:3], MAGIC,
                                  ALU.mult, ALU.add)
                vec.tensor_scalar(tZ[:, slq], tZ[:, slq], MAGIC, None, ALU.subtract)
                vec.tensor_scalar(tZ[:, slq], tZ[:, slq], -8.0, 7.0, ALU.max, ALU.min)
                vec.tensor_scalar(tB16b[:, slq], tZ[:, slq], scr8[:, 1:2], None,
                                  ALU.mult)
                return cnt.val(s_ve)

            def p12_transpose(tg):
                ten.wait_ge(s_ve, q_mark[tg])
                for oc in range(DC):
                    mi = ten.transpose(psT[:, (oc % 4) * 128:(oc % 4) * 128 + 128],
                                       tB16b[:, tg * 1024 + oc * 128: tg * 1024 + oc * 128 + 128],
                                       identb[:])
                    if oc % 4 == 3:
                        mi.then_inc(s_pe, 1); cnt.inc(s_pe, 1)
                        sca.wait_ge(s_pe, cnt.val(s_pe))
                        for j in range(4):
                            ocx = oc - 3 + j
                            si = sca.activation(
                                tB16[:, ocx * NT + tg * 128: ocx * NT + tg * 128 + 128],
                                psT[:, j * 128:j * 128 + 128], ACTF.Copy)
                        ten.wait_ge(s_se, cnt.val(s_se))

            se_done = []; q_mark = {}
            for tg in range(NT // 128):
                if tg >= 1:
                    p12_transpose(tg - 1)
                    ten.wait_ge(s_se, se_done[tg - 1])
                else:
                    ten.wait_ge(s_se, cnt.val(s_se))
                for nh in range(2):
                    bank = psAB[nh]
                    for kc in range(DC):
                        mi = ten.matmul(bank[:, 0:NT],
                                        tB16[:, kc * NT + tg * 128: kc * NT + tg * 128 + 128],
                                        arH[:, (8 + kc) * D + nh * NT: (8 + kc) * D + nh * NT + NT],
                                        start=(kc == 0), stop=(kc == DC - 1))
                mi.then_inc(s_pe, 1); cnt.inc(s_pe, 1)
                sca.wait_ge(s_pe, cnt.val(s_pe))
                sca.activation(tY[:, tg * 1024:tg * 1024 + NT], ps0[:, 0:NT], ACTF.Copy)
                si = sca.activation(tY[:, tg * 1024 + NT:tg * 1024 + 2 * NT],
                                    ps1[:, 0:NT], ACTF.Copy)
                se_done.append(cnt.val(s_se))
                q_mark[tg] = p12_quant(tg)
            p12_transpose(NT // 128 - 1)
            def evac_f(e, bank):
                si = sca.activation(tZ[:, e * NT:(e + 1) * NT], bank[:, 0:NT],
                                    ACTF.Identity)
            se_b = cnt.val(s_se)
            for e in range(DC):
                bank = psAB[e % 2]
                if e >= 2:
                    ten.wait_ge(s_se, se_b + e - 1)
                else:
                    ten.wait_ge(s_se, se_b)
                for kc in range(DC):
                    mi = ten.matmul(bank[:, 0:NT],
                                    arH[:, kc * D + e * 128: kc * D + e * 128 + 128],
                                    tB16[:, kc * NT: kc * NT + NT],
                                    start=(kc == 0), stop=(kc == DC - 1))
                mi.then_inc(s_pe, 1); cnt.inc(s_pe, 1)
                sca.wait_ge(s_pe, cnt.val(s_pe))
                evac_f(e, bank)
            vec.wait_ge(s_se, cnt.val(s_se))
            vec.tensor_tensor(tY[:, 0:DC * NT], tZ[:, 0:DC * NT], tX[:, 0:DC * NT],
                              ALU.add)
            vdone()
            ln_stats_norm(NT, tY, tZ)
            vdone()
            sync.wait_ge(s_ve, cnt.val(s_ve))
            sync.dma_start(out=hv(out_t, k * NT, NT),
                           in_=sv(tZ, NT, 0, DC, 0, NT)).then_inc(s_st, 16)
            cnt.inc(s_st, 16)
            tbar()

        _finalize(nc, sync, cnt, s_st, dbg, debug,
                  [("d_x2", x2_s, 0), ("d_x3", x3_s, 0), ("d_x4", x4_s, 0)])
    return nc


def _finalize(nc, sync, cnt, s_st, dbg, debug, dumps):
    if debug:
        for nm, src, _ in dumps:
            if nm in dbg:
                sync.dma_start(out=dbg[nm], in_=src).then_inc(s_st, 16)
                cnt.inc(s_st, 16)
    sync.wait_ge(s_st, cnt.val(s_st))
    nc.all_engine_barrier()


def _hadamard_const():
    h = np.array([[1.0]], dtype=np.float64)
    while h.shape[0] < D:
        h = np.block([[h, h], [h, -h]])
    return np.ascontiguousarray((h / np.sqrt(D)).astype(np.float32))


def _shard_inputs(inputs):
    f = np.float32
    xs = np.asarray(inputs["L0_attn_hidden_states"], f)
    tr = lambda a: np.ascontiguousarray(np.asarray(a, f).T)
    cc = lambda a: np.ascontiguousarray(np.asarray(a, f))
    common = dict(
        wkT=tr(inputs["L0_attn_Wk"]), wvT=tr(inputs["L0_attn_Wv"]),
        wkaT=tr(inputs["L0_attn_Wka"]),
        w1=cc(inputs["L0_ffn_W1"]), w2=cc(inputs["L0_ffn_W2"]),
        wkeT=tr(inputs["L1_attn_W_key"]), wvaT=tr(inputs["L1_attn_W_value"]),
        wreT=tr(inputs["L1_attn_W_receptance"]), woT=tr(inputs["L1_attn_W_output"]),
        l1fw=cc(inputs["L1_ffn_weight"]),
        bk=cc(inputs["L0_attn_bk"]), bv=cc(inputs["L0_attn_bv"]),
        bka=cc(inputs["L0_attn_bka"]),
        gamma=cc(inputs["L0_attn_gamma"]), beta=cc(inputs["L0_attn_beta"]),
        b1=cc(inputs["L0_ffn_b1"]), b2=cc(inputs["L0_ffn_b2"]),
        theta=cc(inputs["L0_ffn_quantum_phases"]),
        qwr=cc(inputs["L0_ffn_quantum_weights_real"]),
        qwi=cc(inputs["L0_ffn_quantum_weights_imag"]),
        mixk=cc(np.asarray(inputs["L1_attn_mix_k"], f).ravel()),
        mixv=cc(np.asarray(inputs["L1_attn_mix_v"], f).ravel()),
        mixr=cc(np.asarray(inputs["L1_attn_mix_r"], f).ravel()),
        td=cc(inputs["L1_attn_time_decay"]), tf=cc(inputs["L1_attn_time_first"]),
        hadc=_hadamard_const(),
    )
    in_maps = []
    for c in range(8):
        b, half = c // 2, c % 2
        sel = np.zeros((1, 8), f)
        if half == 1:
            sel[0, c - 1] = 1.0
        m = dict(common)
        m["x_in"] = np.ascontiguousarray(xs[b, half * T:(half + 1) * T, :].T)
        m["selrow"] = sel
        m["t0c"] = np.array([[float(half * T)]], f)
        in_maps.append(m)
    return in_maps


def kernel(**inputs):
    in_maps = _shard_inputs(inputs)
    nc = build_program(debug=False, upto="out")
    res = run_bass_kernel_spmd(nc, in_maps, core_ids=list(range(8)))
    out = np.empty((B, S, D), np.float32)
    for c in range(8):
        b, half = c // 2, c % 2
        out[b, half * T:(half + 1) * T, :] = res.results[c]["out"].T
    return out


# version 14
# speedup vs baseline: 1.0127x; 1.0127x over previous
import numpy as np
from contextlib import ExitStack

import concourse.bass as bass
import concourse.mybir as mybir
from concourse.bass_utils import run_bass_kernel_spmd

F32 = mybir.dt.float32
BF16 = mybir.dt.bfloat16
U8 = mybir.dt.uint8
ALU = mybir.AluOpType
ACTF = mybir.ActivationFunctionType

B, S, D = 4, 8192, 1024
T = 4096
DC = 8
H = 16
DF, DFC = 4096, 32
NT = 512
KT = T // NT

C_S = 9.94969
C_ES = float(np.exp(np.float64(np.float32(C_S))))
LN4 = 1.3862943611198906
FMAX = 3.4028235e38
MAGIC = 12582912.0
INV2PI = 0.15915494309189535
TWOPI = 6.283185307179586
PIH = 1.5707963267948966
LN_EPS = 1e-5
AG_W = 3104
EXP_INF = 88.72284
EXP_CLAMP = 88.7228
BIG_C = 16384.0
THF = 1.8443e19
C1 = float(0.25 * 2.0 ** -64)
C_RESC = float(np.float32(C_ES) * np.float32(2.0 ** 64))
WKV_SC = float(2.0 ** -32)
WKV_CL = float(np.float32(3.3e38) * 2.0 ** -32)
RW_SC = float(2.0 ** 25)
LN2_64 = float(np.float32(64.0 * np.log(2.0)))
SC70 = float(2.0 ** -70)
P70 = float(2.0 ** 70)
P76 = float(2.0 ** 76)
SC64M = float(2.0 ** -64)
TQ_S = float(np.float32(FMAX) * 2.0 ** -140)
EPS_S = float(np.float32(LN_EPS) * 2.0 ** -64)
LNB_A = float(-32.0 * np.log(2.0))
LNB_B = float(-102.0 * np.log(2.0))
B1C = float(2.0 ** 58)
TBL = float(1e19 * 2.0 ** -140)
C17 = float(np.float32(3.3e38) * 2.0 ** -70)
CY_S = 1.0e6
VAR_TH = 1.0


class Cnt:
    def __init__(self):
        self.c = {}

    def inc(self, s, n):
        self.c[s.name] = self.c.get(s.name, 0) + n
        return self.c[s.name]

    def val(self, s):
        return self.c.get(s.name, 0)


_DRAIN_OPS = {
    "memset", "memzero", "tensor_copy", "tensor_tensor", "tensor_add",
    "tensor_sub", "tensor_mul", "tensor_max", "tensor_scalar",
    "tensor_single_scalar", "tensor_scalar_add", "tensor_scalar_sub",
    "tensor_scalar_mul", "tensor_scalar_max", "tensor_scalar_min",
    "tensor_relu", "tensor_reduce", "reduce_max", "reduce_sum",
    "scalar_tensor_tensor", "tensor_tensor_scan", "select", "copy_predicated",
    "reciprocal", "transpose", "activation", "sign", "square", "sqrt",
    "mul", "add", "copy", "iota", "affine_select",
}


class _Draining:
    def __init__(self, eng):
        self._e = eng

    def __getattr__(self, name):
        f = getattr(self._e, name)
        if name in _DRAIN_OPS:
            def g(*a, **k):
                self._e.drain()
                return f(*a, **k)
            return g
        return f


def _ap_box(ap):
    off = ap.offset
    span = 0
    for stride, num in ap.ap:
        if num > 1:
            span += (num - 1) * abs(stride)
    return (ap.tensor.name, off, off + span)


def _boxes_overlap(a, b):
    return a[0] == b[0] and a[1] <= b[2] and b[1] <= a[2]


class _Chained:

    def __init__(self, eng, sem, cnt):
        self._e = eng
        self._sem = sem
        self._cnt = cnt
        self._hist = []
        self._waited = 0

    def __getattr__(self, name):
        f = getattr(self._e, name)
        if name not in _DRAIN_OPS:
            return f

        def g(*a, **k):
            aps = [x for x in a if isinstance(x, bass.AP)]
            aps += [x for x in k.values() if isinstance(x, bass.AP)]
            wbox = _ap_box(aps[0]) if aps else None
            rboxes = [_ap_box(x) for x in aps[1:]]
            need = 0
            for seq, kind, box in self._hist:
                if seq <= self._waited:
                    continue
                if kind == "W":
                    if (wbox and _boxes_overlap(box, wbox)) or \
                       any(_boxes_overlap(box, rb) for rb in rboxes):
                        need = max(need, seq)
                elif wbox and _boxes_overlap(box, wbox):
                    need = max(need, seq)
            if need > self._waited:
                self._e.wait_ge(self._sem, need)
                self._waited = need
                self._hist = [h for h in self._hist if h[0] > need]
            res = f(*a, **k)
            res.then_inc(self._sem, 1)
            seq = self._cnt.inc(self._sem, 1)
            if wbox:
                self._hist.append((seq, "W", wbox))
            for rb in rboxes:
                self._hist.append((seq, "R", rb))
            if len(self._hist) > 256:
                self._hist = self._hist[-192:]
            return res
        return g


def build_program(debug=False, upto="out"):
    nc = bass.Bass("TRN2", target_bir_lowering=False)
    es = ExitStack()
    cnt = Cnt()
    sync, ten = nc.sync, nc.tensor
    gp = _Draining(nc.gpsimd)

    def din(name, shape):
        return nc.dram_tensor(name, shape, F32, kind="ExternalInput").ap()

    x_in = din("x_in", [D, T])
    wkT = din("wkT", [D, D]); wvT = din("wvT", [D, D]); wkaT = din("wkaT", [D, D])
    w1 = din("w1", [D, DF]); w2 = din("w2", [DF, D])
    wkeT = din("wkeT", [D, D]); wvaT = din("wvaT", [D, D])
    wreT = din("wreT", [D, D]); woT = din("woT", [D, D])
    l1fw = din("l1fw", [D, D])
    v_bk = din("bk", [D]); v_bv = din("bv", [D]); v_bka = din("bka", [D])
    v_g = din("gamma", [D]); v_be = din("beta", [D])
    v_b1 = din("b1", [DF]); v_b2 = din("b2", [D])
    v_th = din("theta", [DF]); v_qr = din("qwr", [DF]); v_qi = din("qwi", [DF])
    v_mk = din("mixk", [D]); v_mv = din("mixv", [D]); v_mr = din("mixr", [D])
    v_td = din("td", [H]); v_tf = din("tf", [H])
    selrow = din("selrow", [1, 8])
    t0c = din("t0c", [1, 1])
    hadc = din("hadc", [D, D])

    out_t = nc.dram_tensor("out", [D, T], F32, kind="ExternalOutput").ap()

    BFD = mybir.dt.bfloat16
    keys_s = nc.dram_tensor("keys_s", [D, T], F32).ap()
    kw_s = nc.dram_tensor("kw_s", [D, T], BFD).ap()
    pool_s = nc.dram_tensor("pool_s", [D, T], BFD).ap()
    x2_s = nc.dram_tensor("x2_s", [D, T], F32).ap()
    vals_s = nc.dram_tensor("vals_s", [D, T], F32).ap()
    x3_s = nc.dram_tensor("x3_s", [D, T], F32).ap()
    x4_s = nc.dram_tensor("x4_s", [D, T], F32).ap()
    ekv_s = nc.dram_tensor("ekv_s", [D, T], F32).ap()
    r_s = nc.dram_tensor("r_s", [D, T], BFD).ap()
    h_s = nc.dram_tensor("h_s", [D, T], BFD).ap()
    w1b_s = nc.dram_tensor("w1b_s", [8, 128, DF], BFD).ap()
    se_s = nc.dram_tensor("se_s", [H, T], F32).ap()
    w2b_s = nc.dram_tensor("w2b_s", [DC, 128, DF], BFD).ap()
    wqT_s = nc.dram_tensor("wqT_s", [D, D], BFD).ap()
    ag_in = [nc.dram_tensor(f"ag_in{i}", [1, AG_W], F32).ap() for i in range(4)]
    ag_out = [nc.dram_tensor(f"ag_out{i}", [8, AG_W], F32).ap() for i in range(4)]

    dbg = {}
    if debug:
        for nm in ["d_x2", "d_x3", "d_x4", "d_keys", "d_ekv", "d_num", "d_wkv",
                   "d_att"]:
            dbg[nm] = nc.dram_tensor(nm, [D, T], F32, kind="ExternalOutput").ap()
        dbg["d_rdn"] = nc.dram_tensor("d_rdn", [16, T], F32, kind="ExternalOutput").ap()

    def hv(hbm, t0, n, c0=0, ncs=DC):
        return hbm.rearrange("(c p) t -> p c t", p=128)[:, c0:c0 + ncs, t0:t0 + n]

    def sv(tile, per_c, c0, ncs, j0, n):
        return tile[:].rearrange("p (c n) -> p c n", n=per_c)[:, c0:c0 + ncs, j0:j0 + n]

    def sbuf(name, shape, dtype):
        return es.enter_context(nc.sbuf_tensor(name, shape, dtype))

    AR = 3 * DC * D
    arH = sbuf("arH", [128, AR], BF16)
    arK = sbuf("arK", [128, AR], BF16)
    tX = sbuf("tX", [128, DC * NT], F32)
    tY = sbuf("tY", [128, DC * NT], F32)
    tZ = sbuf("tZ", [128, DC * NT], F32)
    tB16 = sbuf("tB16", [128, DC * NT], BF16)
    tB16b = sbuf("tB16b", [128, DC * NT], BF16)
    sc = sbuf("sc", [128, DC * NT], F32)
    svA = sbuf("svA", [128, NT], F32)
    svB = sbuf("svB", [128, NT], F32)
    mu8 = sbuf("mu8", [128, DC * NT // 2], U8)

    pat = sbuf("pat", [128, 64], F32)
    PG, PBE, PBK, PBV, PBKA = 0, 8, 16, 24, 32
    PMK, PMV, PMR = 40, 48, 56
    patB = sbuf("patB", [128, 64], F32)
    QLA, QNLA, QEU, QB2 = 0, 8, 16, 24
    patDF = sbuf("patDF", [128, 4 * DFC], F32)
    PA, PB, PB1C, PB1S = 0, DFC, 2 * DFC, 3 * DFC
    patA1 = sbuf("patA1", [128, 2 * DFC], F32)
    binv = sbuf("binv", [128, DC * KT], F32)
    bpow = sbuf("bpow", [128, DC * KT], F32)
    b16k = sbuf("b16k", [16, 2 * KT], F32)
    la16 = sbuf("la16", [16, 8], F32)
    iot = sbuf("iot", [128, NT], F32)
    iotk = sbuf("iotk", [128, KT], F32)
    onesb = sbuf("onesb", [128, 1], BF16)
    ones2d = sbuf("ones2d", [128, 128], F32)
    zerosb = sbuf("zerosb", [128, NT], BF16)
    selc = sbuf("selc", [128, DC * 16], F32)
    selT = sbuf("selT", [16, D], F32)
    ident = sbuf("ident", [128, 128], F32)
    identb = sbuf("identb", [128, 128], BF16)
    onehot8 = sbuf("onehot8", [8, 1], F32)
    carry = sbuf("carry", [128, 48], F32)
    bmt = sbuf("bmt", [128, NT], F32)
    bigk = sbuf("bigk", [128, 8], F32)
    maxP = sbuf("maxP", [128, 8], F32)
    maxN = sbuf("maxN", [128, 8], F32)
    chainP = sbuf("chainP", [128, 8], F32)
    chainN = sbuf("chainN", [128, 8], F32)
    c16 = sbuf("c16", [16, 8], F32)
    lastK = sbuf("lastK", [128, 8], F32)
    lastP = sbuf("lastP", [128, 8], F32)
    lastA = sbuf("lastA", [128, 8], F32)
    sumkw = sbuf("sumkw", [128, 8], F32)
    scr8 = sbuf("scr8", [128, 8], F32)
    scr8b = sbuf("scr8b", [128, 8], F32)
    hrow = sbuf("hrow", [8, 128], F32)
    t0bc = sbuf("t0bc", [128, 1], F32)
    c_eps = sbuf("c_eps", [128, 1], F32)
    c_epsS = sbuf("c_epsS", [128, 1], F32)
    c_lnA = sbuf("c_lnA", [128, 1], F32)
    c_lnB = sbuf("c_lnB", [128, 1], F32)
    cse = sbuf("cse", [16, NT], F32)
    ccb = sbuf("ccb", [16, NT], F32)
    ct1 = sbuf("ct1", [16, NT], F32)
    ct2 = sbuf("ct2", [16, NT], F32)
    crd = sbuf("crd", [16, NT], F32)
    mu16 = sbuf("mu16", [16, NT], U8)

    ps0 = es.enter_context(nc.psum_tensor("ps0", [128, NT], F32))
    ps1 = es.enter_context(nc.psum_tensor("ps1", [128, NT], F32))
    ps2 = es.enter_context(nc.psum_tensor("ps2", [128, NT], F32))
    ps3 = es.enter_context(nc.psum_tensor("ps3", [128, NT], F32))
    psBig = es.enter_context(nc.psum_tensor("psBig", [128, 2 * NT], F32))
    psT = es.enter_context(nc.psum_tensor("psT", [128, NT], BF16))
    psAB = [ps0, ps1]

    s_ld = es.enter_context(nc.semaphore("s_ld"))
    s_gd = es.enter_context(nc.semaphore("s_gd"))
    s_st = es.enter_context(nc.semaphore("s_st"))
    s_ag = es.enter_context(nc.semaphore("s_ag"))
    s_pe = es.enter_context(nc.semaphore("s_pe"))
    s_ve = es.enter_context(nc.semaphore("s_ve"))
    s_se = es.enter_context(nc.semaphore("s_se"))
    s_gp = es.enter_context(nc.semaphore("s_gp"))
    s_h = es.enter_context(nc.semaphore("s_h"))
    s_pc = es.enter_context(nc.semaphore("s_pc"))
    s_sl0 = es.enter_context(nc.semaphore("s_sl0"))
    s_sl1 = es.enter_context(nc.semaphore("s_sl1"))
    s_la = es.enter_context(nc.semaphore("s_la"))

    vec = _Chained(nc.vector, s_ve, cnt)
    sca = _Chained(nc.scalar, s_se, cnt)

    def bar():
        for s in (s_ld, s_st, s_gd, s_h, s_pc, s_sl0, s_sl1, s_la):
            if cnt.val(s):
                sync.wait_ge(s, cnt.val(s))
        nc.all_engine_barrier()
        for w in (vec, sca):
            w._waited = cnt.val(w._sem)
            w._hist = []

    def tbar():
        sync.wait_ge(s_ve, cnt.val(s_ve))
        sync.wait_ge(s_se, cnt.val(s_se))
        sync.wait_ge(s_pe, cnt.val(s_pe))
        for s in (s_st, s_h):
            if cnt.val(s):
                sync.wait_ge(s, cnt.val(s))

    def vdone():
        pass

    W = {}

    with es:
        def load_pat(dst, col0, v, ncols):
            sync.dma_start(out=dst[:, col0:col0 + ncols],
                           in_=v.rearrange("(c p) -> p c", p=128),
                           allow_slow_non_contiguous=True).then_inc(s_ld, 16)
            cnt.inc(s_ld, 16)

        load_pat(pat, PG, v_g, 8); load_pat(pat, PBE, v_be, 8)
        load_pat(pat, PBK, v_bk, 8); load_pat(pat, PBV, v_bv, 8)
        load_pat(pat, PBKA, v_bka, 8)
        load_pat(pat, PMK, v_mk, 8); load_pat(pat, PMV, v_mv, 8)
        load_pat(pat, PMR, v_mr, 8)
        load_pat(patB, QB2, v_b2, 8)
        load_pat(patA1, 0, v_b1, DFC)
        load_pat(patDF, PA, v_th, DFC)
        load_pat(patDF, PB, v_qr, DFC)
        load_pat(patDF, PB1C, v_qi, DFC)
        sync.dma_start(out=la16[:, 6:7], in_=v_td[:, None]).then_inc(s_ld, 16); cnt.inc(s_ld, 16)
        sync.dma_start(out=la16[:, 7:8], in_=v_tf[:, None]).then_inc(s_ld, 16); cnt.inc(s_ld, 16)
        sync.dma_start(out=onehot8[:], in_=selrow.rearrange("o r -> r o")).then_inc(s_ld, 16)
        cnt.inc(s_ld, 16)
        sync.dma_start(out=t0bc[:], in_=bass.AP(t0c.tensor, 0, [[0, 128], [1, 1]]),
                       allow_slow_non_contiguous=True).then_inc(s_ld, 16)
        cnt.inc(s_ld, 16)

        def wload(dst, col0, srcT, cols=D):
            gp.dma_start(
                out=dst[:].rearrange("p (c e) -> p c e", e=cols)[:, col0 // cols:(col0 // cols) + srcT.shape[0] // 128, :],
                in_=srcT.rearrange("(c p) e -> p c e", p=128)).then_inc(s_gd, 16)
            cnt.inc(s_gd, 16)

        wload(arK, 0, wkT)
        wload(arK, 8 * D, wkaT)
        gp.memset(zerosb[:], 0.0)
        gp.memset(c_eps[:], LN_EPS)
        gp.memset(c_epsS[:], EPS_S)
        gp.memset(c_lnA[:], LNB_A)
        gp.memset(c_lnB[:], LNB_B)
        gp.memset(ones2d[:], 1.0)
        gp.memset(onesb[:], 1.0)
        gp.memset(carry[:], 0.0)
        gp.memset(c16[:], 0.0)
        gp.memset(lastK[:], 0.0)
        gp.memset(lastP[:], 0.0)
        gp.memset(lastA[:], 0.0)
        gp.memset(sumkw[:], 0.0)
        gp.iota(iot[:], [[1, NT]], base=0, channel_multiplier=0,
                allow_small_or_imprecise_dtypes=True)
        gp.iota(iotk[:], [[1, KT]], base=0, channel_multiplier=0,
                allow_small_or_imprecise_dtypes=True)
        def gstep(op):
            op.then_inc(s_gp, 1)
            cnt.inc(s_gp, 1)
            gp.wait_ge(s_gp, cnt.val(s_gp))

        gstep(gp.memset(ident[:], 1.0))
        gstep(gp.affine_select(ident[:], ident[:], [[1, 128]], ALU.is_equal, 0.0,
                               base=0, channel_multiplier=-1))
        gstep(gp.memset(identb[:], 1.0))
        gstep(gp.affine_select(identb[:], identb[:], [[1, 128]], ALU.is_equal, 0.0,
                               base=0, channel_multiplier=-1))
        gstep(gp.memset(selT[:], 1.0))
        gstep(gp.affine_select(selT[:], selT[:], [[1, D]], ALU.is_ge, 0.0,
                               base=0, channel_multiplier=-64))
        gstep(gp.affine_select(selT[:], selT[:], [[-1, D]], ALU.is_ge, 0.0,
                               base=63, channel_multiplier=64))
        gstep(gp.memset(selc[:], 1.0))
        for c in range(DC):
            slc = selc[:, c * 16:(c + 1) * 16]
            gstep(gp.affine_select(slc, slc, [[-64, 16]], ALU.is_ge, 0.0,
                                   base=128 * c, channel_multiplier=1))
            gstep(gp.affine_select(slc, slc, [[64, 16]], ALU.is_ge, 0.0,
                                   base=63 - 128 * c, channel_multiplier=-1))
        gp_done = cnt.val(s_gd)
        gp_done2 = cnt.val(s_gp)

        for e in (vec, sca, ten):
            e.wait_ge(s_ld, cnt.val(s_ld))
            e.wait_ge(s_gd, gp_done)
            e.wait_ge(s_gp, gp_done2)
        sca.activation(la16[:, 5:6], la16[:, 6:7], ACTF.Exp)
        sca.activation(la16[:, 4:5], la16[:, 5:6], ACTF.Exp, scale=-1.0)
        sca.activation(la16[:, 0:1], la16[:, 4:5], ACTF.Ln)
        si = sca.activation(la16[:, 2:3], la16[:, 7:8], ACTF.Exp)
        vec.wait_ge(s_se, cnt.val(s_se))
        vec.tensor_scalar(la16[:, 1:2], la16[:, 0:1], -1.0, None, ALU.mult)
        th = patDF[:, PA:PA + DFC]; qwr = patDF[:, PB:PB + DFC]
        qwi = patDF[:, PB1C:PB1C + DFC]
        thr = patA1[:, DFC:2 * DFC]
        vec.tensor_scalar(thr, th, INV2PI, MAGIC, ALU.mult, ALU.add)
        vec.tensor_scalar(thr, thr, MAGIC, None, ALU.subtract)
        vec.scalar_tensor_tensor(thr, thr, -TWOPI, th, ALU.mult, ALU.add)
        sth = tX[:, 0:DFC]; cth = tX[:, DFC:2 * DFC]
        s2 = tX[:, 2 * DFC:3 * DFC]
        vec.tensor_scalar(s2, th, 1.0, PIH, ALU.mult, ALU.add)
        vec.tensor_scalar(tX[:, 3 * DFC:4 * DFC], s2, INV2PI, MAGIC, ALU.mult, ALU.add)
        vec.tensor_scalar(tX[:, 3 * DFC:4 * DFC], tX[:, 3 * DFC:4 * DFC], MAGIC,
                          None, ALU.subtract)
        vec.scalar_tensor_tensor(s2, tX[:, 3 * DFC:4 * DFC], -TWOPI, s2,
                                 ALU.mult, ALU.add)
        vdone()
        sca.wait_ge(s_ve, cnt.val(s_ve))
        sca.activation(sth, thr, ACTF.Sin)
        si = sca.activation(cth, s2, ACTF.Sin)
        vec.wait_ge(s_se, cnt.val(s_se))
        qp = tY[:, 0:DFC]; scrD = tY[:, DFC:2 * DFC]
        vec.tensor_tensor(qp, cth, qwr, ALU.mult)
        vec.tensor_tensor(scrD, sth, qwi, ALU.mult)
        vec.tensor_tensor(qp, qp, scrD, ALU.add)
        vec.tensor_tensor(scrD, qp, qwi, ALU.mult)
        vec.tensor_scalar(scrD, scrD, 0.1, None, ALU.mult)
        vec.tensor_tensor(tY[:, 2 * DFC:3 * DFC], scrD, sth, ALU.mult)
        vec.tensor_tensor(tY[:, 3 * DFC:4 * DFC], scrD, cth, ALU.mult)
        vec.tensor_copy(patDF[:, PA:PA + DFC], tY[:, 2 * DFC:3 * DFC])
        vec.tensor_copy(patDF[:, PB:PB + DFC], tY[:, 3 * DFC:4 * DFC])
        vec.tensor_scalar(patDF[:, PB1C:PB1C + DFC], patA1[:, 0:DFC], 0.1, PIH,
                          ALU.mult, ALU.add)
        vec.tensor_scalar(patDF[:, PB1S:PB1S + DFC], patA1[:, 0:DFC], 0.1, None,
                          ALU.mult)
        vec.tensor_scalar(pat[:, PBKA:PBKA + 8], pat[:, PBKA:PBKA + 8],
                          1.0 / 32.0, None, ALU.mult)
        vdone()

        ten.wait_ge(s_ve, cnt.val(s_ve))
        for (colsrc, dstcol) in [(0, QLA), (1, QNLA), (2, QEU)]:
            for c in range(DC):
                mi = ten.matmul(ps2[:, c:c + 1], selT[:, c * 128:(c + 1) * 128],
                                la16[:, colsrc:colsrc + 1], start=True, stop=True)
            mi.then_inc(s_pe, 1); cnt.inc(s_pe, 1)
            vec.wait_ge(s_pe, cnt.val(s_pe))
            vec.tensor_copy(patB[:, dstcol:dstcol + 8], ps2[:, 0:8])
            vdone()
            ten.wait_ge(s_ve, cnt.val(s_ve))

        for c in range(DC):
            nla = patB[:, QNLA + c:QNLA + c + 1]
            la_ = patB[:, QLA + c:QLA + c + 1]
            vec.tensor_tensor(scr8[:, 0:1], nla, t0bc[:], ALU.mult)
            vec.tensor_scalar(scr8[:, 1:2], nla, 1.0, LN4, ALU.mult, ALU.add)
            vec.tensor_tensor(scr8[:, 0:1], scr8[:, 0:1], scr8[:, 1:2], ALU.add)
            vec.tensor_scalar(scr8[:, 2:3], nla, float(NT), None, ALU.mult)
            vec.tensor_scalar(svA[:, 0:KT], iotk[:, 0:KT], scr8[:, 2:3], None, ALU.mult)
            vec.tensor_scalar(binv[:, c * KT:(c + 1) * KT], svA[:, 0:KT],
                              scr8[:, 0:1], None, ALU.add)
            vec.tensor_tensor(scr8[:, 3:4], la_, t0bc[:], ALU.mult)
            vec.tensor_scalar(scr8[:, 3:4], scr8[:, 3:4], 1.0, -float(C_S), ALU.mult, ALU.add)
            vec.tensor_scalar(scr8[:, 4:5], la_, float(NT), None, ALU.mult)
            vec.tensor_scalar(svA[:, 0:KT], iotk[:, 0:KT], scr8[:, 4:5], None, ALU.mult)
            vec.tensor_scalar(bpow[:, c * KT:(c + 1) * KT], svA[:, 0:KT],
                              scr8[:, 3:4], None, ALU.add)
        vec.tensor_tensor(la16[:, 3:4], la16[:, 1:2], t0bc[0:16, :], ALU.mult)
        vec.tensor_scalar(la16[:, 6:7], la16[:, 1:2], 1.0, LN4, ALU.mult, ALU.add)
        vec.tensor_tensor(la16[:, 3:4], la16[:, 3:4], la16[:, 6:7], ALU.add)
        vec.tensor_scalar(la16[:, 6:7], la16[:, 1:2], float(NT), None, ALU.mult)
        vec.tensor_scalar(ct1[:, 0:KT], iotk[0:16, 0:KT], la16[:, 6:7], None, ALU.mult)
        vec.tensor_scalar(b16k[:, 0:KT], ct1[:, 0:KT], la16[:, 3:4], None, ALU.add)
        vec.tensor_tensor(la16[:, 3:4], la16[:, 0:1], t0bc[0:16, :], ALU.mult)
        vec.tensor_scalar(la16[:, 3:4], la16[:, 3:4], 1.0, -float(C_S), ALU.mult, ALU.add)
        vec.tensor_scalar(la16[:, 6:7], la16[:, 0:1], float(NT), None, ALU.mult)
        vec.tensor_scalar(ct1[:, 0:KT], iotk[0:16, 0:KT], la16[:, 6:7], None, ALU.mult)
        vec.tensor_scalar(b16k[:, KT:2 * KT], ct1[:, 0:KT], la16[:, 3:4], None, ALU.add)
        vec.memset(tZ[:], 0.0)
        vdone()
        sync.wait_ge(s_ve, cnt.val(s_ve))
        for i in range(4):
            sync.dma_start(out=ag_in[i], in_=tZ[0:1, 0:AG_W]).then_inc(s_st, 16)
            cnt.inc(s_st, 16)
        vec.tensor_scalar(bigk[:, 0:KT], iotk[:, 0:KT], -float(NT), BIG_C,
                          ALU.mult, ALU.add)
        vec.tensor_scalar(bigk[:, 0:KT], bigk[:, 0:KT], t0bc[:], None,
                          ALU.subtract)
        vdone()
        bar()

        evac_cnt = {"v": 0}

        def ln_stats_norm(n, y_tile, out_tile, armored=False):
            if armored:
                vec.tensor_scalar(tX[:, 0:DC * n], y_tile[:, 0:DC * n],
                                  B1C, -B1C, ALU.min, ALU.max)
                vdone()
                sca.wait_ge(s_ve, cnt.val(s_ve))
                si = sca.activation(tB16[:, 0:DC * n], tX[:, 0:DC * n], ACTF.Square)
                ten.wait_ge(s_se, cnt.val(s_se))
                for c in range(DC):
                    mi = ten.matmul(ps2[64:65, 0:n], onesb[:, 0:1],
                                    tB16[:, c * n:(c + 1) * n],
                                    start=(c == 0), stop=(c == DC - 1))
                mi.then_inc(s_pe, 1); cnt.inc(s_pe, 1)
                vec.wait_ge(s_pe, cnt.val(s_pe))
                vec.tensor_scalar(tX[:, 0:DC * n], y_tile[:, 0:DC * n],
                                  SC70, None, ALU.mult)
                vec.tensor_copy(tB16b[:, 0:DC * n], tX[:, 0:DC * n])
                vdone()
                sca.wait_ge(s_ve, cnt.val(s_ve))
                sca.wait_ge(s_pe, cnt.val(s_pe))
                si = sca.activation(tB16[:, 0:DC * n], tX[:, 0:DC * n], ACTF.Square)
            else:
                vec.tensor_copy(tB16b[:, 0:DC * n], y_tile[:, 0:DC * n])
                vdone()
                sca.wait_ge(s_ve, cnt.val(s_ve))
                si = sca.activation(tB16[:, 0:DC * n], y_tile[:, 0:DC * n], ACTF.Square)
            ten.wait_ge(s_ve, cnt.val(s_ve))
            ten.wait_ge(s_se, cnt.val(s_se))
            for c in range(DC):
                mi = ten.matmul(ps2[0:1, 0:n], onesb[:, 0:1],
                                tB16b[:, c * n:(c + 1) * n],
                                start=(c == 0), stop=(c == DC - 1))
            for c in range(DC):
                mi = ten.matmul(ps2[32:33, 0:n], onesb[:, 0:1],
                                tB16[:, c * n:(c + 1) * n],
                                start=(c == 0), stop=(c == DC - 1))
            mi.then_inc(s_pe, 1); cnt.inc(s_pe, 1)
            vec.wait_ge(s_pe, cnt.val(s_pe))
            mu = svA[0:1, 0:n]; var = svB[0:1, 0:n]
            rstd = ct1[0:1, 0:n]; c0r = ct2[0:1, 0:n]; msk = crd[0:1, 0:n]
            vec.tensor_scalar(mu, ps2[0:1, 0:n], 1.0 / D, None, ALU.mult)
            vec.tensor_scalar(var, ps2[32:33, 0:n], 1.0 / D, None, ALU.mult)
            if armored:
                aarg = tX[0:1, 0:n]; barg = tX[0:1, n:2 * n]
                mbl = tX[0:1, 2 * n:3 * n]; muc = tX[0:1, 3 * n:4 * n]
                vlo = tX[0:1, 4 * n:5 * n]; rb = tZ[0:1, 0:n]
                vec.tensor_scalar(msk, ps2[32:33, 0:n], TQ_S, None, ALU.is_lt)
                vec.tensor_tensor(c0r, mu, mu, ALU.mult)
                vec.tensor_tensor(var, var, c0r, ALU.subtract)
                vec.tensor_scalar(muc, mu, P70, B1C, ALU.mult, ALU.min)
                vec.tensor_scalar(muc, muc, -B1C, None, ALU.max)
                vec.tensor_tensor(muc, muc, muc, ALU.mult)
                vec.scalar_tensor_tensor(vlo, ps2[64:65, 0:n], 1.0 / D,
                                         muc, ALU.mult, ALU.subtract)
                vec.tensor_scalar(aarg, vlo, 1.5e19, 0.0, ALU.min, ALU.max)
                vec.tensor_scalar(barg, var, SC64M, 1e-30, ALU.mult, ALU.max)
                vec.tensor_scalar(mbl, var, TBL, None, ALU.is_gt)
                vdone()
                sca.wait_ge(s_ve, cnt.val(s_ve))
                sca.activation(rstd, aarg, ACTF.Ln, bias=c_eps[0:1, 0:1])
                sca.activation(rstd, rstd, ACTF.Exp, scale=-0.5)
                sca.activation(rb, barg, ACTF.Ln)
                si = sca.activation(rb, rb, ACTF.Exp, scale=-0.5,
                                    bias=c_lnB[0:1, 0:1])
                vec.wait_ge(s_se, cnt.val(s_se))
                vec.tensor_tensor(rb, rb, rstd, ALU.subtract)
                vec.tensor_tensor(rb, rb, mbl, ALU.mult)
                vec.tensor_tensor(rstd, rstd, rb, ALU.add)
                vec.tensor_tensor(rstd, rstd, msk, ALU.mult)
                vec.scalar_tensor_tensor(c0r, mu, -1.0, rstd, ALU.mult, ALU.mult)
                vec.tensor_scalar(c0r, c0r, C17, -C17, ALU.min, ALU.max)
                vec.tensor_scalar(c0r, c0r, P70, None, ALU.mult)
                vdone()
            else:
                vec.tensor_tensor(msk, mu, mu, ALU.mult)
                vec.tensor_tensor(var, var, msk, ALU.subtract)
                vdone()
                sca.wait_ge(s_ve, cnt.val(s_ve))
                sca.activation(rstd, var, ACTF.Ln, bias=c_eps[0:1, 0:1])
                si = sca.activation(rstd, rstd, ACTF.Exp, scale=-0.5)
                vec.wait_ge(s_se, cnt.val(s_se))
                vec.tensor_scalar(msk, var, 0.0, 0.0, ALU.mult, ALU.is_equal)
                vec.tensor_tensor(rstd, rstd, msk, ALU.mult)
                vec.scalar_tensor_tensor(c0r, mu, -1.0, rstd, ALU.mult, ALU.mult)
                vdone()
            ten.wait_ge(s_ve, cnt.val(s_ve))
            ten.matmul(ps3[:, 0:n], ones2d[0:1, :], rstd, start=True, stop=True)
            mi = ten.matmul(psBig[:, 0:n], ones2d[0:1, :], c0r, start=True, stop=True)
            mi.then_inc(s_pe, 1); cnt.inc(s_pe, 1)
            vec.wait_ge(s_pe, cnt.val(s_pe))
            for c in range(DC):
                slq = slice(c * n, (c + 1) * n)
                vec.tensor_tensor(out_tile[:, slq], y_tile[:, slq], ps3[:, 0:n], ALU.mult)
                vec.tensor_tensor(out_tile[:, slq], out_tile[:, slq], psBig[:, 0:n], ALU.add)

        for hh in range(2):
            gp.dma_start(
                out=arH[:, 0:4 * DF].rearrange("p (c f) -> p c f", f=DF),
                in_=w1[hh * 512:(hh + 1) * 512, :].rearrange("(c p) f -> p c f", p=128)
            ).then_inc(s_gd, 16)
            cnt.inc(s_gd, 16)
            gp.wait_ge(s_gd, cnt.val(s_gd))
            for q in range(8):
                gp.dma_start(
                    out=w1b_s[q].rearrange("p (kc v) -> p kc v", v=NT)[:, hh * 4:(hh + 1) * 4, :],
                    in_=arH[:, 0:4 * DF].rearrange("p (c f) -> p c f", f=DF)[:, :, q * NT:(q + 1) * NT]
                ).then_inc(s_pc, 16)
                cnt.inc(s_pc, 16)
            gp.wait_ge(s_pc, cnt.val(s_pc))
        for hh in range(4):
            gp.dma_start(
                out=arH[:, 0:8 * D].rearrange("p (c f) -> p c f", f=D),
                in_=w2[hh * 1024:(hh + 1) * 1024, :].rearrange("(c p) f -> p c f", p=128)
            ).then_inc(s_gd, 16)
            cnt.inc(s_gd, 16)
            gp.wait_ge(s_gd, cnt.val(s_gd))
            for e in range(DC):
                gp.dma_start(
                    out=w2b_s[e].rearrange("p (fc j) -> p fc j", j=128)[:, hh * 8:(hh + 1) * 8, :],
                    in_=arH[:, 0:8 * D].rearrange("p (c f) -> p c f", f=D)[:, :, e * 128:(e + 1) * 128]
                ).then_inc(s_pc, 16)
                cnt.inc(s_pc, 16)
            gp.wait_ge(s_pc, cnt.val(s_pc))


        for k in range(KT):
            vec.wait_ge(s_st, cnt.val(s_st))
            sca.wait_ge(s_st, cnt.val(s_st))
            sync.dma_start(out=sv(tX, NT, 0, DC, 0, NT),
                           in_=hv(x_in, k * NT, NT)).then_inc(s_ld, 16)
            cnt.inc(s_ld, 16)
            vec.wait_ge(s_ld, cnt.val(s_ld))
            sca.wait_ge(s_ld, cnt.val(s_ld))
            ln_stats_norm(NT, tX, tY)
            for c in range(DC):
                slq = slice(c * NT, (c + 1) * NT)
                vec.tensor_scalar(tB16[:, slq],
                                  tY[:, slq], pat[:, PG + c:PG + c + 1],
                                  pat[:, PBE + c:PBE + c + 1], ALU.mult, ALU.add)
            vdone()
            sync.wait_ge(s_ve, cnt.val(s_ve))
            sync.dma_start(out=hv(h_s, k * NT, NT),
                           in_=sv(tB16, NT, 0, DC, 0, NT)).then_inc(s_h, 16)
            cnt.inc(s_h, 16)
            ten.wait_ge(s_ve, cnt.val(s_ve))
            se_b = cnt.val(s_se)
            for e in range(DC):
                bank = psAB[e % 2]
                if e >= 2:
                    ten.wait_ge(s_se, se_b + e - 1)
                else:
                    ten.wait_ge(s_se, se_b)
                for kc in range(DC):
                    mi = ten.matmul(bank[:, 0:NT],
                                    arK[:, kc * D + e * 128: kc * D + e * 128 + 128],
                                    tB16[:, kc * NT: kc * NT + NT],
                                    start=(kc == 0), stop=(kc == DC - 1))
                mi.then_inc(s_pe, 1); cnt.inc(s_pe, 1)
                sca.wait_ge(s_pe, cnt.val(s_pe))
                si = sca.activation(tZ[:, e * NT:(e + 1) * NT], bank[:, 0:NT],
                                    ACTF.Identity, bias=pat[:, PBK + e:PBK + e + 1])
            vec.wait_ge(s_se, cnt.val(s_se))
            vec.wait_ge(s_pe, cnt.val(s_pe))
            vec.wait_ge(s_h, cnt.val(s_h))
            vec.tensor_copy(tB16[:, 0:DC * NT], tZ[:, 0:DC * NT])
            vdone()
            sync.wait_ge(s_se, cnt.val(s_se))
            sync.dma_start(out=hv(keys_s, k * NT, NT),
                           in_=sv(tZ, NT, 0, DC, 0, NT)).then_inc(s_st, 16)
            cnt.inc(s_st, 16)
            ten.wait_ge(s_ve, cnt.val(s_ve))
            se_b = cnt.val(s_se)
            for e in range(DC):
                bank = psAB[e % 2]
                if e >= 2:
                    ten.wait_ge(s_se, se_b + e - 1)
                else:
                    ten.wait_ge(s_se, se_b)
                for kc in range(DC):
                    mi = ten.matmul(bank[:, 0:NT],
                                    arK[:, (8 + kc) * D + e * 128: (8 + kc) * D + e * 128 + 128],
                                    tB16[:, kc * NT: kc * NT + NT],
                                    start=(kc == 0), stop=(kc == DC - 1))
                mi.then_inc(s_pe, 1); cnt.inc(s_pe, 1)
                sca.wait_ge(s_pe, cnt.val(s_pe))
                si = sca.activation(sc[:, e * NT:(e + 1) * NT], bank[:, 0:NT],
                                    ACTF.Exp, scale=1.0 / 32.0,
                                    bias=pat[:, PBKA + e:PBKA + e + 1])
                vec.wait_ge(s_se, cnt.val(s_se))
                vec.tensor_copy(tB16b[:, e * NT:(e + 1) * NT], sc[:, e * NT:(e + 1) * NT])
                vec.tensor_reduce(scr8[:, e:e + 1], sc[:, e * NT:(e + 1) * NT],
                                  mybir.AxisListType.X, ALU.add)
                vec.tensor_tensor(sumkw[:, e:e + 1], sumkw[:, e:e + 1],
                                  scr8[:, e:e + 1], ALU.add)
            vdone()
            sync.wait_ge(s_ve, cnt.val(s_ve))
            sync.dma_start(out=hv(kw_s, k * NT, NT),
                           in_=sv(tB16b, NT, 0, DC, 0, NT)).then_inc(s_st, 16)
            cnt.inc(s_st, 16)
            tbar()

        def allgather_carry(idx, src_cols, ccol0, with16=False):
            ten.wait_ge(s_ve, cnt.val(s_ve))
            ten.transpose(ps2[0:8, 0:128], src_cols[:], ident[:]).then_inc(s_pe, 1)
            cnt.inc(s_pe, 1)
            sca.wait_ge(s_pe, cnt.val(s_pe))
            sca.activation(hrow[:], ps2[0:8, 0:128], ACTF.Copy)
            sync.wait_ge(s_se, cnt.val(s_se))
            sync.dma_start(out=ag_in[idx][:, 0:1024].rearrange("o (r c) -> r (o c)", r=8),
                           in_=hrow[:]).then_inc(s_la, 16)
            cnt.inc(s_la, 16)
            if with16:
                sync.dma_start(out=ag_in[idx][:, 1024:1040].rearrange("o (w u) -> (o w) u", u=1),
                               in_=c16[:, 1:2]).then_inc(s_la, 16)
                cnt.inc(s_la, 16)
            gp.wait_ge(s_la, cnt.val(s_la))
            gp.collective_compute("AllGather", ALU.bypass,
                                  replica_groups=[list(range(8))],
                                  ins=[ag_in[idx].opt()],
                                  outs=[ag_out[idx].opt()]).then_inc(s_ag, 1)
            cnt.inc(s_ag, 1)
            sync.wait_ge(s_ag, cnt.val(s_ag))
            gselv = sc[0:8, 0:AG_W]
            sync.dma_start(out=gselv, in_=ag_out[idx]).then_inc(s_la, 16)
            cnt.inc(s_la, 16)
            ten.wait_ge(s_la, cnt.val(s_la))
            ten.matmul(ps2[0:1, 0:NT], onehot8[:], gselv[:, 0:NT], start=True, stop=True)
            mi = ten.matmul(ps2[32:33, 0:NT], onehot8[:], gselv[:, NT:2 * NT],
                            start=True, stop=True)
            if with16:
                mi = ten.matmul(ps2[64:65, 0:16], onehot8[:], gselv[:, 1024:1040],
                                start=True, stop=True)
            mi.then_inc(s_pe, 1); cnt.inc(s_pe, 1)
            sca.wait_ge(s_pe, cnt.val(s_pe))
            sca.activation(svA[0:1, 0:NT], ps2[0:1, 0:NT], ACTF.Copy)
            si = sca.activation(svA[32:33, 0:NT], ps2[32:33, 0:NT], ACTF.Copy)
            if with16:
                si = sca.activation(svA[64:65, 0:16], ps2[64:65, 0:16], ACTF.Copy)
            ten.wait_ge(s_se, cnt.val(s_se))
            for c in range(DC):
                rsrc = (svA[0:1, c * 128:(c + 1) * 128] if c < 4 else
                        svA[32:33, (c - 4) * 128:(c - 3) * 128])
                idk = ones2d[0:1, 0:1] if c < 4 else ones2d[32:33, 0:1]
                ten.transpose(ps3[:, 0:1], rsrc, idk).then_inc(s_pe, 1)
                cnt.inc(s_pe, 1)
                vec.wait_ge(s_pe, cnt.val(s_pe))
                vec.tensor_copy(carry[:, ccol0 + c:ccol0 + c + 1], ps3[:, 0:1])
                vdone()
                ten.wait_ge(s_ve, cnt.val(s_ve))
            if with16:
                ten.transpose(ps3[0:16, 0:1], svA[64:65, 0:16],
                              ones2d[64:65, 0:1]).then_inc(s_pe, 1)
                cnt.inc(s_pe, 1)
                vec.wait_ge(s_pe, cnt.val(s_pe))
                vec.tensor_copy(c16[:, 0:1], ps3[0:16, 0:1])
            bar()

        allgather_carry(0, sumkw, 0)
        vec.tensor_copy(lastK[:], carry[:, 0:8])
        vdone()
        bar()

        gp.wait_ge(s_pe, cnt.val(s_pe))
        wload(arK, 0, wvT)
        ten.wait_ge(s_gd, cnt.val(s_gd))
        for k in range(KT):
            vec.wait_ge(s_st, cnt.val(s_st))
            sca.wait_ge(s_st, cnt.val(s_st))
            sync.dma_start(out=sv(tX, NT, 0, DC, 0, NT),
                           in_=hv(keys_s, k * NT, NT)).then_inc(s_ld, 16)
            cnt.inc(s_ld, 16)
            sync.dma_start(out=sv(tB16, NT, 0, DC, 0, NT),
                           in_=hv(kw_s, k * NT, NT)).then_inc(s_ld, 16)
            cnt.inc(s_ld, 16)
            sync.dma_start(out=sv(arH, NT, 0, DC, 0, NT),
                           in_=hv(h_s, k * NT, NT)).then_inc(s_ld, 16)
            cnt.inc(s_ld, 16)
            ten.wait_ge(s_ld, cnt.val(s_ld))
            vp_done = []
            for e in range(DC):
                bank = psAB[e % 2]
                if e >= 2:
                    ten.wait_ge(s_se, vp_done[e - 2])
                else:
                    ten.wait_ge(s_se, cnt.val(s_se))
                for kc in range(DC):
                    mi = ten.matmul(bank[:, 0:NT],
                                    arK[:, kc * D + e * 128: kc * D + e * 128 + 128],
                                    arH[:, kc * NT: kc * NT + NT],
                                    start=(kc == 0), stop=(kc == DC - 1))
                mi.then_inc(s_pe, 1); cnt.inc(s_pe, 1)
                sca.wait_ge(s_pe, cnt.val(s_pe))
                si = sca.activation(sc[:, e * NT:(e + 1) * NT], bank[:, 0:NT],
                                    ACTF.Identity, bias=pat[:, PBV + e:PBV + e + 1])
                vp_done.append(cnt.val(s_se))
            sync.wait_ge(s_se, cnt.val(s_se))
            sync.dma_start(out=hv(vals_s, k * NT, NT),
                           in_=sv(sc, NT, 0, DC, 0, NT)).then_inc(s_st, 16)
            cnt.inc(s_st, 16)
            vec.wait_ge(s_ld, cnt.val(s_ld))
            for c in range(DC):
                slq = slice(c * NT, (c + 1) * NT)
                vec.tensor_tensor_scan(tY[:, slq], tB16[:, slq], zerosb[:, 0:NT],
                                       lastK[:, c:c + 1], ALU.add, ALU.add)
                vec.tensor_copy(lastK[:, c:c + 1], tY[:, c * NT + NT - 1:c * NT + NT])
            vdone()
            sca.wait_ge(s_ve, cnt.val(s_ve))
            sca.activation(tZ[:, 0:DC * NT], tY[:, 0:DC * NT], ACTF.Ln)
            si = sca.activation(tZ[:, 0:DC * NT], tZ[:, 0:DC * NT], ACTF.Exp, scale=-1.0)
            vec.wait_ge(s_se, cnt.val(s_se))
            for c in range(DC):
                slq = slice(c * NT, (c + 1) * NT)
                vec.tensor_tensor(tY[:, slq], tB16[:, slq], tX[:, slq], ALU.mult)
                vec.tensor_tensor(tY[:, slq], tY[:, slq], tZ[:, slq], ALU.mult)
            for c in range(DC):
                slq = slice(c * NT, (c + 1) * NT)
                vec.tensor_tensor_scan(tZ[:, slq], tY[:, slq], zerosb[:, 0:NT],
                                       lastP[:, c:c + 1], ALU.add, ALU.add)
                vec.tensor_copy(lastP[:, c:c + 1], tZ[:, c * NT + NT - 1:c * NT + NT])
            vec.tensor_copy(tB16b[:, 0:DC * NT], tZ[:, 0:DC * NT])
            vdone()
            sync.wait_ge(s_ve, cnt.val(s_ve))
            sync.dma_start(out=hv(pool_s, k * NT, NT),
                           in_=sv(tB16b, NT, 0, DC, 0, NT)).then_inc(s_st, 16)
            cnt.inc(s_st, 16)
            tbar()

        allgather_carry(1, lastP, 8)

        for k in range(KT):
            vec.wait_ge(s_st, cnt.val(s_st))
            sca.wait_ge(s_st, cnt.val(s_st))
            sync.dma_start(out=sv(tX, NT, 0, DC, 0, NT),
                           in_=hv(x_in, k * NT, NT)).then_inc(s_ld, 16)
            cnt.inc(s_ld, 16)
            sync.dma_start(out=sv(tB16b, NT, 0, DC, 0, NT),
                           in_=hv(pool_s, k * NT, NT)).then_inc(s_ld, 16)
            cnt.inc(s_ld, 16)
            sync.dma_start(out=sv(sc, NT, 0, DC, 0, NT),
                           in_=hv(vals_s, k * NT, NT)).then_inc(s_ld, 16)
            cnt.inc(s_ld, 16)
            vec.wait_ge(s_ld, cnt.val(s_ld))
            for c in range(DC):
                slq = slice(c * NT, (c + 1) * NT)
                vec.scalar_tensor_tensor(tY[:, slq], tB16b[:, slq],
                                         carry[:, 8 + c:8 + c + 1],
                                         sc[:, slq], ALU.add, ALU.mult)
                vec.tensor_tensor(tY[:, slq], tY[:, slq], tX[:, slq], ALU.add)
            vdone()
            sca.wait_ge(s_ve, cnt.val(s_ve))
            ln_stats_norm(NT, tY, tZ)
            vdone()
            sync.wait_ge(s_ve, cnt.val(s_ve))
            sync.dma_start(out=hv(x2_s, k * NT, NT),
                           in_=sv(tZ, NT, 0, DC, 0, NT)).then_inc(s_st, 16)
            cnt.inc(s_st, 16)
            tbar()

        if upto == "x2":
            _finalize(nc, sync, cnt, s_st, dbg, debug,
                      [("d_x2", x2_s, D * T), ("d_keys", keys_s, D * T)])
            return nc

        W2SL = DFC * NT
        bar()
        for k in range(KT):
            vec.wait_ge(s_st, cnt.val(s_st))
            sca.wait_ge(s_st, cnt.val(s_st))
            sync.dma_start(out=sv(tX, NT, 0, DC, 0, NT),
                           in_=hv(x2_s, k * NT, NT)).then_inc(s_ld, 16)
            cnt.inc(s_ld, 16)
            vec.wait_ge(s_ld, cnt.val(s_ld))
            vec.tensor_copy(tB16[:, 0:DC * NT], tX[:, 0:DC * NT])
            vdone()
            ten.wait_ge(s_ve, cnt.val(s_ve))
            banks4 = [psBig[:, 0:NT], psBig[:, NT:2 * NT],
                      ps2[:, 0:NT], ps3[:, 0:NT]]
            pe_q = cnt.val(s_pe)
            se_done4, ve_done4 = [], []
            for q in range(8):
                ssl = s_sl0 if q % 2 == 0 else s_sl1
                if q >= 2:
                    sync.wait_ge(s_pe, pe_q + 4 * (q - 1))
                sync.dma_start(out=arH[:, (q % 2) * DF: (q % 2) * DF + DF],
                               in_=w1b_s[q, :, :]).then_inc(ssl, 16)
                cnt.inc(ssl, 16)
                ten.wait_ge(ssl, cnt.val(ssl))
                woq = (q % 2) * DF
                for fq in range(4):
                    fc = q * 4 + fq
                    bank = banks4[fc % 4]
                    slq = slice((fc % 4) * NT, (fc % 4) * NT + NT)
                    if fc >= 4:
                        ten.wait_ge(s_se, se_done4[fc - 4])
                        ten.wait_ge(s_ve, ve_done4[fc - 4])
                    for kc in range(DC):
                        mi = ten.matmul(bank,
                                        arH[:, woq + kc * NT + fq * 128: woq + kc * NT + fq * 128 + 128],
                                        tB16[:, kc * NT: kc * NT + NT],
                                        start=(kc == 0), stop=(kc == DC - 1))
                    mi.then_inc(s_pe, 1); cnt.inc(s_pe, 1)
                    sca.wait_ge(s_pe, cnt.val(s_pe))
                    sca.activation(tY[:, slq], bank, ACTF.Sin, scale=0.1,
                                   bias=patDF[:, PB1C + fc:PB1C + fc + 1])
                    sca.activation(tZ[:, slq], bank, ACTF.Sin, scale=0.1,
                                   bias=patDF[:, PB1S + fc:PB1S + fc + 1])
                    se_done4.append(cnt.val(s_se))
                    vec.wait_ge(s_se, cnt.val(s_se))
                    vec.scalar_tensor_tensor(sc[:, slq], tY[:, slq],
                                             patDF[:, PA + fc:PA + fc + 1],
                                             bank, ALU.mult, ALU.add)
                    vec.scalar_tensor_tensor(sc[:, slq], tZ[:, slq],
                                             patDF[:, PB + fc:PB + fc + 1],
                                             sc[:, slq], ALU.mult, ALU.add)
                    vec.tensor_scalar(arK[:, fc * NT:fc * NT + NT], sc[:, slq],
                                      patA1[:, fc:fc + 1], 0.0, ALU.add, ALU.max)
                    ve_done4.append(cnt.val(s_ve))
            ten.wait_ge(s_ve, ve_done4[-1])
            se_b = cnt.val(s_se)
            pe_b = cnt.val(s_pe)
            for e in range(DC):
                ssl = s_sl0 if e % 2 == 0 else s_sl1
                if e >= 2:
                    sync.wait_ge(s_pe, pe_b + e - 1)
                else:
                    sync.wait_ge(s_pe, pe_b)
                sync.dma_start(out=arK[:, W2SL + (e % 2) * DFC * 128: W2SL + (e % 2) * DFC * 128 + DFC * 128],
                               in_=w2b_s[e, :, :]).then_inc(ssl, 16)
                cnt.inc(ssl, 16)
                ten.wait_ge(ssl, cnt.val(ssl))
                if e >= 2:
                    ten.wait_ge(s_se, se_b + e - 1)
                else:
                    ten.wait_ge(s_se, se_b)
                bank = psAB[e % 2]
                woff = W2SL + (e % 2) * DFC * 128
                for fc in range(DFC):
                    mi = ten.matmul(bank[:, 0:NT],
                                    arK[:, woff + fc * 128: woff + fc * 128 + 128],
                                    arK[:, fc * NT: fc * NT + NT],
                                    start=(fc == 0), stop=(fc == DFC - 1))
                mi.then_inc(s_pe, 1); cnt.inc(s_pe, 1)
                sca.wait_ge(s_pe, cnt.val(s_pe))
                si = sca.activation(tY[:, e * NT:(e + 1) * NT], bank[:, 0:NT],
                                    ACTF.Identity, bias=patB[:, QB2 + e:QB2 + e + 1])
            vec.wait_ge(s_se, cnt.val(s_se))
            vec.tensor_tensor(tY[:, 0:DC * NT], tY[:, 0:DC * NT], tX[:, 0:DC * NT],
                              ALU.add)
            vdone()
            ln_stats_norm(NT, tY, tZ)
            vdone()
            sync.wait_ge(s_ve, cnt.val(s_ve))
            sync.dma_start(out=hv(x3_s, k * NT, NT),
                           in_=sv(tZ, NT, 0, DC, 0, NT)).then_inc(s_st, 16)
            cnt.inc(s_st, 16)
            tbar()

        if upto == "x3":
            _finalize(nc, sync, cnt, s_st, dbg, debug,
                      [("d_x2", x2_s, D * T), ("d_x3", x3_s, D * T)])
            return nc

        for c in range(DC):
            vec.tensor_copy(scr8b[:, c:c + 1], tZ[:, c * NT + NT - 1:c * NT + NT])
        vdone()
        ten.wait_ge(s_ve, cnt.val(s_ve))
        allgather_carry(2, scr8b, 16)

        wload(arH, 0, wkeT)
        wload(arH, 8 * D, wvaT)
        wload(arH, 16 * D, wreT)
        for e in (vec, sca, ten):
            e.wait_ge(s_gd, cnt.val(s_gd))
        vec.memset(lastA[:], 0.0)
        vec.memset(maxP[:], 0.0)
        vec.memset(maxN[:], 0.0)
        vec.memset(c16[:, 1:2], 0.0)
        vec.memset(c16[:, 6:7], 0.0)
        vdone()
        bar()

        def proj_l1(wbase, ksl_tile, evac, eng=None, rbase=0):
            eng = eng or sca
            esem = eng._sem
            done = []
            for e in range(DC):
                bank = psAB[e % 2]
                if e >= 2:
                    ten.wait_ge(esem, done[e - 2])
                else:
                    ten.wait_ge(esem, cnt.val(esem))
                for kc in range(DC):
                    mi = ten.matmul(bank[:, 0:NT],
                                    arH[:, wbase + kc * D + e * 128: wbase + kc * D + e * 128 + 128],
                                    ksl_tile[:, rbase + kc * NT: rbase + kc * NT + NT],
                                    start=(kc == 0), stop=(kc == DC - 1))
                mi.then_inc(s_pe, 1); cnt.inc(s_pe, 1)
                eng.wait_ge(s_pe, cnt.val(s_pe))
                evac(e, bank)
                done.append(cnt.val(esem))

        def inv_alpha_tile(k, dst, scr_t):
            for c in range(DC):
                slq = slice(c * NT, (c + 1) * NT)
                vec.tensor_scalar(scr_t[:, slq], iot[:, 0:NT],
                                  patB[:, QNLA + c:QNLA + c + 1],
                                  binv[:, c * KT + k:c * KT + k + 1],
                                  ALU.mult, ALU.add)
            vec.tensor_scalar(scr_t[:, 0:DC * NT], scr_t[:, 0:DC * NT], EXP_CLAMP,
                              None, ALU.min)
            vdone()
            sca.wait_ge(s_ve, cnt.val(s_ve))
            si = sca.activation(dst[:, 0:DC * NT], scr_t[:, 0:DC * NT], ACTF.Exp)
            vec.wait_ge(s_se, cnt.val(s_se))

        def inv16_tile(k, dst, scr_c):
            vec.tensor_scalar(scr_c[:, 0:NT], iot[0:16, 0:NT], la16[:, 1:2],
                              b16k[:, k:k + 1], ALU.mult, ALU.add)
            vec.tensor_scalar(scr_c[:, 0:NT], scr_c[:, 0:NT], EXP_CLAMP, None, ALU.min)
            vdone()
            sca.wait_ge(s_ve, cnt.val(s_ve))
            si = sca.activation(dst[:, 0:NT], scr_c[:, 0:NT], ACTF.Exp)
            vec.wait_ge(s_se, cnt.val(s_se))

        for k in range(KT):
            vec.wait_ge(s_st, cnt.val(s_st))
            sca.wait_ge(s_st, cnt.val(s_st))
            sync.dma_start(out=sv(tX, NT, 0, DC, 0, NT),
                           in_=hv(x3_s, k * NT, NT)).then_inc(s_ld, 16)
            cnt.inc(s_ld, 16)
            if k == 0:
                sync.dma_start(out=sv(tY, NT, 0, DC, 1, NT - 1),
                               in_=hv(x3_s, 0, NT - 1)).then_inc(s_ld, 16)
                cnt.inc(s_ld, 16)
                vec.wait_ge(s_ld, cnt.val(s_ld))
                for c in range(DC):
                    vec.tensor_copy(tY[:, c * NT:c * NT + 1],
                                    carry[:, 16 + c:16 + c + 1])
            else:
                sync.dma_start(out=sv(tY, NT, 0, DC, 0, NT),
                               in_=hv(x3_s, k * NT - 1, NT)).then_inc(s_ld, 16)
                cnt.inc(s_ld, 16)
                vec.wait_ge(s_ld, cnt.val(s_ld))
            vec.tensor_tensor(tX[:, 0:DC * NT], tX[:, 0:DC * NT], tY[:, 0:DC * NT],
                              ALU.subtract)
            vdone()

            for c in range(DC):
                slq = slice(c * NT, (c + 1) * NT)
                vec.scalar_tensor_tensor(arK[:, c * NT:c * NT + NT], tX[:, slq],
                                         pat[:, PMK + c:PMK + c + 1],
                                         tY[:, slq], ALU.mult, ALU.add)
                vec.scalar_tensor_tensor(arK[:, DF + c * NT:DF + c * NT + NT],
                                         tX[:, slq],
                                         pat[:, PMV + c:PMV + c + 1],
                                         tY[:, slq], ALU.mult, ALU.add)
                vec.scalar_tensor_tensor(arK[:, 2 * DF + c * NT:2 * DF + c * NT + NT],
                                         tX[:, slq],
                                         pat[:, PMR + c:PMR + c + 1],
                                         tY[:, slq], ALU.mult, ALU.add)
            vdone()
            ten.wait_ge(s_ve, cnt.val(s_ve))

            def evac_k(e, bank):
                si = sca.activation(sc[:, e * NT:(e + 1) * NT], bank[:, 0:NT], ACTF.Exp)
            proj_l1(0, arK, evac_k, rbase=0)

            def evac_v(e, bank):
                si = sca.activation(tZ[:, e * NT:(e + 1) * NT], bank[:, 0:NT],
                                    ACTF.Identity)
            proj_l1(8 * D, arK, evac_v, rbase=DF)

            def evac_r(e, bank):
                si = sca.activation(tB16b[:, e * NT:(e + 1) * NT], bank[:, 0:NT],
                                    ACTF.Sigmoid)
            proj_l1(16 * D, arK, evac_r, rbase=2 * DF)

            ten.wait_ge(s_se, cnt.val(s_se))
            for c in range(DC):
                mi = ten.matmul(ps2[0:16, 0:NT], selc[:, c * 16:(c + 1) * 16],
                                sc[:, c * NT:(c + 1) * NT],
                                start=(c == 0), stop=(c == DC - 1))
            mi.then_inc(s_pe, 1); cnt.inc(s_pe, 1)
            vec.wait_ge(s_pe, cnt.val(s_pe))
            vec.tensor_copy(cse[:, 0:NT], ps2[0:16, 0:NT])
            vdone()
            sync.wait_ge(s_ve, cnt.val(s_ve))
            sync.dma_start(out=se_s[:, k * NT:(k + 1) * NT], in_=cse[:, 0:NT]).then_inc(s_st, 16)
            cnt.inc(s_st, 16)
            sync.wait_ge(s_se, cnt.val(s_se))
            sync.dma_start(out=hv(r_s, k * NT, NT),
                           in_=sv(tB16b, NT, 0, DC, 0, NT)).then_inc(s_st, 16)
            cnt.inc(s_st, 16)

            vec.wait_ge(s_se, cnt.val(s_se))
            vec.tensor_tensor(tZ[:, 0:DC * NT], sc[:, 0:DC * NT], tZ[:, 0:DC * NT],
                              ALU.mult)
            vdone()
            sync.wait_ge(s_ve, cnt.val(s_ve))
            sync.dma_start(out=hv(ekv_s, k * NT, NT),
                           in_=sv(tZ, NT, 0, DC, 0, NT)).then_inc(s_st, 16)
            cnt.inc(s_st, 16)

            inv_alpha_tile(k, tX, tY)
            vec.scalar_tensor_tensor(tX[:, 0:DC * NT], tZ[:, 0:DC * NT], C1,
                                     tX[:, 0:DC * NT], ALU.mult, ALU.mult)
            vec.tensor_scalar(bmt[:, 0:NT], iot[:, 0:NT], -1.0,
                              bigk[:, k:k + 1], ALU.mult, ALU.add)
            vec.wait_ge(s_pe, cnt.val(s_pe))
            u1 = psBig[:, 0:NT]
            for c in range(DC):
                slq = slice(c * NT, (c + 1) * NT)
                vec.tensor_tensor_scan(tY[:, slq], tX[:, slq], zerosb[:, 0:NT],
                                       lastA[:, c:c + 1], ALU.add, ALU.add)
                vec.tensor_copy(lastA[:, c:c + 1], tY[:, c * NT + NT - 1:c * NT + NT])
                vec.tensor_tensor(u1, tX[:, slq], tY[:, slq], ALU.max)
                vec.scalar_tensor_tensor(u1, u1, THF, bmt[:, 0:NT],
                                         ALU.is_ge, ALU.mult)
                vec.tensor_reduce(scr8[:, c:c + 1], u1, mybir.AxisListType.X, ALU.max)
                vec.tensor_tensor(u1, tX[:, slq], tY[:, slq], ALU.min)
                vec.scalar_tensor_tensor(u1, u1, -THF, bmt[:, 0:NT],
                                         ALU.is_le, ALU.mult)
                vec.tensor_reduce(scr8b[:, c:c + 1], u1, mybir.AxisListType.X, ALU.max)
            vec.tensor_tensor(maxP[:], maxP[:], scr8[:], ALU.max)
            vec.tensor_tensor(maxN[:], maxN[:], scr8b[:], ALU.max)
            vdone()
            inv16_tile(k, ct2, ct1)
            vec.scalar_tensor_tensor(ccb[:, 0:NT], cse[:, 0:NT], C1,
                                     ct2[:, 0:NT], ALU.mult, ALU.mult)
            vec.tensor_tensor_scan(ct1[:, 0:NT], ccb[:, 0:NT], zerosb[0:16, 0:NT],
                                   c16[:, 1:2], ALU.add, ALU.add)
            vec.tensor_copy(c16[:, 1:2], ct1[:, NT - 1:NT])
            vec.tensor_tensor(ct2[:, 0:NT], ccb[:, 0:NT], ct1[:, 0:NT], ALU.max)
            vec.scalar_tensor_tensor(ct2[:, 0:NT], ct2[:, 0:NT], THF,
                                     bmt[0:16, 0:NT], ALU.is_ge, ALU.mult)
            vec.tensor_reduce(c16[:, 4:5], ct2[:, 0:NT], mybir.AxisListType.X, ALU.max)
            vec.tensor_tensor(c16[:, 6:7], c16[:, 6:7], c16[:, 4:5], ALU.max)
            vdone()
            tbar()

        ten.wait_ge(s_ve, cnt.val(s_ve))
        for (src_cols, off) in ((lastA[:], 0), (maxP[:], 1024), (maxN[:], 2048)):
            ten.transpose(ps2[0:8, 0:128], src_cols, ident[:]).then_inc(s_pe, 1)
            cnt.inc(s_pe, 1)
            sca.wait_ge(s_pe, cnt.val(s_pe))
            sca.wait_ge(s_la, cnt.val(s_la))
            sca.activation(hrow[:], ps2[0:8, 0:128], ACTF.Copy)
            sync.wait_ge(s_se, cnt.val(s_se))
            sync.dma_start(out=ag_in[3][:, off:off + 1024].rearrange("o (r c) -> r (o c)", r=8),
                           in_=hrow[:]).then_inc(s_la, 16)
            cnt.inc(s_la, 16)
            ten.wait_ge(s_se, cnt.val(s_se))
        sync.dma_start(out=ag_in[3][:, 3072:3088].rearrange("o (w u) -> (o w) u", u=1),
                       in_=c16[:, 1:2]).then_inc(s_la, 16)
        cnt.inc(s_la, 16)
        sync.dma_start(out=ag_in[3][:, 3088:3104].rearrange("o (w u) -> (o w) u", u=1),
                       in_=c16[:, 6:7]).then_inc(s_la, 16)
        cnt.inc(s_la, 16)
        gp.wait_ge(s_la, cnt.val(s_la))
        gp.collective_compute("AllGather", ALU.bypass,
                              replica_groups=[list(range(8))],
                              ins=[ag_in[3].opt()],
                              outs=[ag_out[3].opt()]).then_inc(s_ag, 1)
        cnt.inc(s_ag, 1)
        sync.wait_ge(s_ag, cnt.val(s_ag))
        gselv = sc[0:8, 0:AG_W]
        sync.dma_start(out=gselv, in_=ag_out[3]).then_inc(s_la, 16)
        cnt.inc(s_la, 16)
        ten.wait_ge(s_la, cnt.val(s_la))
        for (off, ccol0) in ((0, 24), (1024, 32), (2048, 40)):
            ten.matmul(ps2[0:1, 0:NT], onehot8[:], gselv[:, off:off + NT],
                       start=True, stop=True)
            mi = ten.matmul(ps2[32:33, 0:NT], onehot8[:], gselv[:, off + NT:off + 2 * NT],
                            start=True, stop=True)
            mi.then_inc(s_pe, 1); cnt.inc(s_pe, 1)
            sca.wait_ge(s_pe, cnt.val(s_pe))
            sca.activation(svA[0:1, 0:NT], ps2[0:1, 0:NT], ACTF.Copy)
            si = sca.activation(svA[32:33, 0:NT], ps2[32:33, 0:NT], ACTF.Copy)
            ten.wait_ge(s_se, cnt.val(s_se))
            for c in range(DC):
                rsrc = (svA[0:1, c * 128:(c + 1) * 128] if c < 4 else
                        svA[32:33, (c - 4) * 128:(c - 3) * 128])
                idk = ones2d[0:1, 0:1] if c < 4 else ones2d[32:33, 0:1]
                ten.transpose(ps3[:, 0:1], rsrc, idk).then_inc(s_pe, 1)
                cnt.inc(s_pe, 1)
                vec.wait_ge(s_pe, cnt.val(s_pe))
                vec.tensor_copy(carry[:, ccol0 + c:ccol0 + c + 1], ps3[:, 0:1])
                vdone()
                ten.wait_ge(s_ve, cnt.val(s_ve))
        ten.matmul(ps2[0:1, 0:16], onehot8[:], gselv[:, 3072:3088], start=True, stop=True)
        mi = ten.matmul(ps2[32:33, 0:16], onehot8[:], gselv[:, 3088:3104],
                        start=True, stop=True)
        mi.then_inc(s_pe, 1); cnt.inc(s_pe, 1)
        sca.wait_ge(s_pe, cnt.val(s_pe))
        sca.activation(svA[0:1, 0:16], ps2[0:1, 0:16], ACTF.Copy)
        si = sca.activation(svA[32:33, 0:16], ps2[32:33, 0:16], ACTF.Copy)
        ten.wait_ge(s_se, cnt.val(s_se))
        ten.transpose(ps3[0:16, 0:1], svA[0:1, 0:16], ones2d[0:1, 0:1]).then_inc(s_pe, 1)
        cnt.inc(s_pe, 1)
        vec.wait_ge(s_pe, cnt.val(s_pe))
        vec.tensor_copy(c16[:, 0:1], ps3[0:16, 0:1])
        vdone()
        ten.wait_ge(s_ve, cnt.val(s_ve))
        ten.transpose(ps3[0:16, 0:1], svA[32:33, 0:16], ones2d[32:33, 0:1]).then_inc(s_pe, 1)
        cnt.inc(s_pe, 1)
        vec.wait_ge(s_pe, cnt.val(s_pe))
        vec.tensor_copy(c16[:, 2:3], ps3[0:16, 0:1])
        vdone()
        bar()

        if upto == "front":
            _finalize(nc, sync, cnt, s_st, dbg, debug,
                      [("d_x2", x2_s, 0), ("d_x3", x3_s, 0), ("d_ekv", ekv_s, 0)])
            return nc

        wload(arH, 0, woT)
        ten.wait_ge(s_gd, cnt.val(s_gd))
        vec.tensor_copy(lastP[:], carry[:, 24:32])
        vec.tensor_copy(chainP[:], carry[:, 32:40])
        vec.tensor_copy(chainN[:], carry[:, 40:48])
        vec.tensor_copy(c16[:, 3:4], c16[:, 0:1])
        vec.tensor_copy(c16[:, 4:5], c16[:, 2:3])
        vec.memset(c16[:, 7:8], -LN2_64)
        vdone()
        bar()
        for k in range(KT):
            vec.wait_ge(s_st, cnt.val(s_st))
            sca.wait_ge(s_st, cnt.val(s_st))
            sync.dma_start(out=sv(tZ, NT, 0, DC, 0, NT),
                           in_=hv(ekv_s, k * NT, NT)).then_inc(s_ld, 16)
            cnt.inc(s_ld, 16)
            sync.dma_start(out=sv(tX, NT, 0, DC, 0, NT),
                           in_=hv(x3_s, k * NT, NT)).then_inc(s_ld, 16)
            cnt.inc(s_ld, 16)
            sync.dma_start(out=sv(tB16b, NT, 0, DC, 0, NT),
                           in_=hv(r_s, k * NT, NT)).then_inc(s_ld, 16)
            cnt.inc(s_ld, 16)
            sync.dma_start(out=cse[:, 0:NT], in_=se_s[:, k * NT:(k + 1) * NT]).then_inc(s_ld, 16)
            cnt.inc(s_ld, 16)
            vec.wait_ge(s_ld, cnt.val(s_ld))
            sca.wait_ge(s_ld, cnt.val(s_ld))
            if k == 0:
                inv_alpha_tile(k, sc, tY)
            vec.scalar_tensor_tensor(sc[:, 0:DC * NT], tZ[:, 0:DC * NT], C1,
                                     sc[:, 0:DC * NT], ALU.mult, ALU.mult)
            vec.tensor_scalar(bmt[:, 0:NT], iot[:, 0:NT], -1.0,
                              bigk[:, k:k + 1], ALU.mult, ALU.add)
            inv16_tile(k, ct2, ct1)
            vec.scalar_tensor_tensor(ccb[:, 0:NT], cse[:, 0:NT], C1,
                                     ct2[:, 0:NT], ALU.mult, ALU.mult)
            vec.tensor_tensor_scan(ct2[:, 0:NT], ccb[:, 0:NT], zerosb[0:16, 0:NT],
                                   c16[:, 3:4], ALU.add, ALU.add)
            vec.tensor_tensor(ct1[:, 0:NT], ccb[:, 0:NT], ct2[:, 0:NT], ALU.max)
            vec.scalar_tensor_tensor(ct1[:, 0:NT], ct1[:, 0:NT], THF,
                                     bmt[0:16, 0:NT], ALU.is_ge, ALU.mult)
            vec.tensor_tensor_scan(ccb[:, 0:NT], ct1[:, 0:NT], zerosb[0:16, 0:NT],
                                   c16[:, 4:5], ALU.max, ALU.max)
            vec.tensor_copy(crd[:, 1:NT], ct2[:, 0:NT - 1])
            vec.tensor_copy(crd[:, 0:1], c16[:, 3:4])
            vec.scalar_tensor_tensor(crd[:, 1:NT], ccb[:, 0:NT - 1], 0.0,
                                     crd[:, 1:NT], ALU.is_le, ALU.mult)
            vec.scalar_tensor_tensor(crd[:, 0:1], c16[:, 4:5], 0.0,
                                     crd[:, 0:1], ALU.is_le, ALU.mult)
            vec.tensor_copy(c16[:, 3:4], ct2[:, NT - 1:NT])
            vec.tensor_copy(c16[:, 4:5], ccb[:, NT - 1:NT])
            vec.tensor_scalar(ct1[:, 0:NT], iot[0:16, 0:NT], la16[:, 0:1],
                              b16k[:, KT + k:KT + k + 1], ALU.mult, ALU.add)
            vdone()
            sca.wait_ge(s_ve, cnt.val(s_ve))
            si = sca.activation(ccb[:, 0:NT], ct1[:, 0:NT], ACTF.Exp)
            vec.wait_ge(s_se, cnt.val(s_se))
            vec.scalar_tensor_tensor(crd[:, 0:NT], ccb[:, 0:NT], C_RESC,
                                     crd[:, 0:NT], ALU.mult, ALU.mult)
            vec.scalar_tensor_tensor(crd[:, 0:NT], cse[:, 0:NT], la16[:, 2:3],
                                     crd[:, 0:NT], ALU.mult, ALU.add)
            vec.tensor_scalar(crd[:, 0:NT], crd[:, 0:NT], 1e-6, None, ALU.max)
            vec.tensor_scalar(crd[:, 0:NT], crd[:, 0:NT], 3.3e38, None, ALU.min)
            vdone()
            sca.wait_ge(s_ve, cnt.val(s_ve))
            sca.activation(ct1[:, 0:NT], crd[:, 0:NT], ACTF.Ln, scale=WKV_SC * WKV_SC)
            si = sca.activation(crd[:, 0:NT], ct1[:, 0:NT], ACTF.Exp, scale=-1.0,
                                bias=c16[:, 7:8])
            vec.wait_ge(s_pe, cnt.val(s_pe))

            def wkv_post(c):
                slq = slice(c * NT, (c + 1) * NT)
                vec.wait_ge(s_se, se_mark[c])
                vec.scalar_tensor_tensor(sc[:, slq], tY[:, slq], C_RESC,
                                         sc[:, slq], ALU.mult, ALU.mult)
                vec.scalar_tensor_tensor(ps2[:, 0:NT], tY[:, slq], 0.0,
                                         ps2[:, 0:NT], ALU.is_gt, ALU.mult)
                vec.scalar_tensor_tensor(sc[:, slq], ps2[:, 0:NT], -FMAX,
                                         sc[:, slq], ALU.mult, ALU.add)
                vec.scalar_tensor_tensor(sc[:, slq], tZ[:, slq],
                                         patB[:, QEU + c:QEU + c + 1],
                                         sc[:, slq], ALU.mult, ALU.add)

            se_mark = {}
            for c in range(DC):
                slq = slice(c * NT, (c + 1) * NT)
                u1 = tY[:, ((c + 1) % DC) * NT:((c + 1) % DC) * NT + NT]
                u2 = tY[:, ((c + 2) % DC) * NT:((c + 2) % DC) * NT + NT]
                u4 = tY[:, ((c + 3) % DC) * NT:((c + 3) % DC) * NT + NT]
                u3 = ps2[:, 0:NT]
                vec.tensor_tensor_scan(tY[:, slq], sc[:, slq], zerosb[:, 0:NT],
                                       lastP[:, c:c + 1], ALU.add, ALU.add)
                vec.tensor_tensor(u1, sc[:, slq], tY[:, slq], ALU.max)
                vec.scalar_tensor_tensor(u1, u1, THF, bmt[:, 0:NT],
                                         ALU.is_ge, ALU.mult)
                vec.tensor_tensor_scan(u2, u1, zerosb[:, 0:NT],
                                       chainP[:, c:c + 1], ALU.max, ALU.max)
                if c >= 1:
                    wkv_post(c - 1)
                vec.tensor_tensor(u3, sc[:, slq], tY[:, slq], ALU.min)
                vec.scalar_tensor_tensor(u3, u3, -THF, bmt[:, 0:NT],
                                         ALU.is_le, ALU.mult)
                vec.tensor_tensor_scan(u4, u3, zerosb[:, 0:NT],
                                       chainN[:, c:c + 1], ALU.max, ALU.max)
                vec.tensor_tensor(u3[:, 1:NT], u4[:, 0:NT - 1],
                                  u2[:, 0:NT - 1], ALU.is_gt)
                vec.tensor_tensor(u3[:, 0:1], chainN[:, c:c + 1],
                                  chainP[:, c:c + 1], ALU.is_gt)
                vec.tensor_tensor(u1[:, 1:NT], u2[:, 0:NT - 1],
                                  u4[:, 0:NT - 1], ALU.max)
                vec.tensor_tensor(u1[:, 0:1], chainP[:, c:c + 1],
                                  chainN[:, c:c + 1], ALU.max)
                vec.tensor_copy(chainP[:, c:c + 1], u2[:, NT - 1:NT])
                vec.tensor_copy(chainN[:, c:c + 1], u4[:, NT - 1:NT])
                vec.tensor_copy(sc[:, c * NT + 1:(c + 1) * NT],
                                tY[:, c * NT:c * NT + NT - 1])
                vec.tensor_copy(sc[:, c * NT:c * NT + 1], lastP[:, c:c + 1])
                vec.tensor_copy(lastP[:, c:c + 1], tY[:, c * NT + NT - 1:c * NT + NT])
                vec.scalar_tensor_tensor(sc[:, slq], u1, 0.0, sc[:, slq],
                                         ALU.is_le, ALU.mult)
                sca.wait_ge(s_ve, cnt.val(s_ve))
                si = sca.activation(tY[:, slq], iot[:, 0:NT],
                                    ACTF.Exp, scale=patB[:, QLA + c:QLA + c + 1],
                                    bias=bpow[:, c * KT + k:c * KT + k + 1])
                se_mark[c] = cnt.val(s_se)
            wkv_post(DC - 1)
            vdone()
            if debug:
                sync.wait_ge(s_ve, cnt.val(s_ve))
                sync.dma_start(out=hv(dbg["d_num"], k * NT, NT),
                               in_=sv(sc, NT, 0, DC, 0, NT)).then_inc(s_st, 16)
                cnt.inc(s_st, 16)
                vec.wait_ge(s_st, cnt.val(s_st))
            ten.wait_ge(s_se, cnt.val(s_se))
            ten.wait_ge(s_ve, cnt.val(s_ve))
            banks4b = [ps2[:, 0:NT], ps3[:, 0:NT],
                       psBig[:, 0:NT], psBig[:, NT:2 * NT]]
            bdone = []
            for c in range(DC):
                bank = banks4b[c % 4]
                if c >= 4:
                    ten.wait_ge(s_ve, bdone[c - 4])
                mi = ten.matmul(bank, selT[:, c * 128:(c + 1) * 128],
                                crd[:, 0:NT], start=True, stop=True)
                mi.then_inc(s_pe, 1); cnt.inc(s_pe, 1)
                vec.wait_ge(s_pe, cnt.val(s_pe))
                slq = slice(c * NT, (c + 1) * NT)
                vec.scalar_tensor_tensor(sc[:, slq], sc[:, slq], WKV_SC,
                                         bank, ALU.mult, ALU.mult)
                vec.tensor_scalar(sc[:, slq], sc[:, slq], WKV_CL, -WKV_CL,
                                  ALU.min, ALU.max)
                vec.tensor_tensor(tY[:, slq], sc[:, slq], tB16b[:, slq], ALU.mult)
                vec.tensor_scalar(tY[:, slq], tY[:, slq], RW_SC, 2.656e35,
                                  ALU.mult, ALU.min)
                vec.tensor_scalar(tB16[:, slq], tY[:, slq], -2.656e35, None, ALU.max)
                bdone.append(cnt.val(s_ve))
            if debug:
                sync.wait_ge(s_ve, cnt.val(s_ve))
                sync.dma_start(out=hv(dbg["d_wkv"], k * NT, NT),
                               in_=sv(sc, NT, 0, DC, 0, NT)).then_inc(s_st, 16)
                cnt.inc(s_st, 16)
                sync.dma_start(out=dbg["d_rdn"][:, k * NT:(k + 1) * NT],
                               in_=crd[:, 0:NT]).then_inc(s_st, 16)
                cnt.inc(s_st, 16)
                vec.wait_ge(s_st, cnt.val(s_st))
                sca.wait_ge(s_st, cnt.val(s_st))
            if k + 1 < KT:
                inv_alpha_tile(k + 1, sc, tZ)
            def evac_att(e, bank):
                vi = vec.tensor_scalar(tZ[:, e * NT:(e + 1) * NT], bank[:, 0:NT],
                                       2.656e36, -2.656e36, ALU.min, ALU.max)
            proj_l1(0, tB16, evac_att, eng=vec)
            if debug:
                vdone()
                sync.wait_ge(s_ve, cnt.val(s_ve))
                sync.dma_start(out=hv(dbg["d_att"], k * NT, NT),
                               in_=sv(tZ, NT, 0, DC, 0, NT)).then_inc(s_st, 16)
                cnt.inc(s_st, 16)
                vec.wait_ge(s_st, cnt.val(s_st))
            vec.scalar_tensor_tensor(tY[:, 0:DC * NT], tZ[:, 0:DC * NT], 128.0,
                                     tX[:, 0:DC * NT], ALU.mult, ALU.add)
            vdone()
            ln_stats_norm(NT, tY, tZ, armored=True)
            vdone()
            sync.wait_ge(s_ve, cnt.val(s_ve))
            sync.dma_start(out=hv(x4_s, k * NT, NT),
                           in_=sv(tZ, NT, 0, DC, 0, NT)).then_inc(s_st, 16)
            cnt.inc(s_st, 16)
            tbar()

        if upto == "x4":
            _finalize(nc, sync, cnt, s_st, dbg, debug,
                      [("d_x2", x2_s, 0), ("d_x3", x3_s, 0), ("d_x4", x4_s, 0)])
            return nc

        for oc in range(DC):
            vec.wait_ge(s_st, cnt.val(s_st))
            sca.wait_ge(s_st, cnt.val(s_st))
            sync.wait_ge(s_ld, cnt.val(s_ld))
            sync.dma_start(out=tX[:, 0:D], in_=l1fw[oc * 128:(oc + 1) * 128, :]).then_inc(s_ld, 16)
            cnt.inc(s_ld, 16)
            vec.wait_ge(s_ld, cnt.val(s_ld))
            vec.tensor_reduce(scr8[:, 0:1], tX[:, 0:D], mybir.AxisListType.X,
                              ALU.add, apply_absolute_value=True)
            vec.tensor_scalar(scr8[:, 0:1], scr8[:, 0:1], 1.0 / D, None, ALU.mult)
            vec.tensor_scalar(scr8[:, 0:1], scr8[:, 0:1], 1e-5, None, ALU.max)
            vdone()
            sca.wait_ge(s_ve, cnt.val(s_ve))
            sca.activation(scr8[:, 1:2], scr8[:, 0:1], ACTF.Ln)
            si = sca.activation(scr8[:, 1:2], scr8[:, 1:2], ACTF.Exp, scale=-1.0)
            vec.wait_ge(s_se, cnt.val(s_se))
            vec.tensor_scalar(tY[:, 0:D], tX[:, 0:D], scr8[:, 1:2], None, ALU.mult)
            vec.tensor_scalar(tZ[:, 0:D], tY[:, 0:D], 0.5, 1.0, ALU.is_gt, ALU.mult)
            vec.tensor_scalar(tX[:, 0:D], tY[:, 0:D], -0.5, 1.0, ALU.is_lt, ALU.mult)
            vec.tensor_tensor(tZ[:, 0:D], tZ[:, 0:D], tX[:, 0:D], ALU.subtract)
            vec.tensor_scalar(tB16[:, 0:D], tZ[:, 0:D], scr8[:, 0:1], None, ALU.mult)
            vdone()
            ten.wait_ge(s_ve, cnt.val(s_ve))
            sca.wait_ge(s_h, cnt.val(s_h))
            for ic in range(DC):
                ten.transpose(psT[:, 0:128],
                              tB16[:, ic * 128:(ic + 1) * 128], identb[:]).then_inc(s_pe, 1)
                cnt.inc(s_pe, 1)
                sca.wait_ge(s_pe, cnt.val(s_pe))
                si = sca.activation(tB16b[:, ic * 128:(ic + 1) * 128], psT[:, 0:128],
                                    ACTF.Copy)
                ten.wait_ge(s_se, cnt.val(s_se))
            sync.wait_ge(s_se, cnt.val(s_se))
            sync.dma_start(out=wqT_s.rearrange("(c p) e -> p c e", p=128)[:, :, oc * 128:(oc + 1) * 128],
                           in_=tB16b[:, 0:D].rearrange("p (c e) -> p c e", e=128)).then_inc(s_h, 16)
            cnt.inc(s_h, 16)
            tbar()
        sync.dma_start(out=arH[:, 0:8 * D].rearrange("p (c e) -> p c e", e=D),
                       in_=wqT_s.rearrange("(c p) e -> p c e", p=128)).then_inc(s_ld, 16)
        cnt.inc(s_ld, 16)
        wload(arH, 8 * D, hadc)
        bar()

        for e in (vec, sca, ten):
            e.wait_ge(s_ld, cnt.val(s_ld))
        for k in range(KT):
            vec.wait_ge(s_st, cnt.val(s_st))
            sca.wait_ge(s_st, cnt.val(s_st))
            sync.dma_start(out=sv(tX, NT, 0, DC, 0, NT),
                           in_=hv(x4_s, k * NT, NT)).then_inc(s_ld, 16)
            cnt.inc(s_ld, 16)
            vec.wait_ge(s_ld, cnt.val(s_ld))
            vec.tensor_copy(tB16[:, 0:DC * NT], tX[:, 0:DC * NT])
            vdone()
            ten.wait_ge(s_ve, cnt.val(s_ve))

            def p12_quant(tg):
                slq = slice(tg * 1024, tg * 1024 + 1024)
                vec.wait_ge(s_se, se_done[tg])
                vec.tensor_reduce(scr8[:, 0:1], tY[:, slq], mybir.AxisListType.X,
                                  ALU.max, apply_absolute_value=True)
                vec.tensor_scalar(scr8[:, 0:1], scr8[:, 0:1], 1e-5, None, ALU.max)
                vec.tensor_scalar(scr8[:, 1:2], scr8[:, 0:1], 1.0 / 7.0, None, ALU.mult)
                vec.reciprocal(scr8[:, 2:3], scr8[:, 1:2])
                vec.tensor_scalar(tZ[:, slq], tY[:, slq], scr8[:, 2:3], MAGIC,
                                  ALU.mult, ALU.add)
                vec.tensor_scalar(tZ[:, slq], tZ[:, slq], MAGIC, None, ALU.subtract)
                vec.tensor_scalar(tZ[:, slq], tZ[:, slq], -8.0, 7.0, ALU.max, ALU.min)
                vec.tensor_scalar(tB16b[:, slq], tZ[:, slq], scr8[:, 1:2], None,
                                  ALU.mult)
                return cnt.val(s_ve)

            def p12_transpose(tg):
                ten.wait_ge(s_ve, q_mark[tg])
                for oc in range(DC):
                    mi = ten.transpose(psT[:, (oc % 4) * 128:(oc % 4) * 128 + 128],
                                       tB16b[:, tg * 1024 + oc * 128: tg * 1024 + oc * 128 + 128],
                                       identb[:])
                    if oc % 4 == 3:
                        mi.then_inc(s_pe, 1); cnt.inc(s_pe, 1)
                        sca.wait_ge(s_pe, cnt.val(s_pe))
                        for j in range(4):
                            ocx = oc - 3 + j
                            si = sca.activation(
                                tB16[:, ocx * NT + tg * 128: ocx * NT + tg * 128 + 128],
                                psT[:, j * 128:j * 128 + 128], ACTF.Copy)
                        ten.wait_ge(s_se, cnt.val(s_se))

            se_done = []; q_mark = {}
            for tg in range(NT // 128):
                if tg >= 1:
                    p12_transpose(tg - 1)
                    ten.wait_ge(s_se, se_done[tg - 1])
                else:
                    ten.wait_ge(s_se, cnt.val(s_se))
                for nh in range(2):
                    bank = psAB[nh]
                    for kc in range(DC):
                        mi = ten.matmul(bank[:, 0:NT],
                                        tB16[:, kc * NT + tg * 128: kc * NT + tg * 128 + 128],
                                        arH[:, (8 + kc) * D + nh * NT: (8 + kc) * D + nh * NT + NT],
                                        start=(kc == 0), stop=(kc == DC - 1))
                mi.then_inc(s_pe, 1); cnt.inc(s_pe, 1)
                sca.wait_ge(s_pe, cnt.val(s_pe))
                sca.activation(tY[:, tg * 1024:tg * 1024 + NT], ps0[:, 0:NT], ACTF.Copy)
                si = sca.activation(tY[:, tg * 1024 + NT:tg * 1024 + 2 * NT],
                                    ps1[:, 0:NT], ACTF.Copy)
                se_done.append(cnt.val(s_se))
                q_mark[tg] = p12_quant(tg)
            p12_transpose(NT // 128 - 1)
            def evac_f(e, bank):
                si = sca.activation(tZ[:, e * NT:(e + 1) * NT], bank[:, 0:NT],
                                    ACTF.Identity)
            se_b = cnt.val(s_se)
            for e in range(DC):
                bank = psAB[e % 2]
                if e >= 2:
                    ten.wait_ge(s_se, se_b + e - 1)
                else:
                    ten.wait_ge(s_se, se_b)
                for kc in range(DC):
                    mi = ten.matmul(bank[:, 0:NT],
                                    arH[:, kc * D + e * 128: kc * D + e * 128 + 128],
                                    tB16[:, kc * NT: kc * NT + NT],
                                    start=(kc == 0), stop=(kc == DC - 1))
                mi.then_inc(s_pe, 1); cnt.inc(s_pe, 1)
                sca.wait_ge(s_pe, cnt.val(s_pe))
                evac_f(e, bank)
            vec.wait_ge(s_se, cnt.val(s_se))
            vec.tensor_tensor(tY[:, 0:DC * NT], tZ[:, 0:DC * NT], tX[:, 0:DC * NT],
                              ALU.add)
            vdone()
            ln_stats_norm(NT, tY, tZ)
            vdone()
            sync.wait_ge(s_ve, cnt.val(s_ve))
            sync.dma_start(out=hv(out_t, k * NT, NT),
                           in_=sv(tZ, NT, 0, DC, 0, NT)).then_inc(s_st, 16)
            cnt.inc(s_st, 16)
            tbar()

        _finalize(nc, sync, cnt, s_st, dbg, debug,
                  [("d_x2", x2_s, 0), ("d_x3", x3_s, 0), ("d_x4", x4_s, 0)])
    return nc


def _finalize(nc, sync, cnt, s_st, dbg, debug, dumps):
    if debug:
        for nm, src, _ in dumps:
            if nm in dbg:
                sync.dma_start(out=dbg[nm], in_=src).then_inc(s_st, 16)
                cnt.inc(s_st, 16)
    sync.wait_ge(s_st, cnt.val(s_st))
    nc.all_engine_barrier()


def _hadamard_const():
    h = np.array([[1.0]], dtype=np.float64)
    while h.shape[0] < D:
        h = np.block([[h, h], [h, -h]])
    return np.ascontiguousarray((h / np.sqrt(D)).astype(np.float32))


def _shard_inputs(inputs):
    f = np.float32
    xs = np.asarray(inputs["L0_attn_hidden_states"], f)
    tr = lambda a: np.ascontiguousarray(np.asarray(a, f).T)
    cc = lambda a: np.ascontiguousarray(np.asarray(a, f))
    common = dict(
        wkT=tr(inputs["L0_attn_Wk"]), wvT=tr(inputs["L0_attn_Wv"]),
        wkaT=tr(inputs["L0_attn_Wka"]),
        w1=cc(inputs["L0_ffn_W1"]), w2=cc(inputs["L0_ffn_W2"]),
        wkeT=tr(inputs["L1_attn_W_key"]), wvaT=tr(inputs["L1_attn_W_value"]),
        wreT=tr(inputs["L1_attn_W_receptance"]), woT=tr(inputs["L1_attn_W_output"]),
        l1fw=cc(inputs["L1_ffn_weight"]),
        bk=cc(inputs["L0_attn_bk"]), bv=cc(inputs["L0_attn_bv"]),
        bka=cc(inputs["L0_attn_bka"]),
        gamma=cc(inputs["L0_attn_gamma"]), beta=cc(inputs["L0_attn_beta"]),
        b1=cc(inputs["L0_ffn_b1"]), b2=cc(inputs["L0_ffn_b2"]),
        theta=cc(inputs["L0_ffn_quantum_phases"]),
        qwr=cc(inputs["L0_ffn_quantum_weights_real"]),
        qwi=cc(inputs["L0_ffn_quantum_weights_imag"]),
        mixk=cc(np.asarray(inputs["L1_attn_mix_k"], f).ravel()),
        mixv=cc(np.asarray(inputs["L1_attn_mix_v"], f).ravel()),
        mixr=cc(np.asarray(inputs["L1_attn_mix_r"], f).ravel()),
        td=cc(inputs["L1_attn_time_decay"]), tf=cc(inputs["L1_attn_time_first"]),
        hadc=_hadamard_const(),
    )
    in_maps = []
    for c in range(8):
        b, half = c // 2, c % 2
        sel = np.zeros((1, 8), f)
        if half == 1:
            sel[0, c - 1] = 1.0
        m = dict(common)
        m["x_in"] = np.ascontiguousarray(xs[b, half * T:(half + 1) * T, :].T)
        m["selrow"] = sel
        m["t0c"] = np.array([[float(half * T)]], f)
        in_maps.append(m)
    return in_maps


def kernel(**inputs):
    in_maps = _shard_inputs(inputs)
    nc = build_program(debug=False, upto="out")
    res = run_bass_kernel_spmd(nc, in_maps, core_ids=list(range(8)))
    out = np.empty((B, S, D), np.float32)
    for c in range(8):
        b, half = c // 2, c % 2
        out[b, half * T:(half + 1) * T, :] = res.results[c]["out"].T
    return out


# version 15
# speedup vs baseline: 1.0245x; 1.0116x over previous
import numpy as np
from contextlib import ExitStack

import concourse.bass as bass
import concourse.mybir as mybir
from concourse.bass_utils import run_bass_kernel_spmd

F32 = mybir.dt.float32
BF16 = mybir.dt.bfloat16
U8 = mybir.dt.uint8
ALU = mybir.AluOpType
ACTF = mybir.ActivationFunctionType

B, S, D = 4, 8192, 1024
T = 4096
DC = 8
H = 16
DF, DFC = 4096, 32
NT = 512
KT = T // NT

C_S = 9.94969
C_ES = float(np.exp(np.float64(np.float32(C_S))))
LN4 = 1.3862943611198906
FMAX = 3.4028235e38
MAGIC = 12582912.0
INV2PI = 0.15915494309189535
TWOPI = 6.283185307179586
PIH = 1.5707963267948966
LN_EPS = 1e-5
AG_W = 3104
EXP_INF = 88.72284
EXP_CLAMP = 88.7228
BIG_C = 16384.0
THF = 1.8443e19
C1 = float(0.25 * 2.0 ** -64)
C_RESC = float(np.float32(C_ES) * np.float32(2.0 ** 64))
WKV_SC = float(2.0 ** -32)
WKV_CL = float(np.float32(3.3e38) * 2.0 ** -32)
RW_SC = float(2.0 ** 25)
LN2_64 = float(np.float32(64.0 * np.log(2.0)))
SC70 = float(2.0 ** -70)
P70 = float(2.0 ** 70)
P76 = float(2.0 ** 76)
SC64M = float(2.0 ** -64)
TQ_S = float(np.float32(FMAX) * 2.0 ** -140)
EPS_S = float(np.float32(LN_EPS) * 2.0 ** -64)
LNB_A = float(-32.0 * np.log(2.0))
LNB_B = float(-102.0 * np.log(2.0))
B1C = float(2.0 ** 58)
TBL = float(1e19 * 2.0 ** -140)
C17 = float(np.float32(3.3e38) * 2.0 ** -70)
CY_S = 1.0e6
VAR_TH = 1.0


class Cnt:
    def __init__(self):
        self.c = {}

    def inc(self, s, n):
        self.c[s.name] = self.c.get(s.name, 0) + n
        return self.c[s.name]

    def val(self, s):
        return self.c.get(s.name, 0)


_DRAIN_OPS = {
    "memset", "memzero", "tensor_copy", "tensor_tensor", "tensor_add",
    "tensor_sub", "tensor_mul", "tensor_max", "tensor_scalar",
    "tensor_single_scalar", "tensor_scalar_add", "tensor_scalar_sub",
    "tensor_scalar_mul", "tensor_scalar_max", "tensor_scalar_min",
    "tensor_relu", "tensor_reduce", "reduce_max", "reduce_sum",
    "scalar_tensor_tensor", "tensor_tensor_scan", "select", "copy_predicated",
    "reciprocal", "transpose", "activation", "sign", "square", "sqrt",
    "mul", "add", "copy", "iota", "affine_select",
}


class _Draining:
    def __init__(self, eng):
        self._e = eng

    def __getattr__(self, name):
        f = getattr(self._e, name)
        if name in _DRAIN_OPS:
            def g(*a, **k):
                self._e.drain()
                return f(*a, **k)
            return g
        return f


def _ap_box(ap):
    off = ap.offset
    span = 0
    for stride, num in ap.ap:
        if num > 1:
            span += (num - 1) * abs(stride)
    return (ap.tensor.name, off, off + span)


def _boxes_overlap(a, b):
    return a[0] == b[0] and a[1] <= b[2] and b[1] <= a[2]


class _Chained:

    def __init__(self, eng, sem, cnt):
        self._e = eng
        self._sem = sem
        self._cnt = cnt
        self._hist = []
        self._waited = 0

    def __getattr__(self, name):
        f = getattr(self._e, name)
        if name not in _DRAIN_OPS:
            return f

        def g(*a, **k):
            aps = [x for x in a if isinstance(x, bass.AP)]
            aps += [x for x in k.values() if isinstance(x, bass.AP)]
            wbox = _ap_box(aps[0]) if aps else None
            rboxes = [_ap_box(x) for x in aps[1:]]
            need = 0
            for seq, kind, box in self._hist:
                if seq <= self._waited:
                    continue
                if kind == "W":
                    if (wbox and _boxes_overlap(box, wbox)) or \
                       any(_boxes_overlap(box, rb) for rb in rboxes):
                        need = max(need, seq)
                elif wbox and _boxes_overlap(box, wbox):
                    need = max(need, seq)
            if need > self._waited:
                self._e.wait_ge(self._sem, need)
                self._waited = need
                self._hist = [h for h in self._hist if h[0] > need]
            res = f(*a, **k)
            res.then_inc(self._sem, 1)
            seq = self._cnt.inc(self._sem, 1)
            if wbox:
                self._hist.append((seq, "W", wbox))
            for rb in rboxes:
                self._hist.append((seq, "R", rb))
            if len(self._hist) > 256:
                self._hist = self._hist[-192:]
            return res
        return g


def build_program(debug=False, upto="out"):
    nc = bass.Bass("TRN2", target_bir_lowering=False)
    es = ExitStack()
    cnt = Cnt()
    sync, ten = nc.sync, nc.tensor
    gp = _Draining(nc.gpsimd)

    def din(name, shape):
        return nc.dram_tensor(name, shape, F32, kind="ExternalInput").ap()

    x_in = din("x_in", [D, T])
    wkT = din("wkT", [D, D]); wvT = din("wvT", [D, D]); wkaT = din("wkaT", [D, D])
    w1 = din("w1", [D, DF]); w2 = din("w2", [DF, D])
    wkeT = din("wkeT", [D, D]); wvaT = din("wvaT", [D, D])
    wreT = din("wreT", [D, D]); woT = din("woT", [D, D])
    l1fw = din("l1fw", [D, D])
    v_bk = din("bk", [D]); v_bv = din("bv", [D]); v_bka = din("bka", [D])
    v_g = din("gamma", [D]); v_be = din("beta", [D])
    v_b1 = din("b1", [DF]); v_b2 = din("b2", [D])
    v_th = din("theta", [DF]); v_qr = din("qwr", [DF]); v_qi = din("qwi", [DF])
    v_mk = din("mixk", [D]); v_mv = din("mixv", [D]); v_mr = din("mixr", [D])
    v_td = din("td", [H]); v_tf = din("tf", [H])
    selrow = din("selrow", [1, 8])
    t0c = din("t0c", [1, 1])
    hadc = din("hadc", [D, D])

    out_t = nc.dram_tensor("out", [D, T], F32, kind="ExternalOutput").ap()

    BFD = mybir.dt.bfloat16
    keys_s = nc.dram_tensor("keys_s", [D, T], F32).ap()
    kw_s = nc.dram_tensor("kw_s", [D, T], BFD).ap()
    pool_s = nc.dram_tensor("pool_s", [D, T], BFD).ap()
    x2_s = nc.dram_tensor("x2_s", [D, T], F32).ap()
    vals_s = nc.dram_tensor("vals_s", [D, T], F32).ap()
    x3_s = nc.dram_tensor("x3_s", [D, T], F32).ap()
    x4_s = nc.dram_tensor("x4_s", [D, T], F32).ap()
    ekv_s = nc.dram_tensor("ekv_s", [D, T], F32).ap()
    r_s = nc.dram_tensor("r_s", [D, T], BFD).ap()
    h_s = nc.dram_tensor("h_s", [D, T], BFD).ap()
    w1b_s = nc.dram_tensor("w1b_s", [8, 128, DF], BFD).ap()
    se_s = nc.dram_tensor("se_s", [H, T], F32).ap()
    w2b_s = nc.dram_tensor("w2b_s", [DC, 128, DF], BFD).ap()
    wqT_s = nc.dram_tensor("wqT_s", [D, D], BFD).ap()
    ag_in = [nc.dram_tensor(f"ag_in{i}", [1, AG_W], F32).ap() for i in range(4)]
    ag_out = [nc.dram_tensor(f"ag_out{i}", [8, AG_W], F32).ap() for i in range(4)]

    dbg = {}
    if debug:
        for nm in ["d_x2", "d_x3", "d_x4", "d_keys", "d_ekv", "d_num", "d_wkv",
                   "d_att"]:
            dbg[nm] = nc.dram_tensor(nm, [D, T], F32, kind="ExternalOutput").ap()
        dbg["d_rdn"] = nc.dram_tensor("d_rdn", [16, T], F32, kind="ExternalOutput").ap()

    def hv(hbm, t0, n, c0=0, ncs=DC):
        return hbm.rearrange("(c p) t -> p c t", p=128)[:, c0:c0 + ncs, t0:t0 + n]

    def sv(tile, per_c, c0, ncs, j0, n):
        return tile[:].rearrange("p (c n) -> p c n", n=per_c)[:, c0:c0 + ncs, j0:j0 + n]

    def sbuf(name, shape, dtype):
        return es.enter_context(nc.sbuf_tensor(name, shape, dtype))

    AR = 3 * DC * D
    arH = sbuf("arH", [128, AR], BF16)
    arK = sbuf("arK", [128, AR], BF16)
    tX = sbuf("tX", [128, DC * NT], F32)
    tY = sbuf("tY", [128, DC * NT], F32)
    tZ = sbuf("tZ", [128, DC * NT], F32)
    tB16 = sbuf("tB16", [128, DC * NT], BF16)
    tB16b = sbuf("tB16b", [128, DC * NT], BF16)
    sc = sbuf("sc", [128, DC * NT], F32)
    svA = sbuf("svA", [128, NT], F32)
    svB = sbuf("svB", [128, NT], F32)
    mu8 = sbuf("mu8", [128, DC * NT // 2], U8)

    pat = sbuf("pat", [128, 64], F32)
    PG, PBE, PBK, PBV, PBKA = 0, 8, 16, 24, 32
    PMK, PMV, PMR = 40, 48, 56
    patB = sbuf("patB", [128, 64], F32)
    QLA, QNLA, QEU, QB2 = 0, 8, 16, 24
    patDF = sbuf("patDF", [128, 4 * DFC], F32)
    PA, PB, PB1C, PB1S = 0, DFC, 2 * DFC, 3 * DFC
    patA1 = sbuf("patA1", [128, 2 * DFC], F32)
    binv = sbuf("binv", [128, DC * KT], F32)
    bpow = sbuf("bpow", [128, DC * KT], F32)
    b16k = sbuf("b16k", [16, 2 * KT], F32)
    la16 = sbuf("la16", [16, 8], F32)
    iot = sbuf("iot", [128, NT], F32)
    iotk = sbuf("iotk", [128, KT], F32)
    onesb = sbuf("onesb", [128, 1], BF16)
    ones2d = sbuf("ones2d", [128, 128], F32)
    zerosb = sbuf("zerosb", [128, NT], BF16)
    selc = sbuf("selc", [128, DC * 16], F32)
    selT = sbuf("selT", [16, D], F32)
    ident = sbuf("ident", [128, 128], F32)
    identb = sbuf("identb", [128, 128], BF16)
    onehot8 = sbuf("onehot8", [8, 1], F32)
    carry = sbuf("carry", [128, 48], F32)
    bmt = sbuf("bmt", [128, NT], F32)
    bigk = sbuf("bigk", [128, 8], F32)
    maxP = sbuf("maxP", [128, 8], F32)
    maxN = sbuf("maxN", [128, 8], F32)
    chainP = sbuf("chainP", [128, 8], F32)
    chainN = sbuf("chainN", [128, 8], F32)
    c16 = sbuf("c16", [16, 8], F32)
    lastK = sbuf("lastK", [128, 8], F32)
    lastP = sbuf("lastP", [128, 8], F32)
    lastA = sbuf("lastA", [128, 8], F32)
    sumkw = sbuf("sumkw", [128, 8], F32)
    scr8 = sbuf("scr8", [128, 8], F32)
    scr8b = sbuf("scr8b", [128, 8], F32)
    hrow = sbuf("hrow", [8, 128], F32)
    t0bc = sbuf("t0bc", [128, 1], F32)
    c_eps = sbuf("c_eps", [128, 1], F32)
    c_epsS = sbuf("c_epsS", [128, 1], F32)
    c_lnA = sbuf("c_lnA", [128, 1], F32)
    c_lnB = sbuf("c_lnB", [128, 1], F32)
    cse = sbuf("cse", [16, NT], F32)
    ccb = sbuf("ccb", [16, NT], F32)
    ct1 = sbuf("ct1", [16, NT], F32)
    ct2 = sbuf("ct2", [16, NT], F32)
    crd = sbuf("crd", [16, NT], F32)
    mu16 = sbuf("mu16", [16, NT], U8)

    ps0 = es.enter_context(nc.psum_tensor("ps0", [128, NT], F32))
    ps1 = es.enter_context(nc.psum_tensor("ps1", [128, NT], F32))
    ps2 = es.enter_context(nc.psum_tensor("ps2", [128, NT], F32))
    ps3 = es.enter_context(nc.psum_tensor("ps3", [128, NT], F32))
    psBig = es.enter_context(nc.psum_tensor("psBig", [128, 2 * NT], F32))
    psT = es.enter_context(nc.psum_tensor("psT", [128, NT], BF16))
    psAB = [ps0, ps1]

    s_ld = es.enter_context(nc.semaphore("s_ld"))
    s_gd = es.enter_context(nc.semaphore("s_gd"))
    s_st = es.enter_context(nc.semaphore("s_st"))
    s_ag = es.enter_context(nc.semaphore("s_ag"))
    s_pe = es.enter_context(nc.semaphore("s_pe"))
    s_ve = es.enter_context(nc.semaphore("s_ve"))
    s_se = es.enter_context(nc.semaphore("s_se"))
    s_gp = es.enter_context(nc.semaphore("s_gp"))
    s_h = es.enter_context(nc.semaphore("s_h"))
    s_pc = es.enter_context(nc.semaphore("s_pc"))
    s_sl0 = es.enter_context(nc.semaphore("s_sl0"))
    s_sl1 = es.enter_context(nc.semaphore("s_sl1"))
    s_la = es.enter_context(nc.semaphore("s_la"))

    vec = _Chained(nc.vector, s_ve, cnt)
    sca = _Chained(nc.scalar, s_se, cnt)

    def bar():
        for s in (s_ld, s_st, s_gd, s_h, s_pc, s_sl0, s_sl1, s_la):
            if cnt.val(s):
                sync.wait_ge(s, cnt.val(s))
        nc.all_engine_barrier()
        for w in (vec, sca):
            w._waited = cnt.val(w._sem)
            w._hist = []

    def tbar():
        sync.wait_ge(s_ve, cnt.val(s_ve))
        sync.wait_ge(s_se, cnt.val(s_se))
        sync.wait_ge(s_pe, cnt.val(s_pe))
        for s in (s_st, s_h):
            if cnt.val(s):
                sync.wait_ge(s, cnt.val(s))

    def vdone():
        pass

    W = {}

    with es:
        def load_pat(dst, col0, v, ncols):
            sync.dma_start(out=dst[:, col0:col0 + ncols],
                           in_=v.rearrange("(c p) -> p c", p=128),
                           allow_slow_non_contiguous=True).then_inc(s_ld, 16)
            cnt.inc(s_ld, 16)

        load_pat(pat, PG, v_g, 8); load_pat(pat, PBE, v_be, 8)
        load_pat(pat, PBK, v_bk, 8); load_pat(pat, PBV, v_bv, 8)
        load_pat(pat, PBKA, v_bka, 8)
        load_pat(pat, PMK, v_mk, 8); load_pat(pat, PMV, v_mv, 8)
        load_pat(pat, PMR, v_mr, 8)
        load_pat(patB, QB2, v_b2, 8)
        load_pat(patA1, 0, v_b1, DFC)
        load_pat(patDF, PA, v_th, DFC)
        load_pat(patDF, PB, v_qr, DFC)
        load_pat(patDF, PB1C, v_qi, DFC)
        sync.dma_start(out=la16[:, 6:7], in_=v_td[:, None]).then_inc(s_ld, 16); cnt.inc(s_ld, 16)
        sync.dma_start(out=la16[:, 7:8], in_=v_tf[:, None]).then_inc(s_ld, 16); cnt.inc(s_ld, 16)
        sync.dma_start(out=onehot8[:], in_=selrow.rearrange("o r -> r o")).then_inc(s_ld, 16)
        cnt.inc(s_ld, 16)
        sync.dma_start(out=t0bc[:], in_=bass.AP(t0c.tensor, 0, [[0, 128], [1, 1]]),
                       allow_slow_non_contiguous=True).then_inc(s_ld, 16)
        cnt.inc(s_ld, 16)

        def wload(dst, col0, srcT, cols=D):
            gp.dma_start(
                out=dst[:].rearrange("p (c e) -> p c e", e=cols)[:, col0 // cols:(col0 // cols) + srcT.shape[0] // 128, :],
                in_=srcT.rearrange("(c p) e -> p c e", p=128)).then_inc(s_gd, 16)
            cnt.inc(s_gd, 16)

        wload(arK, 0, wkT)
        wload(arK, 8 * D, wkaT)
        gp.memset(zerosb[:], 0.0)
        gp.memset(c_eps[:], LN_EPS)
        gp.memset(c_epsS[:], EPS_S)
        gp.memset(c_lnA[:], LNB_A)
        gp.memset(c_lnB[:], LNB_B)
        gp.memset(ones2d[:], 1.0)
        gp.memset(onesb[:], 1.0)
        gp.memset(carry[:], 0.0)
        gp.memset(c16[:], 0.0)
        gp.memset(lastK[:], 0.0)
        gp.memset(lastP[:], 0.0)
        gp.memset(lastA[:], 0.0)
        gp.memset(sumkw[:], 0.0)
        gp.iota(iot[:], [[1, NT]], base=0, channel_multiplier=0,
                allow_small_or_imprecise_dtypes=True)
        gp.iota(iotk[:], [[1, KT]], base=0, channel_multiplier=0,
                allow_small_or_imprecise_dtypes=True)
        def gstep(op):
            op.then_inc(s_gp, 1)
            cnt.inc(s_gp, 1)
            gp.wait_ge(s_gp, cnt.val(s_gp))

        gstep(gp.memset(ident[:], 1.0))
        gstep(gp.affine_select(ident[:], ident[:], [[1, 128]], ALU.is_equal, 0.0,
                               base=0, channel_multiplier=-1))
        gstep(gp.memset(identb[:], 1.0))
        gstep(gp.affine_select(identb[:], identb[:], [[1, 128]], ALU.is_equal, 0.0,
                               base=0, channel_multiplier=-1))
        gstep(gp.memset(selT[:], 1.0))
        gstep(gp.affine_select(selT[:], selT[:], [[1, D]], ALU.is_ge, 0.0,
                               base=0, channel_multiplier=-64))
        gstep(gp.affine_select(selT[:], selT[:], [[-1, D]], ALU.is_ge, 0.0,
                               base=63, channel_multiplier=64))
        gstep(gp.memset(selc[:], 1.0))
        for c in range(DC):
            slc = selc[:, c * 16:(c + 1) * 16]
            gstep(gp.affine_select(slc, slc, [[-64, 16]], ALU.is_ge, 0.0,
                                   base=128 * c, channel_multiplier=1))
            gstep(gp.affine_select(slc, slc, [[64, 16]], ALU.is_ge, 0.0,
                                   base=63 - 128 * c, channel_multiplier=-1))
        gp_done = cnt.val(s_gd)
        gp_done2 = cnt.val(s_gp)

        for e in (vec, sca, ten):
            e.wait_ge(s_ld, cnt.val(s_ld))
            e.wait_ge(s_gd, gp_done)
            e.wait_ge(s_gp, gp_done2)
        sca.activation(la16[:, 5:6], la16[:, 6:7], ACTF.Exp)
        sca.activation(la16[:, 4:5], la16[:, 5:6], ACTF.Exp, scale=-1.0)
        sca.activation(la16[:, 0:1], la16[:, 4:5], ACTF.Ln)
        si = sca.activation(la16[:, 2:3], la16[:, 7:8], ACTF.Exp)
        vec.wait_ge(s_se, cnt.val(s_se))
        vec.tensor_scalar(la16[:, 1:2], la16[:, 0:1], -1.0, None, ALU.mult)
        th = patDF[:, PA:PA + DFC]; qwr = patDF[:, PB:PB + DFC]
        qwi = patDF[:, PB1C:PB1C + DFC]
        thr = patA1[:, DFC:2 * DFC]
        vec.tensor_scalar(thr, th, INV2PI, MAGIC, ALU.mult, ALU.add)
        vec.tensor_scalar(thr, thr, MAGIC, None, ALU.subtract)
        vec.scalar_tensor_tensor(thr, thr, -TWOPI, th, ALU.mult, ALU.add)
        sth = tX[:, 0:DFC]; cth = tX[:, DFC:2 * DFC]
        s2 = tX[:, 2 * DFC:3 * DFC]
        vec.tensor_scalar(s2, th, 1.0, PIH, ALU.mult, ALU.add)
        vec.tensor_scalar(tX[:, 3 * DFC:4 * DFC], s2, INV2PI, MAGIC, ALU.mult, ALU.add)
        vec.tensor_scalar(tX[:, 3 * DFC:4 * DFC], tX[:, 3 * DFC:4 * DFC], MAGIC,
                          None, ALU.subtract)
        vec.scalar_tensor_tensor(s2, tX[:, 3 * DFC:4 * DFC], -TWOPI, s2,
                                 ALU.mult, ALU.add)
        vdone()
        sca.wait_ge(s_ve, cnt.val(s_ve))
        sca.activation(sth, thr, ACTF.Sin)
        si = sca.activation(cth, s2, ACTF.Sin)
        vec.wait_ge(s_se, cnt.val(s_se))
        qp = tY[:, 0:DFC]; scrD = tY[:, DFC:2 * DFC]
        vec.tensor_tensor(qp, cth, qwr, ALU.mult)
        vec.tensor_tensor(scrD, sth, qwi, ALU.mult)
        vec.tensor_tensor(qp, qp, scrD, ALU.add)
        vec.tensor_tensor(scrD, qp, qwi, ALU.mult)
        vec.tensor_scalar(scrD, scrD, 0.1, None, ALU.mult)
        vec.tensor_tensor(tY[:, 2 * DFC:3 * DFC], scrD, sth, ALU.mult)
        vec.tensor_tensor(tY[:, 3 * DFC:4 * DFC], scrD, cth, ALU.mult)
        vec.tensor_copy(patDF[:, PA:PA + DFC], tY[:, 2 * DFC:3 * DFC])
        vec.tensor_copy(patDF[:, PB:PB + DFC], tY[:, 3 * DFC:4 * DFC])
        vec.tensor_scalar(patDF[:, PB1C:PB1C + DFC], patA1[:, 0:DFC], 0.1, PIH,
                          ALU.mult, ALU.add)
        vec.tensor_scalar(patDF[:, PB1S:PB1S + DFC], patA1[:, 0:DFC], 0.1, None,
                          ALU.mult)
        vec.tensor_scalar(pat[:, PBKA:PBKA + 8], pat[:, PBKA:PBKA + 8],
                          1.0 / 32.0, None, ALU.mult)
        vdone()

        ten.wait_ge(s_ve, cnt.val(s_ve))
        for (colsrc, dstcol) in [(0, QLA), (1, QNLA), (2, QEU)]:
            for c in range(DC):
                mi = ten.matmul(ps2[:, c:c + 1], selT[:, c * 128:(c + 1) * 128],
                                la16[:, colsrc:colsrc + 1], start=True, stop=True)
            mi.then_inc(s_pe, 1); cnt.inc(s_pe, 1)
            vec.wait_ge(s_pe, cnt.val(s_pe))
            vec.tensor_copy(patB[:, dstcol:dstcol + 8], ps2[:, 0:8])
            vdone()
            ten.wait_ge(s_ve, cnt.val(s_ve))

        for c in range(DC):
            nla = patB[:, QNLA + c:QNLA + c + 1]
            la_ = patB[:, QLA + c:QLA + c + 1]
            vec.tensor_tensor(scr8[:, 0:1], nla, t0bc[:], ALU.mult)
            vec.tensor_scalar(scr8[:, 1:2], nla, 1.0, LN4, ALU.mult, ALU.add)
            vec.tensor_tensor(scr8[:, 0:1], scr8[:, 0:1], scr8[:, 1:2], ALU.add)
            vec.tensor_scalar(scr8[:, 2:3], nla, float(NT), None, ALU.mult)
            vec.tensor_scalar(svA[:, 0:KT], iotk[:, 0:KT], scr8[:, 2:3], None, ALU.mult)
            vec.tensor_scalar(binv[:, c * KT:(c + 1) * KT], svA[:, 0:KT],
                              scr8[:, 0:1], None, ALU.add)
            vec.tensor_tensor(scr8[:, 3:4], la_, t0bc[:], ALU.mult)
            vec.tensor_scalar(scr8[:, 3:4], scr8[:, 3:4], 1.0, -float(C_S), ALU.mult, ALU.add)
            vec.tensor_scalar(scr8[:, 4:5], la_, float(NT), None, ALU.mult)
            vec.tensor_scalar(svA[:, 0:KT], iotk[:, 0:KT], scr8[:, 4:5], None, ALU.mult)
            vec.tensor_scalar(bpow[:, c * KT:(c + 1) * KT], svA[:, 0:KT],
                              scr8[:, 3:4], None, ALU.add)
        vec.tensor_tensor(la16[:, 3:4], la16[:, 1:2], t0bc[0:16, :], ALU.mult)
        vec.tensor_scalar(la16[:, 6:7], la16[:, 1:2], 1.0, LN4, ALU.mult, ALU.add)
        vec.tensor_tensor(la16[:, 3:4], la16[:, 3:4], la16[:, 6:7], ALU.add)
        vec.tensor_scalar(la16[:, 6:7], la16[:, 1:2], float(NT), None, ALU.mult)
        vec.tensor_scalar(ct1[:, 0:KT], iotk[0:16, 0:KT], la16[:, 6:7], None, ALU.mult)
        vec.tensor_scalar(b16k[:, 0:KT], ct1[:, 0:KT], la16[:, 3:4], None, ALU.add)
        vec.tensor_tensor(la16[:, 3:4], la16[:, 0:1], t0bc[0:16, :], ALU.mult)
        vec.tensor_scalar(la16[:, 3:4], la16[:, 3:4], 1.0, -float(C_S), ALU.mult, ALU.add)
        vec.tensor_scalar(la16[:, 6:7], la16[:, 0:1], float(NT), None, ALU.mult)
        vec.tensor_scalar(ct1[:, 0:KT], iotk[0:16, 0:KT], la16[:, 6:7], None, ALU.mult)
        vec.tensor_scalar(b16k[:, KT:2 * KT], ct1[:, 0:KT], la16[:, 3:4], None, ALU.add)
        vec.memset(tZ[:], 0.0)
        vdone()
        sync.wait_ge(s_ve, cnt.val(s_ve))
        for i in range(4):
            sync.dma_start(out=ag_in[i], in_=tZ[0:1, 0:AG_W]).then_inc(s_st, 16)
            cnt.inc(s_st, 16)
        vec.tensor_scalar(bigk[:, 0:KT], iotk[:, 0:KT], -float(NT), BIG_C,
                          ALU.mult, ALU.add)
        vec.tensor_scalar(bigk[:, 0:KT], bigk[:, 0:KT], t0bc[:], None,
                          ALU.subtract)
        vdone()
        bar()

        evac_cnt = {"v": 0}

        def ln_stats_norm(n, y_tile, out_tile, armored=False):
            if armored:
                vec.tensor_scalar(sc[:, 0:DC * n], y_tile[:, 0:DC * n],
                                  B1C, -B1C, ALU.min, ALU.max)
                vdone()
                sca.wait_ge(s_ve, cnt.val(s_ve))
                si = sca.activation(tB16[:, 0:DC * n], sc[:, 0:DC * n], ACTF.Square)
                ten.wait_ge(s_se, cnt.val(s_se))
                for c in range(DC):
                    mi = ten.matmul(ps2[64:65, 0:n], onesb[:, 0:1],
                                    tB16[:, c * n:(c + 1) * n],
                                    start=(c == 0), stop=(c == DC - 1))
                mi.then_inc(s_pe, 1); cnt.inc(s_pe, 1)
                vec.wait_ge(s_pe, cnt.val(s_pe))
                vec.tensor_scalar(sc[:, 0:DC * n], y_tile[:, 0:DC * n],
                                  SC70, None, ALU.mult)
                vec.tensor_copy(tB16b[:, 0:DC * n], sc[:, 0:DC * n])
                vdone()
                sca.wait_ge(s_ve, cnt.val(s_ve))
                sca.wait_ge(s_pe, cnt.val(s_pe))
                si = sca.activation(tB16[:, 0:DC * n], sc[:, 0:DC * n], ACTF.Square)
            else:
                vec.tensor_copy(tB16b[:, 0:DC * n], y_tile[:, 0:DC * n])
                vdone()
                sca.wait_ge(s_ve, cnt.val(s_ve))
                si = sca.activation(tB16[:, 0:DC * n], y_tile[:, 0:DC * n], ACTF.Square)
            ten.wait_ge(s_ve, cnt.val(s_ve))
            ten.wait_ge(s_se, cnt.val(s_se))
            for c in range(DC):
                mi = ten.matmul(ps2[0:1, 0:n], onesb[:, 0:1],
                                tB16b[:, c * n:(c + 1) * n],
                                start=(c == 0), stop=(c == DC - 1))
            for c in range(DC):
                mi = ten.matmul(ps2[32:33, 0:n], onesb[:, 0:1],
                                tB16[:, c * n:(c + 1) * n],
                                start=(c == 0), stop=(c == DC - 1))
            mi.then_inc(s_pe, 1); cnt.inc(s_pe, 1)
            vec.wait_ge(s_pe, cnt.val(s_pe))
            mu = svA[0:1, 0:n]; var = svB[0:1, 0:n]
            rstd = ct1[0:1, 0:n]; c0r = ct2[0:1, 0:n]; msk = crd[0:1, 0:n]
            vec.tensor_scalar(mu, ps2[0:1, 0:n], 1.0 / D, None, ALU.mult)
            vec.tensor_scalar(var, ps2[32:33, 0:n], 1.0 / D, None, ALU.mult)
            if armored:
                aarg = sc[0:1, 0:n]; barg = sc[0:1, n:2 * n]
                mbl = sc[0:1, 2 * n:3 * n]; muc = sc[0:1, 3 * n:4 * n]
                vlo = sc[0:1, 4 * n:5 * n]; rb = tZ[0:1, 0:n]
                vec.tensor_scalar(msk, ps2[32:33, 0:n], TQ_S, None, ALU.is_lt)
                vec.tensor_tensor(c0r, mu, mu, ALU.mult)
                vec.tensor_tensor(var, var, c0r, ALU.subtract)
                vec.tensor_scalar(muc, mu, P70, B1C, ALU.mult, ALU.min)
                vec.tensor_scalar(muc, muc, -B1C, None, ALU.max)
                vec.tensor_tensor(muc, muc, muc, ALU.mult)
                vec.scalar_tensor_tensor(vlo, ps2[64:65, 0:n], 1.0 / D,
                                         muc, ALU.mult, ALU.subtract)
                vec.tensor_scalar(aarg, vlo, 1.5e19, 0.0, ALU.min, ALU.max)
                vec.tensor_scalar(barg, var, SC64M, 1e-30, ALU.mult, ALU.max)
                vec.tensor_scalar(mbl, var, TBL, None, ALU.is_gt)
                vdone()
                sca.wait_ge(s_ve, cnt.val(s_ve))
                sca.activation(rstd, aarg, ACTF.Ln, bias=c_eps[0:1, 0:1])
                sca.activation(rstd, rstd, ACTF.Exp, scale=-0.5)
                sca.activation(rb, barg, ACTF.Ln)
                si = sca.activation(rb, rb, ACTF.Exp, scale=-0.5,
                                    bias=c_lnB[0:1, 0:1])
                vec.wait_ge(s_se, cnt.val(s_se))
                vec.tensor_tensor(rb, rb, rstd, ALU.subtract)
                vec.tensor_tensor(rb, rb, mbl, ALU.mult)
                vec.tensor_tensor(rstd, rstd, rb, ALU.add)
                vec.tensor_tensor(rstd, rstd, msk, ALU.mult)
                vec.scalar_tensor_tensor(c0r, mu, -1.0, rstd, ALU.mult, ALU.mult)
                vec.tensor_scalar(c0r, c0r, C17, -C17, ALU.min, ALU.max)
                vec.tensor_scalar(c0r, c0r, P70, None, ALU.mult)
                vdone()
            else:
                vec.tensor_tensor(msk, mu, mu, ALU.mult)
                vec.tensor_tensor(var, var, msk, ALU.subtract)
                vdone()
                sca.wait_ge(s_ve, cnt.val(s_ve))
                sca.activation(rstd, var, ACTF.Ln, bias=c_eps[0:1, 0:1])
                si = sca.activation(rstd, rstd, ACTF.Exp, scale=-0.5)
                vec.wait_ge(s_se, cnt.val(s_se))
                vec.tensor_scalar(msk, var, 0.0, 0.0, ALU.mult, ALU.is_equal)
                vec.tensor_tensor(rstd, rstd, msk, ALU.mult)
                vec.scalar_tensor_tensor(c0r, mu, -1.0, rstd, ALU.mult, ALU.mult)
                vdone()
            ten.wait_ge(s_ve, cnt.val(s_ve))
            ten.matmul(ps3[:, 0:n], ones2d[0:1, :], rstd, start=True, stop=True)
            mi = ten.matmul(psBig[:, 0:n], ones2d[0:1, :], c0r, start=True, stop=True)
            mi.then_inc(s_pe, 1); cnt.inc(s_pe, 1)
            vec.wait_ge(s_pe, cnt.val(s_pe))
            for c in range(DC):
                slq = slice(c * n, (c + 1) * n)
                vec.tensor_tensor(out_tile[:, slq], y_tile[:, slq], ps3[:, 0:n], ALU.mult)
                vec.tensor_tensor(out_tile[:, slq], out_tile[:, slq], psBig[:, 0:n], ALU.add)

        for hh in range(2):
            gp.dma_start(
                out=arH[:, 0:4 * DF].rearrange("p (c f) -> p c f", f=DF),
                in_=w1[hh * 512:(hh + 1) * 512, :].rearrange("(c p) f -> p c f", p=128)
            ).then_inc(s_gd, 16)
            cnt.inc(s_gd, 16)
            gp.wait_ge(s_gd, cnt.val(s_gd))
            for q in range(8):
                gp.dma_start(
                    out=w1b_s[q].rearrange("p (kc v) -> p kc v", v=NT)[:, hh * 4:(hh + 1) * 4, :],
                    in_=arH[:, 0:4 * DF].rearrange("p (c f) -> p c f", f=DF)[:, :, q * NT:(q + 1) * NT]
                ).then_inc(s_pc, 16)
                cnt.inc(s_pc, 16)
            gp.wait_ge(s_pc, cnt.val(s_pc))
        for hh in range(4):
            gp.dma_start(
                out=arH[:, 0:8 * D].rearrange("p (c f) -> p c f", f=D),
                in_=w2[hh * 1024:(hh + 1) * 1024, :].rearrange("(c p) f -> p c f", p=128)
            ).then_inc(s_gd, 16)
            cnt.inc(s_gd, 16)
            gp.wait_ge(s_gd, cnt.val(s_gd))
            for e in range(DC):
                gp.dma_start(
                    out=w2b_s[e].rearrange("p (fc j) -> p fc j", j=128)[:, hh * 8:(hh + 1) * 8, :],
                    in_=arH[:, 0:8 * D].rearrange("p (c f) -> p c f", f=D)[:, :, e * 128:(e + 1) * 128]
                ).then_inc(s_pc, 16)
                cnt.inc(s_pc, 16)
            gp.wait_ge(s_pc, cnt.val(s_pc))


        for k in range(KT):
            vec.wait_ge(s_st, cnt.val(s_st))
            sca.wait_ge(s_st, cnt.val(s_st))
            sync.dma_start(out=sv(tX, NT, 0, DC, 0, NT),
                           in_=hv(x_in, k * NT, NT)).then_inc(s_ld, 16)
            cnt.inc(s_ld, 16)
            vec.wait_ge(s_ld, cnt.val(s_ld))
            sca.wait_ge(s_ld, cnt.val(s_ld))
            ln_stats_norm(NT, tX, tY)
            for c in range(DC):
                slq = slice(c * NT, (c + 1) * NT)
                vec.tensor_scalar(tB16[:, slq],
                                  tY[:, slq], pat[:, PG + c:PG + c + 1],
                                  pat[:, PBE + c:PBE + c + 1], ALU.mult, ALU.add)
            vdone()
            sync.wait_ge(s_ve, cnt.val(s_ve))
            sync.dma_start(out=hv(h_s, k * NT, NT),
                           in_=sv(tB16, NT, 0, DC, 0, NT)).then_inc(s_h, 16)
            cnt.inc(s_h, 16)
            ten.wait_ge(s_ve, cnt.val(s_ve))
            se_b = cnt.val(s_se)
            for e in range(DC):
                bank = psAB[e % 2]
                if e >= 2:
                    ten.wait_ge(s_se, se_b + e - 1)
                else:
                    ten.wait_ge(s_se, se_b)
                for kc in range(DC):
                    mi = ten.matmul(bank[:, 0:NT],
                                    arK[:, kc * D + e * 128: kc * D + e * 128 + 128],
                                    tB16[:, kc * NT: kc * NT + NT],
                                    start=(kc == 0), stop=(kc == DC - 1))
                mi.then_inc(s_pe, 1); cnt.inc(s_pe, 1)
                sca.wait_ge(s_pe, cnt.val(s_pe))
                si = sca.activation(tZ[:, e * NT:(e + 1) * NT], bank[:, 0:NT],
                                    ACTF.Identity, bias=pat[:, PBK + e:PBK + e + 1])
            vec.wait_ge(s_se, cnt.val(s_se))
            vec.wait_ge(s_pe, cnt.val(s_pe))
            vec.wait_ge(s_h, cnt.val(s_h))
            vec.tensor_copy(tB16[:, 0:DC * NT], tZ[:, 0:DC * NT])
            vdone()
            sync.wait_ge(s_se, cnt.val(s_se))
            sync.dma_start(out=hv(keys_s, k * NT, NT),
                           in_=sv(tZ, NT, 0, DC, 0, NT)).then_inc(s_st, 16)
            cnt.inc(s_st, 16)
            ten.wait_ge(s_ve, cnt.val(s_ve))
            se_b = cnt.val(s_se)
            for e in range(DC):
                bank = psAB[e % 2]
                if e >= 2:
                    ten.wait_ge(s_se, se_b + e - 1)
                else:
                    ten.wait_ge(s_se, se_b)
                for kc in range(DC):
                    mi = ten.matmul(bank[:, 0:NT],
                                    arK[:, (8 + kc) * D + e * 128: (8 + kc) * D + e * 128 + 128],
                                    tB16[:, kc * NT: kc * NT + NT],
                                    start=(kc == 0), stop=(kc == DC - 1))
                mi.then_inc(s_pe, 1); cnt.inc(s_pe, 1)
                sca.wait_ge(s_pe, cnt.val(s_pe))
                si = sca.activation(sc[:, e * NT:(e + 1) * NT], bank[:, 0:NT],
                                    ACTF.Exp, scale=1.0 / 32.0,
                                    bias=pat[:, PBKA + e:PBKA + e + 1])
                vec.wait_ge(s_se, cnt.val(s_se))
                vec.tensor_copy(tB16b[:, e * NT:(e + 1) * NT], sc[:, e * NT:(e + 1) * NT])
                vec.tensor_reduce(scr8[:, e:e + 1], sc[:, e * NT:(e + 1) * NT],
                                  mybir.AxisListType.X, ALU.add)
                vec.tensor_tensor(sumkw[:, e:e + 1], sumkw[:, e:e + 1],
                                  scr8[:, e:e + 1], ALU.add)
            vdone()
            sync.wait_ge(s_ve, cnt.val(s_ve))
            sync.dma_start(out=hv(kw_s, k * NT, NT),
                           in_=sv(tB16b, NT, 0, DC, 0, NT)).then_inc(s_st, 16)
            cnt.inc(s_st, 16)
            tbar()

        def allgather_carry(idx, src_cols, ccol0, with16=False):
            ten.wait_ge(s_ve, cnt.val(s_ve))
            ten.transpose(ps2[0:8, 0:128], src_cols[:], ident[:]).then_inc(s_pe, 1)
            cnt.inc(s_pe, 1)
            sca.wait_ge(s_pe, cnt.val(s_pe))
            sca.activation(hrow[:], ps2[0:8, 0:128], ACTF.Copy)
            sync.wait_ge(s_se, cnt.val(s_se))
            sync.dma_start(out=ag_in[idx][:, 0:1024].rearrange("o (r c) -> r (o c)", r=8),
                           in_=hrow[:]).then_inc(s_la, 16)
            cnt.inc(s_la, 16)
            if with16:
                sync.dma_start(out=ag_in[idx][:, 1024:1040].rearrange("o (w u) -> (o w) u", u=1),
                               in_=c16[:, 1:2]).then_inc(s_la, 16)
                cnt.inc(s_la, 16)
            gp.wait_ge(s_la, cnt.val(s_la))
            gp.collective_compute("AllGather", ALU.bypass,
                                  replica_groups=[list(range(8))],
                                  ins=[ag_in[idx].opt()],
                                  outs=[ag_out[idx].opt()]).then_inc(s_ag, 1)
            cnt.inc(s_ag, 1)
            sync.wait_ge(s_ag, cnt.val(s_ag))
            gselv = sc[0:8, 0:AG_W]
            sync.dma_start(out=gselv, in_=ag_out[idx]).then_inc(s_la, 16)
            cnt.inc(s_la, 16)
            ten.wait_ge(s_la, cnt.val(s_la))
            ten.matmul(ps2[0:1, 0:NT], onehot8[:], gselv[:, 0:NT], start=True, stop=True)
            mi = ten.matmul(ps2[32:33, 0:NT], onehot8[:], gselv[:, NT:2 * NT],
                            start=True, stop=True)
            if with16:
                mi = ten.matmul(ps2[64:65, 0:16], onehot8[:], gselv[:, 1024:1040],
                                start=True, stop=True)
            mi.then_inc(s_pe, 1); cnt.inc(s_pe, 1)
            sca.wait_ge(s_pe, cnt.val(s_pe))
            sca.activation(svA[0:1, 0:NT], ps2[0:1, 0:NT], ACTF.Copy)
            si = sca.activation(svA[32:33, 0:NT], ps2[32:33, 0:NT], ACTF.Copy)
            if with16:
                si = sca.activation(svA[64:65, 0:16], ps2[64:65, 0:16], ACTF.Copy)
            ten.wait_ge(s_se, cnt.val(s_se))
            for c in range(DC):
                rsrc = (svA[0:1, c * 128:(c + 1) * 128] if c < 4 else
                        svA[32:33, (c - 4) * 128:(c - 3) * 128])
                idk = ones2d[0:1, 0:1] if c < 4 else ones2d[32:33, 0:1]
                ten.transpose(ps3[:, 0:1], rsrc, idk).then_inc(s_pe, 1)
                cnt.inc(s_pe, 1)
                vec.wait_ge(s_pe, cnt.val(s_pe))
                vec.tensor_copy(carry[:, ccol0 + c:ccol0 + c + 1], ps3[:, 0:1])
                vdone()
                ten.wait_ge(s_ve, cnt.val(s_ve))
            if with16:
                ten.transpose(ps3[0:16, 0:1], svA[64:65, 0:16],
                              ones2d[64:65, 0:1]).then_inc(s_pe, 1)
                cnt.inc(s_pe, 1)
                vec.wait_ge(s_pe, cnt.val(s_pe))
                vec.tensor_copy(c16[:, 0:1], ps3[0:16, 0:1])
            bar()

        allgather_carry(0, sumkw, 0)
        vec.tensor_copy(lastK[:], carry[:, 0:8])
        vdone()
        bar()

        gp.wait_ge(s_pe, cnt.val(s_pe))
        wload(arK, 0, wvT)
        ten.wait_ge(s_gd, cnt.val(s_gd))
        for k in range(KT):
            vec.wait_ge(s_st, cnt.val(s_st))
            sca.wait_ge(s_st, cnt.val(s_st))
            sync.dma_start(out=sv(tX, NT, 0, DC, 0, NT),
                           in_=hv(keys_s, k * NT, NT)).then_inc(s_ld, 16)
            cnt.inc(s_ld, 16)
            sync.dma_start(out=sv(tB16, NT, 0, DC, 0, NT),
                           in_=hv(kw_s, k * NT, NT)).then_inc(s_ld, 16)
            cnt.inc(s_ld, 16)
            sync.dma_start(out=sv(arH, NT, 0, DC, 0, NT),
                           in_=hv(h_s, k * NT, NT)).then_inc(s_ld, 16)
            cnt.inc(s_ld, 16)
            ten.wait_ge(s_ld, cnt.val(s_ld))
            vp_done = []
            for e in range(DC):
                bank = psAB[e % 2]
                if e >= 2:
                    ten.wait_ge(s_se, vp_done[e - 2])
                else:
                    ten.wait_ge(s_se, cnt.val(s_se))
                for kc in range(DC):
                    mi = ten.matmul(bank[:, 0:NT],
                                    arK[:, kc * D + e * 128: kc * D + e * 128 + 128],
                                    arH[:, kc * NT: kc * NT + NT],
                                    start=(kc == 0), stop=(kc == DC - 1))
                mi.then_inc(s_pe, 1); cnt.inc(s_pe, 1)
                sca.wait_ge(s_pe, cnt.val(s_pe))
                si = sca.activation(sc[:, e * NT:(e + 1) * NT], bank[:, 0:NT],
                                    ACTF.Identity, bias=pat[:, PBV + e:PBV + e + 1])
                vp_done.append(cnt.val(s_se))
            sync.wait_ge(s_se, cnt.val(s_se))
            sync.dma_start(out=hv(vals_s, k * NT, NT),
                           in_=sv(sc, NT, 0, DC, 0, NT)).then_inc(s_st, 16)
            cnt.inc(s_st, 16)
            vec.wait_ge(s_ld, cnt.val(s_ld))
            for c in range(DC):
                slq = slice(c * NT, (c + 1) * NT)
                vec.tensor_tensor_scan(tY[:, slq], tB16[:, slq], zerosb[:, 0:NT],
                                       lastK[:, c:c + 1], ALU.add, ALU.add)
                vec.tensor_copy(lastK[:, c:c + 1], tY[:, c * NT + NT - 1:c * NT + NT])
            vdone()
            sca.wait_ge(s_ve, cnt.val(s_ve))
            sca.activation(tZ[:, 0:DC * NT], tY[:, 0:DC * NT], ACTF.Ln)
            si = sca.activation(tZ[:, 0:DC * NT], tZ[:, 0:DC * NT], ACTF.Exp, scale=-1.0)
            vec.wait_ge(s_se, cnt.val(s_se))
            for c in range(DC):
                slq = slice(c * NT, (c + 1) * NT)
                vec.tensor_tensor(tY[:, slq], tB16[:, slq], tX[:, slq], ALU.mult)
                vec.tensor_tensor(tY[:, slq], tY[:, slq], tZ[:, slq], ALU.mult)
            for c in range(DC):
                slq = slice(c * NT, (c + 1) * NT)
                vec.tensor_tensor_scan(tZ[:, slq], tY[:, slq], zerosb[:, 0:NT],
                                       lastP[:, c:c + 1], ALU.add, ALU.add)
                vec.tensor_copy(lastP[:, c:c + 1], tZ[:, c * NT + NT - 1:c * NT + NT])
            vec.tensor_copy(tB16b[:, 0:DC * NT], tZ[:, 0:DC * NT])
            vdone()
            sync.wait_ge(s_ve, cnt.val(s_ve))
            sync.dma_start(out=hv(pool_s, k * NT, NT),
                           in_=sv(tB16b, NT, 0, DC, 0, NT)).then_inc(s_st, 16)
            cnt.inc(s_st, 16)
            tbar()

        allgather_carry(1, lastP, 8)

        for k in range(KT):
            vec.wait_ge(s_st, cnt.val(s_st))
            sca.wait_ge(s_st, cnt.val(s_st))
            sync.dma_start(out=sv(tX, NT, 0, DC, 0, NT),
                           in_=hv(x_in, k * NT, NT)).then_inc(s_ld, 16)
            cnt.inc(s_ld, 16)
            sync.dma_start(out=sv(tB16b, NT, 0, DC, 0, NT),
                           in_=hv(pool_s, k * NT, NT)).then_inc(s_ld, 16)
            cnt.inc(s_ld, 16)
            sync.dma_start(out=sv(sc, NT, 0, DC, 0, NT),
                           in_=hv(vals_s, k * NT, NT)).then_inc(s_ld, 16)
            cnt.inc(s_ld, 16)
            vec.wait_ge(s_ld, cnt.val(s_ld))
            for c in range(DC):
                slq = slice(c * NT, (c + 1) * NT)
                vec.scalar_tensor_tensor(tY[:, slq], tB16b[:, slq],
                                         carry[:, 8 + c:8 + c + 1],
                                         sc[:, slq], ALU.add, ALU.mult)
                vec.tensor_tensor(tY[:, slq], tY[:, slq], tX[:, slq], ALU.add)
            vdone()
            sca.wait_ge(s_ve, cnt.val(s_ve))
            ln_stats_norm(NT, tY, tZ)
            vdone()
            sync.wait_ge(s_ve, cnt.val(s_ve))
            sync.dma_start(out=hv(x2_s, k * NT, NT),
                           in_=sv(tZ, NT, 0, DC, 0, NT)).then_inc(s_st, 16)
            cnt.inc(s_st, 16)
            tbar()

        if upto == "x2":
            _finalize(nc, sync, cnt, s_st, dbg, debug,
                      [("d_x2", x2_s, D * T), ("d_keys", keys_s, D * T)])
            return nc

        W2SL = DFC * NT
        bar()
        for k in range(KT):
            vec.wait_ge(s_st, cnt.val(s_st))
            sca.wait_ge(s_st, cnt.val(s_st))
            sync.dma_start(out=sv(tX, NT, 0, DC, 0, NT),
                           in_=hv(x2_s, k * NT, NT)).then_inc(s_ld, 16)
            cnt.inc(s_ld, 16)
            vec.wait_ge(s_ld, cnt.val(s_ld))
            vec.tensor_copy(tB16[:, 0:DC * NT], tX[:, 0:DC * NT])
            vdone()
            ten.wait_ge(s_ve, cnt.val(s_ve))
            banks4 = [psBig[:, 0:NT], psBig[:, NT:2 * NT],
                      ps2[:, 0:NT], ps3[:, 0:NT]]
            pe_q = cnt.val(s_pe)
            se_done4, ve_done4 = [], []
            for q in range(8):
                ssl = s_sl0 if q % 2 == 0 else s_sl1
                if q >= 2:
                    sync.wait_ge(s_pe, pe_q + 4 * (q - 1))
                sync.dma_start(out=arH[:, (q % 2) * DF: (q % 2) * DF + DF],
                               in_=w1b_s[q, :, :]).then_inc(ssl, 16)
                cnt.inc(ssl, 16)
                ten.wait_ge(ssl, cnt.val(ssl))
                woq = (q % 2) * DF
                for fq in range(4):
                    fc = q * 4 + fq
                    bank = banks4[fc % 4]
                    slq = slice((fc % 4) * NT, (fc % 4) * NT + NT)
                    if fc >= 4:
                        ten.wait_ge(s_se, se_done4[fc - 4])
                        ten.wait_ge(s_ve, ve_done4[fc - 4])
                    for kc in range(DC):
                        mi = ten.matmul(bank,
                                        arH[:, woq + kc * NT + fq * 128: woq + kc * NT + fq * 128 + 128],
                                        tB16[:, kc * NT: kc * NT + NT],
                                        start=(kc == 0), stop=(kc == DC - 1))
                    mi.then_inc(s_pe, 1); cnt.inc(s_pe, 1)
                    sca.wait_ge(s_pe, cnt.val(s_pe))
                    sca.activation(tY[:, slq], bank, ACTF.Sin, scale=0.1,
                                   bias=patDF[:, PB1C + fc:PB1C + fc + 1])
                    sca.activation(tZ[:, slq], bank, ACTF.Sin, scale=0.1,
                                   bias=patDF[:, PB1S + fc:PB1S + fc + 1])
                    se_done4.append(cnt.val(s_se))
                    vec.wait_ge(s_se, cnt.val(s_se))
                    vec.scalar_tensor_tensor(sc[:, slq], tY[:, slq],
                                             patDF[:, PA + fc:PA + fc + 1],
                                             bank, ALU.mult, ALU.add)
                    vec.scalar_tensor_tensor(sc[:, slq], tZ[:, slq],
                                             patDF[:, PB + fc:PB + fc + 1],
                                             sc[:, slq], ALU.mult, ALU.add)
                    vec.tensor_scalar(arK[:, fc * NT:fc * NT + NT], sc[:, slq],
                                      patA1[:, fc:fc + 1], 0.0, ALU.add, ALU.max)
                    ve_done4.append(cnt.val(s_ve))
            ten.wait_ge(s_ve, ve_done4[-1])
            se_b = cnt.val(s_se)
            pe_b = cnt.val(s_pe)
            for e in range(DC):
                ssl = s_sl0 if e % 2 == 0 else s_sl1
                if e >= 2:
                    sync.wait_ge(s_pe, pe_b + e - 1)
                else:
                    sync.wait_ge(s_pe, pe_b)
                sync.dma_start(out=arK[:, W2SL + (e % 2) * DFC * 128: W2SL + (e % 2) * DFC * 128 + DFC * 128],
                               in_=w2b_s[e, :, :]).then_inc(ssl, 16)
                cnt.inc(ssl, 16)
                ten.wait_ge(ssl, cnt.val(ssl))
                if e >= 2:
                    ten.wait_ge(s_se, se_b + e - 1)
                else:
                    ten.wait_ge(s_se, se_b)
                bank = psAB[e % 2]
                woff = W2SL + (e % 2) * DFC * 128
                for fc in range(DFC):
                    mi = ten.matmul(bank[:, 0:NT],
                                    arK[:, woff + fc * 128: woff + fc * 128 + 128],
                                    arK[:, fc * NT: fc * NT + NT],
                                    start=(fc == 0), stop=(fc == DFC - 1))
                mi.then_inc(s_pe, 1); cnt.inc(s_pe, 1)
                sca.wait_ge(s_pe, cnt.val(s_pe))
                si = sca.activation(tY[:, e * NT:(e + 1) * NT], bank[:, 0:NT],
                                    ACTF.Identity, bias=patB[:, QB2 + e:QB2 + e + 1])
            vec.wait_ge(s_se, cnt.val(s_se))
            vec.tensor_tensor(tY[:, 0:DC * NT], tY[:, 0:DC * NT], tX[:, 0:DC * NT],
                              ALU.add)
            vdone()
            ln_stats_norm(NT, tY, tZ)
            vdone()
            sync.wait_ge(s_ve, cnt.val(s_ve))
            sync.dma_start(out=hv(x3_s, k * NT, NT),
                           in_=sv(tZ, NT, 0, DC, 0, NT)).then_inc(s_st, 16)
            cnt.inc(s_st, 16)
            tbar()

        if upto == "x3":
            _finalize(nc, sync, cnt, s_st, dbg, debug,
                      [("d_x2", x2_s, D * T), ("d_x3", x3_s, D * T)])
            return nc

        for c in range(DC):
            vec.tensor_copy(scr8b[:, c:c + 1], tZ[:, c * NT + NT - 1:c * NT + NT])
        vdone()
        ten.wait_ge(s_ve, cnt.val(s_ve))
        allgather_carry(2, scr8b, 16)

        wload(arH, 0, wkeT)
        wload(arH, 8 * D, wvaT)
        wload(arH, 16 * D, wreT)
        for e in (vec, sca, ten):
            e.wait_ge(s_gd, cnt.val(s_gd))
        vec.memset(lastA[:], 0.0)
        vec.memset(maxP[:], 0.0)
        vec.memset(maxN[:], 0.0)
        vec.memset(c16[:, 1:2], 0.0)
        vec.memset(c16[:, 6:7], 0.0)
        vdone()
        bar()

        def proj_l1(wbase, ksl_tile, evac, eng=None, rbase=0):
            eng = eng or sca
            esem = eng._sem
            done = []
            for e in range(DC):
                bank = psAB[e % 2]
                if e >= 2:
                    ten.wait_ge(esem, done[e - 2])
                else:
                    ten.wait_ge(esem, cnt.val(esem))
                for kc in range(DC):
                    mi = ten.matmul(bank[:, 0:NT],
                                    arH[:, wbase + kc * D + e * 128: wbase + kc * D + e * 128 + 128],
                                    ksl_tile[:, rbase + kc * NT: rbase + kc * NT + NT],
                                    start=(kc == 0), stop=(kc == DC - 1))
                mi.then_inc(s_pe, 1); cnt.inc(s_pe, 1)
                eng.wait_ge(s_pe, cnt.val(s_pe))
                evac(e, bank)
                done.append(cnt.val(esem))

        def inv_alpha_tile(k, dst, scr_t):
            for c in range(DC):
                slq = slice(c * NT, (c + 1) * NT)
                vec.tensor_scalar(scr_t[:, slq], iot[:, 0:NT],
                                  patB[:, QNLA + c:QNLA + c + 1],
                                  binv[:, c * KT + k:c * KT + k + 1],
                                  ALU.mult, ALU.add)
            vec.tensor_scalar(scr_t[:, 0:DC * NT], scr_t[:, 0:DC * NT], EXP_CLAMP,
                              None, ALU.min)
            vdone()
            sca.wait_ge(s_ve, cnt.val(s_ve))
            si = sca.activation(dst[:, 0:DC * NT], scr_t[:, 0:DC * NT], ACTF.Exp)
            vec.wait_ge(s_se, cnt.val(s_se))

        def inv16_tile(k, dst, scr_c):
            vec.tensor_scalar(scr_c[:, 0:NT], iot[0:16, 0:NT], la16[:, 1:2],
                              b16k[:, k:k + 1], ALU.mult, ALU.add)
            vec.tensor_scalar(scr_c[:, 0:NT], scr_c[:, 0:NT], EXP_CLAMP, None, ALU.min)
            vdone()
            sca.wait_ge(s_ve, cnt.val(s_ve))
            si = sca.activation(dst[:, 0:NT], scr_c[:, 0:NT], ACTF.Exp)
            vec.wait_ge(s_se, cnt.val(s_se))

        for k in range(KT):
            vec.wait_ge(s_st, cnt.val(s_st))
            sca.wait_ge(s_st, cnt.val(s_st))
            sync.dma_start(out=sv(tX, NT, 0, DC, 0, NT),
                           in_=hv(x3_s, k * NT, NT)).then_inc(s_ld, 16)
            cnt.inc(s_ld, 16)
            if k == 0:
                sync.dma_start(out=sv(tY, NT, 0, DC, 1, NT - 1),
                               in_=hv(x3_s, 0, NT - 1)).then_inc(s_ld, 16)
                cnt.inc(s_ld, 16)
                vec.wait_ge(s_ld, cnt.val(s_ld))
                for c in range(DC):
                    vec.tensor_copy(tY[:, c * NT:c * NT + 1],
                                    carry[:, 16 + c:16 + c + 1])
            else:
                sync.dma_start(out=sv(tY, NT, 0, DC, 0, NT),
                               in_=hv(x3_s, k * NT - 1, NT)).then_inc(s_ld, 16)
                cnt.inc(s_ld, 16)
                vec.wait_ge(s_ld, cnt.val(s_ld))
            vec.tensor_tensor(tX[:, 0:DC * NT], tX[:, 0:DC * NT], tY[:, 0:DC * NT],
                              ALU.subtract)
            vdone()

            for c in range(DC):
                slq = slice(c * NT, (c + 1) * NT)
                vec.scalar_tensor_tensor(arK[:, c * NT:c * NT + NT], tX[:, slq],
                                         pat[:, PMK + c:PMK + c + 1],
                                         tY[:, slq], ALU.mult, ALU.add)
                vec.scalar_tensor_tensor(arK[:, DF + c * NT:DF + c * NT + NT],
                                         tX[:, slq],
                                         pat[:, PMV + c:PMV + c + 1],
                                         tY[:, slq], ALU.mult, ALU.add)
                vec.scalar_tensor_tensor(arK[:, 2 * DF + c * NT:2 * DF + c * NT + NT],
                                         tX[:, slq],
                                         pat[:, PMR + c:PMR + c + 1],
                                         tY[:, slq], ALU.mult, ALU.add)
            vdone()
            ten.wait_ge(s_ve, cnt.val(s_ve))

            def evac_k(e, bank):
                si = sca.activation(sc[:, e * NT:(e + 1) * NT], bank[:, 0:NT], ACTF.Exp)
            proj_l1(0, arK, evac_k, rbase=0)

            def evac_v(e, bank):
                si = sca.activation(tZ[:, e * NT:(e + 1) * NT], bank[:, 0:NT],
                                    ACTF.Identity)
            proj_l1(8 * D, arK, evac_v, rbase=DF)

            def evac_r(e, bank):
                si = sca.activation(tB16b[:, e * NT:(e + 1) * NT], bank[:, 0:NT],
                                    ACTF.Sigmoid)
            proj_l1(16 * D, arK, evac_r, rbase=2 * DF)

            ten.wait_ge(s_se, cnt.val(s_se))
            for c in range(DC):
                mi = ten.matmul(ps2[0:16, 0:NT], selc[:, c * 16:(c + 1) * 16],
                                sc[:, c * NT:(c + 1) * NT],
                                start=(c == 0), stop=(c == DC - 1))
            mi.then_inc(s_pe, 1); cnt.inc(s_pe, 1)
            vec.wait_ge(s_pe, cnt.val(s_pe))
            vec.tensor_copy(cse[:, 0:NT], ps2[0:16, 0:NT])
            vdone()
            sync.wait_ge(s_ve, cnt.val(s_ve))
            sync.dma_start(out=se_s[:, k * NT:(k + 1) * NT], in_=cse[:, 0:NT]).then_inc(s_st, 16)
            cnt.inc(s_st, 16)
            sync.wait_ge(s_se, cnt.val(s_se))
            sync.dma_start(out=hv(r_s, k * NT, NT),
                           in_=sv(tB16b, NT, 0, DC, 0, NT)).then_inc(s_st, 16)
            cnt.inc(s_st, 16)

            vec.wait_ge(s_se, cnt.val(s_se))
            vec.tensor_tensor(tZ[:, 0:DC * NT], sc[:, 0:DC * NT], tZ[:, 0:DC * NT],
                              ALU.mult)
            vdone()
            sync.wait_ge(s_ve, cnt.val(s_ve))
            sync.dma_start(out=hv(ekv_s, k * NT, NT),
                           in_=sv(tZ, NT, 0, DC, 0, NT)).then_inc(s_st, 16)
            cnt.inc(s_st, 16)

            inv_alpha_tile(k, tX, tY)
            vec.scalar_tensor_tensor(tX[:, 0:DC * NT], tZ[:, 0:DC * NT], C1,
                                     tX[:, 0:DC * NT], ALU.mult, ALU.mult)
            vec.tensor_scalar(bmt[:, 0:NT], iot[:, 0:NT], -1.0,
                              bigk[:, k:k + 1], ALU.mult, ALU.add)
            vec.wait_ge(s_pe, cnt.val(s_pe))
            u1 = psBig[:, 0:NT]
            for c in range(DC):
                slq = slice(c * NT, (c + 1) * NT)
                vec.tensor_tensor_scan(tY[:, slq], tX[:, slq], zerosb[:, 0:NT],
                                       lastA[:, c:c + 1], ALU.add, ALU.add)
                vec.tensor_copy(lastA[:, c:c + 1], tY[:, c * NT + NT - 1:c * NT + NT])
                vec.tensor_tensor(u1, tX[:, slq], tY[:, slq], ALU.max)
                vec.scalar_tensor_tensor(u1, u1, THF, bmt[:, 0:NT],
                                         ALU.is_ge, ALU.mult)
                vec.tensor_reduce(scr8[:, c:c + 1], u1, mybir.AxisListType.X, ALU.max)
                vec.tensor_tensor(u1, tX[:, slq], tY[:, slq], ALU.min)
                vec.scalar_tensor_tensor(u1, u1, -THF, bmt[:, 0:NT],
                                         ALU.is_le, ALU.mult)
                vec.tensor_reduce(scr8b[:, c:c + 1], u1, mybir.AxisListType.X, ALU.max)
            vec.tensor_tensor(maxP[:], maxP[:], scr8[:], ALU.max)
            vec.tensor_tensor(maxN[:], maxN[:], scr8b[:], ALU.max)
            vdone()
            inv16_tile(k, ct2, ct1)
            vec.scalar_tensor_tensor(ccb[:, 0:NT], cse[:, 0:NT], C1,
                                     ct2[:, 0:NT], ALU.mult, ALU.mult)
            vec.tensor_tensor_scan(ct1[:, 0:NT], ccb[:, 0:NT], zerosb[0:16, 0:NT],
                                   c16[:, 1:2], ALU.add, ALU.add)
            vec.tensor_copy(c16[:, 1:2], ct1[:, NT - 1:NT])
            vec.tensor_tensor(ct2[:, 0:NT], ccb[:, 0:NT], ct1[:, 0:NT], ALU.max)
            vec.scalar_tensor_tensor(ct2[:, 0:NT], ct2[:, 0:NT], THF,
                                     bmt[0:16, 0:NT], ALU.is_ge, ALU.mult)
            vec.tensor_reduce(c16[:, 4:5], ct2[:, 0:NT], mybir.AxisListType.X, ALU.max)
            vec.tensor_tensor(c16[:, 6:7], c16[:, 6:7], c16[:, 4:5], ALU.max)
            vdone()
            tbar()

        ten.wait_ge(s_ve, cnt.val(s_ve))
        for (src_cols, off) in ((lastA[:], 0), (maxP[:], 1024), (maxN[:], 2048)):
            ten.transpose(ps2[0:8, 0:128], src_cols, ident[:]).then_inc(s_pe, 1)
            cnt.inc(s_pe, 1)
            sca.wait_ge(s_pe, cnt.val(s_pe))
            sca.wait_ge(s_la, cnt.val(s_la))
            sca.activation(hrow[:], ps2[0:8, 0:128], ACTF.Copy)
            sync.wait_ge(s_se, cnt.val(s_se))
            sync.dma_start(out=ag_in[3][:, off:off + 1024].rearrange("o (r c) -> r (o c)", r=8),
                           in_=hrow[:]).then_inc(s_la, 16)
            cnt.inc(s_la, 16)
            ten.wait_ge(s_se, cnt.val(s_se))
        sync.dma_start(out=ag_in[3][:, 3072:3088].rearrange("o (w u) -> (o w) u", u=1),
                       in_=c16[:, 1:2]).then_inc(s_la, 16)
        cnt.inc(s_la, 16)
        sync.dma_start(out=ag_in[3][:, 3088:3104].rearrange("o (w u) -> (o w) u", u=1),
                       in_=c16[:, 6:7]).then_inc(s_la, 16)
        cnt.inc(s_la, 16)
        gp.wait_ge(s_la, cnt.val(s_la))
        gp.collective_compute("AllGather", ALU.bypass,
                              replica_groups=[list(range(8))],
                              ins=[ag_in[3].opt()],
                              outs=[ag_out[3].opt()]).then_inc(s_ag, 1)
        cnt.inc(s_ag, 1)
        sync.wait_ge(s_ag, cnt.val(s_ag))
        gselv = sc[0:8, 0:AG_W]
        sync.dma_start(out=gselv, in_=ag_out[3]).then_inc(s_la, 16)
        cnt.inc(s_la, 16)
        ten.wait_ge(s_la, cnt.val(s_la))
        for (off, ccol0) in ((0, 24), (1024, 32), (2048, 40)):
            ten.matmul(ps2[0:1, 0:NT], onehot8[:], gselv[:, off:off + NT],
                       start=True, stop=True)
            mi = ten.matmul(ps2[32:33, 0:NT], onehot8[:], gselv[:, off + NT:off + 2 * NT],
                            start=True, stop=True)
            mi.then_inc(s_pe, 1); cnt.inc(s_pe, 1)
            sca.wait_ge(s_pe, cnt.val(s_pe))
            sca.activation(svA[0:1, 0:NT], ps2[0:1, 0:NT], ACTF.Copy)
            si = sca.activation(svA[32:33, 0:NT], ps2[32:33, 0:NT], ACTF.Copy)
            ten.wait_ge(s_se, cnt.val(s_se))
            for c in range(DC):
                rsrc = (svA[0:1, c * 128:(c + 1) * 128] if c < 4 else
                        svA[32:33, (c - 4) * 128:(c - 3) * 128])
                idk = ones2d[0:1, 0:1] if c < 4 else ones2d[32:33, 0:1]
                ten.transpose(ps3[:, 0:1], rsrc, idk).then_inc(s_pe, 1)
                cnt.inc(s_pe, 1)
                vec.wait_ge(s_pe, cnt.val(s_pe))
                vec.tensor_copy(carry[:, ccol0 + c:ccol0 + c + 1], ps3[:, 0:1])
                vdone()
                ten.wait_ge(s_ve, cnt.val(s_ve))
        ten.matmul(ps2[0:1, 0:16], onehot8[:], gselv[:, 3072:3088], start=True, stop=True)
        mi = ten.matmul(ps2[32:33, 0:16], onehot8[:], gselv[:, 3088:3104],
                        start=True, stop=True)
        mi.then_inc(s_pe, 1); cnt.inc(s_pe, 1)
        sca.wait_ge(s_pe, cnt.val(s_pe))
        sca.activation(svA[0:1, 0:16], ps2[0:1, 0:16], ACTF.Copy)
        si = sca.activation(svA[32:33, 0:16], ps2[32:33, 0:16], ACTF.Copy)
        ten.wait_ge(s_se, cnt.val(s_se))
        ten.transpose(ps3[0:16, 0:1], svA[0:1, 0:16], ones2d[0:1, 0:1]).then_inc(s_pe, 1)
        cnt.inc(s_pe, 1)
        vec.wait_ge(s_pe, cnt.val(s_pe))
        vec.tensor_copy(c16[:, 0:1], ps3[0:16, 0:1])
        vdone()
        ten.wait_ge(s_ve, cnt.val(s_ve))
        ten.transpose(ps3[0:16, 0:1], svA[32:33, 0:16], ones2d[32:33, 0:1]).then_inc(s_pe, 1)
        cnt.inc(s_pe, 1)
        vec.wait_ge(s_pe, cnt.val(s_pe))
        vec.tensor_copy(c16[:, 2:3], ps3[0:16, 0:1])
        vdone()
        bar()

        if upto == "front":
            _finalize(nc, sync, cnt, s_st, dbg, debug,
                      [("d_x2", x2_s, 0), ("d_x3", x3_s, 0), ("d_ekv", ekv_s, 0)])
            return nc

        wload(arH, 0, woT)
        ten.wait_ge(s_gd, cnt.val(s_gd))
        vec.tensor_copy(lastP[:], carry[:, 24:32])
        vec.tensor_copy(chainP[:], carry[:, 32:40])
        vec.tensor_copy(chainN[:], carry[:, 40:48])
        vec.tensor_copy(c16[:, 3:4], c16[:, 0:1])
        vec.tensor_copy(c16[:, 4:5], c16[:, 2:3])
        vec.memset(c16[:, 7:8], -LN2_64)
        vdone()
        bar()
        for k in range(KT):
            vec.wait_ge(s_st, cnt.val(s_st))
            sca.wait_ge(s_st, cnt.val(s_st))
            sync.dma_start(out=cse[:, 0:NT], in_=se_s[:, k * NT:(k + 1) * NT]).then_inc(s_ld, 16)
            cnt.inc(s_ld, 16)
            ld_cse = cnt.val(s_ld)
            sync.dma_start(out=sv(tZ, NT, 0, DC, 0, NT),
                           in_=hv(ekv_s, k * NT, NT)).then_inc(s_ld, 16)
            cnt.inc(s_ld, 16)
            ld_ekv = cnt.val(s_ld)
            sync.dma_start(out=sv(tX, NT, 0, DC, 0, NT),
                           in_=hv(x3_s, k * NT, NT)).then_inc(s_ld, 16)
            cnt.inc(s_ld, 16)
            sync.dma_start(out=sv(tB16b, NT, 0, DC, 0, NT),
                           in_=hv(r_s, k * NT, NT)).then_inc(s_ld, 16)
            cnt.inc(s_ld, 16)
            vec.wait_ge(s_ld, ld_cse)
            sca.wait_ge(s_ld, ld_cse)
            vec.tensor_scalar(bmt[:, 0:NT], iot[:, 0:NT], -1.0,
                              bigk[:, k:k + 1], ALU.mult, ALU.add)
            inv16_tile(k, ct2, ct1)
            vec.scalar_tensor_tensor(ccb[:, 0:NT], cse[:, 0:NT], C1,
                                     ct2[:, 0:NT], ALU.mult, ALU.mult)
            vec.tensor_tensor_scan(ct2[:, 0:NT], ccb[:, 0:NT], zerosb[0:16, 0:NT],
                                   c16[:, 3:4], ALU.add, ALU.add)
            vec.tensor_tensor(ct1[:, 0:NT], ccb[:, 0:NT], ct2[:, 0:NT], ALU.max)
            vec.scalar_tensor_tensor(ct1[:, 0:NT], ct1[:, 0:NT], THF,
                                     bmt[0:16, 0:NT], ALU.is_ge, ALU.mult)
            vec.tensor_tensor_scan(ccb[:, 0:NT], ct1[:, 0:NT], zerosb[0:16, 0:NT],
                                   c16[:, 4:5], ALU.max, ALU.max)
            vec.tensor_copy(crd[:, 1:NT], ct2[:, 0:NT - 1])
            vec.tensor_copy(crd[:, 0:1], c16[:, 3:4])
            vec.scalar_tensor_tensor(crd[:, 1:NT], ccb[:, 0:NT - 1], 0.0,
                                     crd[:, 1:NT], ALU.is_le, ALU.mult)
            vec.scalar_tensor_tensor(crd[:, 0:1], c16[:, 4:5], 0.0,
                                     crd[:, 0:1], ALU.is_le, ALU.mult)
            vec.tensor_copy(c16[:, 3:4], ct2[:, NT - 1:NT])
            vec.tensor_copy(c16[:, 4:5], ccb[:, NT - 1:NT])
            vec.tensor_scalar(ct1[:, 0:NT], iot[0:16, 0:NT], la16[:, 0:1],
                              b16k[:, KT + k:KT + k + 1], ALU.mult, ALU.add)
            vdone()
            sca.wait_ge(s_ve, cnt.val(s_ve))
            si = sca.activation(ccb[:, 0:NT], ct1[:, 0:NT], ACTF.Exp)
            vec.wait_ge(s_se, cnt.val(s_se))
            vec.scalar_tensor_tensor(crd[:, 0:NT], ccb[:, 0:NT], C_RESC,
                                     crd[:, 0:NT], ALU.mult, ALU.mult)
            vec.scalar_tensor_tensor(crd[:, 0:NT], cse[:, 0:NT], la16[:, 2:3],
                                     crd[:, 0:NT], ALU.mult, ALU.add)
            vec.tensor_scalar(crd[:, 0:NT], crd[:, 0:NT], 1e-6, None, ALU.max)
            vec.tensor_scalar(crd[:, 0:NT], crd[:, 0:NT], 3.3e38, None, ALU.min)
            vdone()
            sca.wait_ge(s_ve, cnt.val(s_ve))
            sca.activation(ct1[:, 0:NT], crd[:, 0:NT], ACTF.Ln, scale=WKV_SC * WKV_SC)
            si = sca.activation(crd[:, 0:NT], ct1[:, 0:NT], ACTF.Exp, scale=-1.0,
                                bias=c16[:, 7:8])
            vec.wait_ge(s_ld, ld_ekv)
            sca.wait_ge(s_ld, ld_ekv)
            inv_alpha_tile(k, sc, tY)
            vec.scalar_tensor_tensor(sc[:, 0:DC * NT], tZ[:, 0:DC * NT], C1,
                                     sc[:, 0:DC * NT], ALU.mult, ALU.mult)
            vec.wait_ge(s_pe, cnt.val(s_pe))

            def wkv_post(c):
                slq = slice(c * NT, (c + 1) * NT)
                vec.wait_ge(s_se, se_mark[c])
                vec.scalar_tensor_tensor(sc[:, slq], tY[:, slq], C_RESC,
                                         sc[:, slq], ALU.mult, ALU.mult)
                vec.scalar_tensor_tensor(ps2[:, 0:NT], tY[:, slq], 0.0,
                                         ps2[:, 0:NT], ALU.is_gt, ALU.mult)
                vec.scalar_tensor_tensor(sc[:, slq], ps2[:, 0:NT], -FMAX,
                                         sc[:, slq], ALU.mult, ALU.add)
                vec.scalar_tensor_tensor(sc[:, slq], tZ[:, slq],
                                         patB[:, QEU + c:QEU + c + 1],
                                         sc[:, slq], ALU.mult, ALU.add)

            se_mark = {}
            for c in range(DC):
                slq = slice(c * NT, (c + 1) * NT)
                u1 = tY[:, ((c + 1) % DC) * NT:((c + 1) % DC) * NT + NT]
                u2 = tY[:, ((c + 2) % DC) * NT:((c + 2) % DC) * NT + NT]
                u4 = tY[:, ((c + 3) % DC) * NT:((c + 3) % DC) * NT + NT]
                u3 = ps2[:, 0:NT]
                vec.tensor_tensor_scan(tY[:, slq], sc[:, slq], zerosb[:, 0:NT],
                                       lastP[:, c:c + 1], ALU.add, ALU.add)
                vec.tensor_tensor(u1, sc[:, slq], tY[:, slq], ALU.max)
                vec.scalar_tensor_tensor(u1, u1, THF, bmt[:, 0:NT],
                                         ALU.is_ge, ALU.mult)
                vec.tensor_tensor_scan(u2, u1, zerosb[:, 0:NT],
                                       chainP[:, c:c + 1], ALU.max, ALU.max)
                if c >= 1:
                    wkv_post(c - 1)
                vec.tensor_tensor(u3, sc[:, slq], tY[:, slq], ALU.min)
                vec.scalar_tensor_tensor(u3, u3, -THF, bmt[:, 0:NT],
                                         ALU.is_le, ALU.mult)
                vec.tensor_tensor_scan(u4, u3, zerosb[:, 0:NT],
                                       chainN[:, c:c + 1], ALU.max, ALU.max)
                vec.tensor_tensor(u3[:, 1:NT], u4[:, 0:NT - 1],
                                  u2[:, 0:NT - 1], ALU.is_gt)
                vec.tensor_tensor(u3[:, 0:1], chainN[:, c:c + 1],
                                  chainP[:, c:c + 1], ALU.is_gt)
                vec.tensor_tensor(u1[:, 1:NT], u2[:, 0:NT - 1],
                                  u4[:, 0:NT - 1], ALU.max)
                vec.tensor_tensor(u1[:, 0:1], chainP[:, c:c + 1],
                                  chainN[:, c:c + 1], ALU.max)
                vec.tensor_copy(chainP[:, c:c + 1], u2[:, NT - 1:NT])
                vec.tensor_copy(chainN[:, c:c + 1], u4[:, NT - 1:NT])
                vec.tensor_copy(sc[:, c * NT + 1:(c + 1) * NT],
                                tY[:, c * NT:c * NT + NT - 1])
                vec.tensor_copy(sc[:, c * NT:c * NT + 1], lastP[:, c:c + 1])
                vec.tensor_copy(lastP[:, c:c + 1], tY[:, c * NT + NT - 1:c * NT + NT])
                vec.scalar_tensor_tensor(sc[:, slq], u1, 0.0, sc[:, slq],
                                         ALU.is_le, ALU.mult)
                sca.wait_ge(s_ve, cnt.val(s_ve))
                si = sca.activation(tY[:, slq], iot[:, 0:NT],
                                    ACTF.Exp, scale=patB[:, QLA + c:QLA + c + 1],
                                    bias=bpow[:, c * KT + k:c * KT + k + 1])
                se_mark[c] = cnt.val(s_se)
            wkv_post(DC - 1)
            vdone()
            if debug:
                sync.wait_ge(s_ve, cnt.val(s_ve))
                sync.dma_start(out=hv(dbg["d_num"], k * NT, NT),
                               in_=sv(sc, NT, 0, DC, 0, NT)).then_inc(s_st, 16)
                cnt.inc(s_st, 16)
                vec.wait_ge(s_st, cnt.val(s_st))
            vec.wait_ge(s_ld, cnt.val(s_ld))
            ten.wait_ge(s_se, cnt.val(s_se))
            ten.wait_ge(s_ve, cnt.val(s_ve))
            banks4b = [ps2[:, 0:NT], ps3[:, 0:NT],
                       psBig[:, 0:NT], psBig[:, NT:2 * NT]]
            bdone = []
            for c in range(DC):
                bank = banks4b[c % 4]
                if c >= 4:
                    ten.wait_ge(s_ve, bdone[c - 4])
                mi = ten.matmul(bank, selT[:, c * 128:(c + 1) * 128],
                                crd[:, 0:NT], start=True, stop=True)
                mi.then_inc(s_pe, 1); cnt.inc(s_pe, 1)
                vec.wait_ge(s_pe, cnt.val(s_pe))
                slq = slice(c * NT, (c + 1) * NT)
                vec.scalar_tensor_tensor(sc[:, slq], sc[:, slq], WKV_SC,
                                         bank, ALU.mult, ALU.mult)
                vec.tensor_scalar(sc[:, slq], sc[:, slq], WKV_CL, -WKV_CL,
                                  ALU.min, ALU.max)
                vec.tensor_tensor(tY[:, slq], sc[:, slq], tB16b[:, slq], ALU.mult)
                vec.tensor_scalar(tY[:, slq], tY[:, slq], RW_SC, 2.656e35,
                                  ALU.mult, ALU.min)
                vec.tensor_scalar(tB16[:, slq], tY[:, slq], -2.656e35, None, ALU.max)
                bdone.append(cnt.val(s_ve))
            if debug:
                sync.wait_ge(s_ve, cnt.val(s_ve))
                sync.dma_start(out=hv(dbg["d_wkv"], k * NT, NT),
                               in_=sv(sc, NT, 0, DC, 0, NT)).then_inc(s_st, 16)
                cnt.inc(s_st, 16)
                sync.dma_start(out=dbg["d_rdn"][:, k * NT:(k + 1) * NT],
                               in_=crd[:, 0:NT]).then_inc(s_st, 16)
                cnt.inc(s_st, 16)
                vec.wait_ge(s_st, cnt.val(s_st))
                sca.wait_ge(s_st, cnt.val(s_st))
            def evac_att(e, bank):
                vi = vec.tensor_scalar(tZ[:, e * NT:(e + 1) * NT], bank[:, 0:NT],
                                       2.656e36, -2.656e36, ALU.min, ALU.max)
            proj_l1(0, tB16, evac_att, eng=vec)
            if debug:
                vdone()
                sync.wait_ge(s_ve, cnt.val(s_ve))
                sync.dma_start(out=hv(dbg["d_att"], k * NT, NT),
                               in_=sv(tZ, NT, 0, DC, 0, NT)).then_inc(s_st, 16)
                cnt.inc(s_st, 16)
                vec.wait_ge(s_st, cnt.val(s_st))
            vec.scalar_tensor_tensor(tY[:, 0:DC * NT], tZ[:, 0:DC * NT], 128.0,
                                     tX[:, 0:DC * NT], ALU.mult, ALU.add)
            vdone()
            ln_stats_norm(NT, tY, tZ, armored=True)
            vdone()
            sync.wait_ge(s_ve, cnt.val(s_ve))
            sync.dma_start(out=hv(x4_s, k * NT, NT),
                           in_=sv(tZ, NT, 0, DC, 0, NT)).then_inc(s_st, 16)
            cnt.inc(s_st, 16)
            tbar()

        if upto == "x4":
            _finalize(nc, sync, cnt, s_st, dbg, debug,
                      [("d_x2", x2_s, 0), ("d_x3", x3_s, 0), ("d_x4", x4_s, 0)])
            return nc

        for oc in range(DC):
            vec.wait_ge(s_st, cnt.val(s_st))
            sca.wait_ge(s_st, cnt.val(s_st))
            sync.wait_ge(s_ld, cnt.val(s_ld))
            sync.dma_start(out=tX[:, 0:D], in_=l1fw[oc * 128:(oc + 1) * 128, :]).then_inc(s_ld, 16)
            cnt.inc(s_ld, 16)
            vec.wait_ge(s_ld, cnt.val(s_ld))
            vec.tensor_reduce(scr8[:, 0:1], tX[:, 0:D], mybir.AxisListType.X,
                              ALU.add, apply_absolute_value=True)
            vec.tensor_scalar(scr8[:, 0:1], scr8[:, 0:1], 1.0 / D, None, ALU.mult)
            vec.tensor_scalar(scr8[:, 0:1], scr8[:, 0:1], 1e-5, None, ALU.max)
            vdone()
            sca.wait_ge(s_ve, cnt.val(s_ve))
            sca.activation(scr8[:, 1:2], scr8[:, 0:1], ACTF.Ln)
            si = sca.activation(scr8[:, 1:2], scr8[:, 1:2], ACTF.Exp, scale=-1.0)
            vec.wait_ge(s_se, cnt.val(s_se))
            vec.tensor_scalar(tY[:, 0:D], tX[:, 0:D], scr8[:, 1:2], None, ALU.mult)
            vec.tensor_scalar(tZ[:, 0:D], tY[:, 0:D], 0.5, 1.0, ALU.is_gt, ALU.mult)
            vec.tensor_scalar(tX[:, 0:D], tY[:, 0:D], -0.5, 1.0, ALU.is_lt, ALU.mult)
            vec.tensor_tensor(tZ[:, 0:D], tZ[:, 0:D], tX[:, 0:D], ALU.subtract)
            vec.tensor_scalar(tB16[:, 0:D], tZ[:, 0:D], scr8[:, 0:1], None, ALU.mult)
            vdone()
            ten.wait_ge(s_ve, cnt.val(s_ve))
            sca.wait_ge(s_h, cnt.val(s_h))
            for ic in range(DC):
                ten.transpose(psT[:, 0:128],
                              tB16[:, ic * 128:(ic + 1) * 128], identb[:]).then_inc(s_pe, 1)
                cnt.inc(s_pe, 1)
                sca.wait_ge(s_pe, cnt.val(s_pe))
                si = sca.activation(tB16b[:, ic * 128:(ic + 1) * 128], psT[:, 0:128],
                                    ACTF.Copy)
                ten.wait_ge(s_se, cnt.val(s_se))
            sync.wait_ge(s_se, cnt.val(s_se))
            sync.dma_start(out=wqT_s.rearrange("(c p) e -> p c e", p=128)[:, :, oc * 128:(oc + 1) * 128],
                           in_=tB16b[:, 0:D].rearrange("p (c e) -> p c e", e=128)).then_inc(s_h, 16)
            cnt.inc(s_h, 16)
            tbar()
        sync.dma_start(out=arH[:, 0:8 * D].rearrange("p (c e) -> p c e", e=D),
                       in_=wqT_s.rearrange("(c p) e -> p c e", p=128)).then_inc(s_ld, 16)
        cnt.inc(s_ld, 16)
        wload(arH, 8 * D, hadc)
        bar()

        for e in (vec, sca, ten):
            e.wait_ge(s_ld, cnt.val(s_ld))
        for k in range(KT):
            vec.wait_ge(s_st, cnt.val(s_st))
            sca.wait_ge(s_st, cnt.val(s_st))
            sync.dma_start(out=sv(tX, NT, 0, DC, 0, NT),
                           in_=hv(x4_s, k * NT, NT)).then_inc(s_ld, 16)
            cnt.inc(s_ld, 16)
            vec.wait_ge(s_ld, cnt.val(s_ld))
            vec.tensor_copy(tB16[:, 0:DC * NT], tX[:, 0:DC * NT])
            vdone()
            ten.wait_ge(s_ve, cnt.val(s_ve))

            def p12_quant(tg):
                slq = slice(tg * 1024, tg * 1024 + 1024)
                vec.wait_ge(s_se, se_done[tg])
                vec.tensor_reduce(scr8[:, 0:1], tY[:, slq], mybir.AxisListType.X,
                                  ALU.max, apply_absolute_value=True)
                vec.tensor_scalar(scr8[:, 0:1], scr8[:, 0:1], 1e-5, None, ALU.max)
                vec.tensor_scalar(scr8[:, 1:2], scr8[:, 0:1], 1.0 / 7.0, None, ALU.mult)
                vec.reciprocal(scr8[:, 2:3], scr8[:, 1:2])
                vec.tensor_scalar(tZ[:, slq], tY[:, slq], scr8[:, 2:3], MAGIC,
                                  ALU.mult, ALU.add)
                vec.tensor_scalar(tZ[:, slq], tZ[:, slq], MAGIC, None, ALU.subtract)
                vec.tensor_scalar(tZ[:, slq], tZ[:, slq], -8.0, 7.0, ALU.max, ALU.min)
                vec.tensor_scalar(tB16b[:, slq], tZ[:, slq], scr8[:, 1:2], None,
                                  ALU.mult)
                return cnt.val(s_ve)

            def p12_transpose(tg):
                ten.wait_ge(s_ve, q_mark[tg])
                for oc in range(DC):
                    mi = ten.transpose(psT[:, (oc % 4) * 128:(oc % 4) * 128 + 128],
                                       tB16b[:, tg * 1024 + oc * 128: tg * 1024 + oc * 128 + 128],
                                       identb[:])
                    if oc % 4 == 3:
                        mi.then_inc(s_pe, 1); cnt.inc(s_pe, 1)
                        sca.wait_ge(s_pe, cnt.val(s_pe))
                        for j in range(4):
                            ocx = oc - 3 + j
                            si = sca.activation(
                                tB16[:, ocx * NT + tg * 128: ocx * NT + tg * 128 + 128],
                                psT[:, j * 128:j * 128 + 128], ACTF.Copy)
                        ten.wait_ge(s_se, cnt.val(s_se))

            se_done = []; q_mark = {}
            for tg in range(NT // 128):
                if tg >= 1:
                    p12_transpose(tg - 1)
                    ten.wait_ge(s_se, se_done[tg - 1])
                else:
                    ten.wait_ge(s_se, cnt.val(s_se))
                for nh in range(2):
                    bank = psAB[nh]
                    for kc in range(DC):
                        mi = ten.matmul(bank[:, 0:NT],
                                        tB16[:, kc * NT + tg * 128: kc * NT + tg * 128 + 128],
                                        arH[:, (8 + kc) * D + nh * NT: (8 + kc) * D + nh * NT + NT],
                                        start=(kc == 0), stop=(kc == DC - 1))
                mi.then_inc(s_pe, 1); cnt.inc(s_pe, 1)
                sca.wait_ge(s_pe, cnt.val(s_pe))
                sca.activation(tY[:, tg * 1024:tg * 1024 + NT], ps0[:, 0:NT], ACTF.Copy)
                si = sca.activation(tY[:, tg * 1024 + NT:tg * 1024 + 2 * NT],
                                    ps1[:, 0:NT], ACTF.Copy)
                se_done.append(cnt.val(s_se))
                q_mark[tg] = p12_quant(tg)
            p12_transpose(NT // 128 - 1)
            def evac_f(e, bank):
                si = sca.activation(tZ[:, e * NT:(e + 1) * NT], bank[:, 0:NT],
                                    ACTF.Identity)
            se_b = cnt.val(s_se)
            for e in range(DC):
                bank = psAB[e % 2]
                if e >= 2:
                    ten.wait_ge(s_se, se_b + e - 1)
                else:
                    ten.wait_ge(s_se, se_b)
                for kc in range(DC):
                    mi = ten.matmul(bank[:, 0:NT],
                                    arH[:, kc * D + e * 128: kc * D + e * 128 + 128],
                                    tB16[:, kc * NT: kc * NT + NT],
                                    start=(kc == 0), stop=(kc == DC - 1))
                mi.then_inc(s_pe, 1); cnt.inc(s_pe, 1)
                sca.wait_ge(s_pe, cnt.val(s_pe))
                evac_f(e, bank)
            vec.wait_ge(s_se, cnt.val(s_se))
            vec.tensor_tensor(tY[:, 0:DC * NT], tZ[:, 0:DC * NT], tX[:, 0:DC * NT],
                              ALU.add)
            vdone()
            ln_stats_norm(NT, tY, tZ)
            vdone()
            sync.wait_ge(s_ve, cnt.val(s_ve))
            sync.dma_start(out=hv(out_t, k * NT, NT),
                           in_=sv(tZ, NT, 0, DC, 0, NT)).then_inc(s_st, 16)
            cnt.inc(s_st, 16)
            tbar()

        _finalize(nc, sync, cnt, s_st, dbg, debug,
                  [("d_x2", x2_s, 0), ("d_x3", x3_s, 0), ("d_x4", x4_s, 0)])
    return nc


def _finalize(nc, sync, cnt, s_st, dbg, debug, dumps):
    if debug:
        for nm, src, _ in dumps:
            if nm in dbg:
                sync.dma_start(out=dbg[nm], in_=src).then_inc(s_st, 16)
                cnt.inc(s_st, 16)
    sync.wait_ge(s_st, cnt.val(s_st))
    nc.all_engine_barrier()


def _hadamard_const():
    h = np.array([[1.0]], dtype=np.float64)
    while h.shape[0] < D:
        h = np.block([[h, h], [h, -h]])
    return np.ascontiguousarray((h / np.sqrt(D)).astype(np.float32))


def _shard_inputs(inputs):
    f = np.float32
    xs = np.asarray(inputs["L0_attn_hidden_states"], f)
    tr = lambda a: np.ascontiguousarray(np.asarray(a, f).T)
    cc = lambda a: np.ascontiguousarray(np.asarray(a, f))
    common = dict(
        wkT=tr(inputs["L0_attn_Wk"]), wvT=tr(inputs["L0_attn_Wv"]),
        wkaT=tr(inputs["L0_attn_Wka"]),
        w1=cc(inputs["L0_ffn_W1"]), w2=cc(inputs["L0_ffn_W2"]),
        wkeT=tr(inputs["L1_attn_W_key"]), wvaT=tr(inputs["L1_attn_W_value"]),
        wreT=tr(inputs["L1_attn_W_receptance"]), woT=tr(inputs["L1_attn_W_output"]),
        l1fw=cc(inputs["L1_ffn_weight"]),
        bk=cc(inputs["L0_attn_bk"]), bv=cc(inputs["L0_attn_bv"]),
        bka=cc(inputs["L0_attn_bka"]),
        gamma=cc(inputs["L0_attn_gamma"]), beta=cc(inputs["L0_attn_beta"]),
        b1=cc(inputs["L0_ffn_b1"]), b2=cc(inputs["L0_ffn_b2"]),
        theta=cc(inputs["L0_ffn_quantum_phases"]),
        qwr=cc(inputs["L0_ffn_quantum_weights_real"]),
        qwi=cc(inputs["L0_ffn_quantum_weights_imag"]),
        mixk=cc(np.asarray(inputs["L1_attn_mix_k"], f).ravel()),
        mixv=cc(np.asarray(inputs["L1_attn_mix_v"], f).ravel()),
        mixr=cc(np.asarray(inputs["L1_attn_mix_r"], f).ravel()),
        td=cc(inputs["L1_attn_time_decay"]), tf=cc(inputs["L1_attn_time_first"]),
        hadc=_hadamard_const(),
    )
    in_maps = []
    for c in range(8):
        b, half = c // 2, c % 2
        sel = np.zeros((1, 8), f)
        if half == 1:
            sel[0, c - 1] = 1.0
        m = dict(common)
        m["x_in"] = np.ascontiguousarray(xs[b, half * T:(half + 1) * T, :].T)
        m["selrow"] = sel
        m["t0c"] = np.array([[float(half * T)]], f)
        in_maps.append(m)
    return in_maps


def kernel(**inputs):
    in_maps = _shard_inputs(inputs)
    nc = build_program(debug=False, upto="out")
    res = run_bass_kernel_spmd(nc, in_maps, core_ids=list(range(8)))
    out = np.empty((B, S, D), np.float32)
    for c in range(8):
        b, half = c // 2, c % 2
        out[b, half * T:(half + 1) * T, :] = res.results[c]["out"].T
    return out


# version 16 (final)
# speedup vs baseline: 1.0265x; 1.0019x over previous
import numpy as np
from contextlib import ExitStack

import concourse.bass as bass
import concourse.mybir as mybir
from concourse.bass_utils import run_bass_kernel_spmd

F32 = mybir.dt.float32
BF16 = mybir.dt.bfloat16
U8 = mybir.dt.uint8
ALU = mybir.AluOpType
ACTF = mybir.ActivationFunctionType

B, S, D = 4, 8192, 1024
T = 4096
DC = 8
H = 16
DF, DFC = 4096, 32
NT = 512
KT = T // NT

C_S = 9.94969
C_ES = float(np.exp(np.float64(np.float32(C_S))))
LN4 = 1.3862943611198906
FMAX = 3.4028235e38
MAGIC = 12582912.0
INV2PI = 0.15915494309189535
TWOPI = 6.283185307179586
PIH = 1.5707963267948966
LN_EPS = 1e-5
AG_W = 3104
EXP_INF = 88.72284
EXP_CLAMP = 88.7228
BIG_C = 16384.0
THF = 1.8443e19
C1 = float(0.25 * 2.0 ** -64)
C_RESC = float(np.float32(C_ES) * np.float32(2.0 ** 64))
WKV_SC = float(2.0 ** -32)
WKV_CL = float(np.float32(3.3e38) * 2.0 ** -32)
RW_SC = float(2.0 ** 25)
LN2_64 = float(np.float32(64.0 * np.log(2.0)))
SC70 = float(2.0 ** -70)
P70 = float(2.0 ** 70)
P76 = float(2.0 ** 76)
SC64M = float(2.0 ** -64)
TQ_S = float(np.float32(FMAX) * 2.0 ** -140)
EPS_S = float(np.float32(LN_EPS) * 2.0 ** -64)
LNB_A = float(-32.0 * np.log(2.0))
LNB_B = float(-102.0 * np.log(2.0))
B1C = float(2.0 ** 58)
TBL = float(1e19 * 2.0 ** -140)
C17 = float(np.float32(3.3e38) * 2.0 ** -70)
CY_S = 1.0e6
VAR_TH = 1.0


class Cnt:
    def __init__(self):
        self.c = {}

    def inc(self, s, n):
        self.c[s.name] = self.c.get(s.name, 0) + n
        return self.c[s.name]

    def val(self, s):
        return self.c.get(s.name, 0)


_DRAIN_OPS = {
    "memset", "memzero", "tensor_copy", "tensor_tensor", "tensor_add",
    "tensor_sub", "tensor_mul", "tensor_max", "tensor_scalar",
    "tensor_single_scalar", "tensor_scalar_add", "tensor_scalar_sub",
    "tensor_scalar_mul", "tensor_scalar_max", "tensor_scalar_min",
    "tensor_relu", "tensor_reduce", "reduce_max", "reduce_sum",
    "scalar_tensor_tensor", "tensor_tensor_scan", "select", "copy_predicated",
    "reciprocal", "transpose", "activation", "sign", "square", "sqrt",
    "mul", "add", "copy", "iota", "affine_select",
}


class _Draining:
    def __init__(self, eng):
        self._e = eng

    def __getattr__(self, name):
        f = getattr(self._e, name)
        if name in _DRAIN_OPS:
            def g(*a, **k):
                self._e.drain()
                return f(*a, **k)
            return g
        return f


def _ap_box(ap):
    off = ap.offset
    span = 0
    for stride, num in ap.ap:
        if num > 1:
            span += (num - 1) * abs(stride)
    return (ap.tensor.name, off, off + span)


def _boxes_overlap(a, b):
    return a[0] == b[0] and a[1] <= b[2] and b[1] <= a[2]


class _Chained:

    def __init__(self, eng, sem, cnt):
        self._e = eng
        self._sem = sem
        self._cnt = cnt
        self._hist = []
        self._waited = 0

    def __getattr__(self, name):
        f = getattr(self._e, name)
        if name not in _DRAIN_OPS:
            return f

        def g(*a, **k):
            aps = [x for x in a if isinstance(x, bass.AP)]
            aps += [x for x in k.values() if isinstance(x, bass.AP)]
            wbox = _ap_box(aps[0]) if aps else None
            rboxes = [_ap_box(x) for x in aps[1:]]
            need = 0
            for seq, kind, box in self._hist:
                if seq <= self._waited:
                    continue
                if kind == "W":
                    if (wbox and _boxes_overlap(box, wbox)) or \
                       any(_boxes_overlap(box, rb) for rb in rboxes):
                        need = max(need, seq)
                elif wbox and _boxes_overlap(box, wbox):
                    need = max(need, seq)
            if need > self._waited:
                self._e.wait_ge(self._sem, need)
                self._waited = need
                self._hist = [h for h in self._hist if h[0] > need]
            res = f(*a, **k)
            res.then_inc(self._sem, 1)
            seq = self._cnt.inc(self._sem, 1)
            if wbox:
                self._hist.append((seq, "W", wbox))
            for rb in rboxes:
                self._hist.append((seq, "R", rb))
            if len(self._hist) > 256:
                self._hist = self._hist[-192:]
            return res
        return g


def build_program(debug=False, upto="out"):
    nc = bass.Bass("TRN2", target_bir_lowering=False)
    es = ExitStack()
    cnt = Cnt()
    sync, ten = nc.sync, nc.tensor
    gp = _Draining(nc.gpsimd)

    def din(name, shape):
        return nc.dram_tensor(name, shape, F32, kind="ExternalInput").ap()

    x_in = din("x_in", [D, T])
    wkT = din("wkT", [D, D]); wvT = din("wvT", [D, D]); wkaT = din("wkaT", [D, D])
    w1 = din("w1", [D, DF]); w2 = din("w2", [DF, D])
    wkeT = din("wkeT", [D, D]); wvaT = din("wvaT", [D, D])
    wreT = din("wreT", [D, D]); woT = din("woT", [D, D])
    l1fw = din("l1fw", [D, D])
    v_bk = din("bk", [D]); v_bv = din("bv", [D]); v_bka = din("bka", [D])
    v_g = din("gamma", [D]); v_be = din("beta", [D])
    v_b1 = din("b1", [DF]); v_b2 = din("b2", [D])
    v_th = din("theta", [DF]); v_qr = din("qwr", [DF]); v_qi = din("qwi", [DF])
    v_mk = din("mixk", [D]); v_mv = din("mixv", [D]); v_mr = din("mixr", [D])
    v_td = din("td", [H]); v_tf = din("tf", [H])
    selrow = din("selrow", [1, 8])
    t0c = din("t0c", [1, 1])
    hadc = din("hadc", [D, D])

    out_t = nc.dram_tensor("out", [D, T], F32, kind="ExternalOutput").ap()

    BFD = mybir.dt.bfloat16
    keys_s = nc.dram_tensor("keys_s", [D, T], F32).ap()
    kw_s = nc.dram_tensor("kw_s", [D, T], BFD).ap()
    pool_s = nc.dram_tensor("pool_s", [D, T], BFD).ap()
    x2_s = nc.dram_tensor("x2_s", [D, T], F32).ap()
    vals_s = nc.dram_tensor("vals_s", [D, T], F32).ap()
    x3_s = nc.dram_tensor("x3_s", [D, T], F32).ap()
    x4_s = nc.dram_tensor("x4_s", [D, T], F32).ap()
    ekv_s = nc.dram_tensor("ekv_s", [D, T], F32).ap()
    r_s = nc.dram_tensor("r_s", [D, T], BFD).ap()
    h_s = nc.dram_tensor("h_s", [D, T], BFD).ap()
    w1b_s = nc.dram_tensor("w1b_s", [8, 128, DF], BFD).ap()
    se_s = nc.dram_tensor("se_s", [H, T], F32).ap()
    w2b_s = nc.dram_tensor("w2b_s", [DC, 128, DF], BFD).ap()
    wqT_s = nc.dram_tensor("wqT_s", [D, D], BFD).ap()
    ag_in = [nc.dram_tensor(f"ag_in{i}", [1, AG_W], F32).ap() for i in range(4)]
    ag_out = [nc.dram_tensor(f"ag_out{i}", [8, AG_W], F32).ap() for i in range(4)]

    dbg = {}
    if debug:
        for nm in ["d_x2", "d_x3", "d_x4", "d_keys", "d_ekv", "d_num", "d_wkv",
                   "d_att"]:
            dbg[nm] = nc.dram_tensor(nm, [D, T], F32, kind="ExternalOutput").ap()
        dbg["d_rdn"] = nc.dram_tensor("d_rdn", [16, T], F32, kind="ExternalOutput").ap()

    def hv(hbm, t0, n, c0=0, ncs=DC):
        return hbm.rearrange("(c p) t -> p c t", p=128)[:, c0:c0 + ncs, t0:t0 + n]

    def sv(tile, per_c, c0, ncs, j0, n):
        return tile[:].rearrange("p (c n) -> p c n", n=per_c)[:, c0:c0 + ncs, j0:j0 + n]

    def sbuf(name, shape, dtype):
        return es.enter_context(nc.sbuf_tensor(name, shape, dtype))

    AR = 3 * DC * D
    arH = sbuf("arH", [128, AR], BF16)
    arK = sbuf("arK", [128, AR], BF16)
    tX = sbuf("tX", [128, DC * NT], F32)
    tY = sbuf("tY", [128, DC * NT], F32)
    tZ = sbuf("tZ", [128, DC * NT], F32)
    tB16 = sbuf("tB16", [128, DC * NT], BF16)
    tB16b = sbuf("tB16b", [128, DC * NT], BF16)
    sc = sbuf("sc", [128, DC * NT], F32)
    svA = sbuf("svA", [128, NT], F32)
    svB = sbuf("svB", [128, NT], F32)
    mu8 = sbuf("mu8", [128, DC * NT // 2], U8)

    pat = sbuf("pat", [128, 64], F32)
    PG, PBE, PBK, PBV, PBKA = 0, 8, 16, 24, 32
    PMK, PMV, PMR = 40, 48, 56
    patB = sbuf("patB", [128, 64], F32)
    QLA, QNLA, QEU, QB2 = 0, 8, 16, 24
    patDF = sbuf("patDF", [128, 4 * DFC], F32)
    PA, PB, PB1C, PB1S = 0, DFC, 2 * DFC, 3 * DFC
    patA1 = sbuf("patA1", [128, 2 * DFC], F32)
    binv = sbuf("binv", [128, DC * KT], F32)
    bpow = sbuf("bpow", [128, DC * KT], F32)
    b16k = sbuf("b16k", [16, 2 * KT], F32)
    la16 = sbuf("la16", [16, 8], F32)
    iot = sbuf("iot", [128, NT], F32)
    iotk = sbuf("iotk", [128, KT], F32)
    onesb = sbuf("onesb", [128, 1], BF16)
    ones2d = sbuf("ones2d", [128, 128], F32)
    zerosb = sbuf("zerosb", [128, NT], BF16)
    selc = sbuf("selc", [128, DC * 16], F32)
    selT = sbuf("selT", [16, D], F32)
    ident = sbuf("ident", [128, 128], F32)
    identb = sbuf("identb", [128, 128], BF16)
    onehot8 = sbuf("onehot8", [8, 1], F32)
    carry = sbuf("carry", [128, 48], F32)
    bmt = sbuf("bmt", [128, NT], F32)
    bigk = sbuf("bigk", [128, 8], F32)
    maxP = sbuf("maxP", [128, 8], F32)
    maxN = sbuf("maxN", [128, 8], F32)
    chainP = sbuf("chainP", [128, 8], F32)
    chainN = sbuf("chainN", [128, 8], F32)
    c16 = sbuf("c16", [16, 8], F32)
    lastK = sbuf("lastK", [128, 8], F32)
    lastP = sbuf("lastP", [128, 8], F32)
    lastA = sbuf("lastA", [128, 8], F32)
    sumkw = sbuf("sumkw", [128, 8], F32)
    scr8 = sbuf("scr8", [128, 8], F32)
    scr8b = sbuf("scr8b", [128, 8], F32)
    hrow = sbuf("hrow", [8, 128], F32)
    t0bc = sbuf("t0bc", [128, 1], F32)
    c_eps = sbuf("c_eps", [128, 1], F32)
    c_epsS = sbuf("c_epsS", [128, 1], F32)
    c_lnA = sbuf("c_lnA", [128, 1], F32)
    c_lnB = sbuf("c_lnB", [128, 1], F32)
    cse = sbuf("cse", [16, NT], F32)
    ccb = sbuf("ccb", [16, NT], F32)
    ct1 = sbuf("ct1", [16, NT], F32)
    ct2 = sbuf("ct2", [16, NT], F32)
    crd = sbuf("crd", [16, NT], F32)
    mu16 = sbuf("mu16", [16, NT], U8)

    ps0 = es.enter_context(nc.psum_tensor("ps0", [128, NT], F32))
    ps1 = es.enter_context(nc.psum_tensor("ps1", [128, NT], F32))
    ps2 = es.enter_context(nc.psum_tensor("ps2", [128, NT], F32))
    ps3 = es.enter_context(nc.psum_tensor("ps3", [128, NT], F32))
    psBig = es.enter_context(nc.psum_tensor("psBig", [128, 2 * NT], F32))
    psT = es.enter_context(nc.psum_tensor("psT", [128, NT], BF16))
    psAB = [ps0, ps1]

    s_ld = es.enter_context(nc.semaphore("s_ld"))
    s_gd = es.enter_context(nc.semaphore("s_gd"))
    s_st = es.enter_context(nc.semaphore("s_st"))
    s_ag = es.enter_context(nc.semaphore("s_ag"))
    s_pe = es.enter_context(nc.semaphore("s_pe"))
    s_ve = es.enter_context(nc.semaphore("s_ve"))
    s_se = es.enter_context(nc.semaphore("s_se"))
    s_gp = es.enter_context(nc.semaphore("s_gp"))
    s_h = es.enter_context(nc.semaphore("s_h"))
    s_pc = es.enter_context(nc.semaphore("s_pc"))
    s_sl0 = es.enter_context(nc.semaphore("s_sl0"))
    s_sl1 = es.enter_context(nc.semaphore("s_sl1"))
    s_la = es.enter_context(nc.semaphore("s_la"))

    vec = _Chained(nc.vector, s_ve, cnt)
    sca = _Chained(nc.scalar, s_se, cnt)

    def bar():
        for s in (s_ld, s_st, s_gd, s_h, s_pc, s_sl0, s_sl1, s_la):
            if cnt.val(s):
                sync.wait_ge(s, cnt.val(s))
        nc.all_engine_barrier()
        for w in (vec, sca):
            w._waited = cnt.val(w._sem)
            w._hist = []

    def tbar():
        sync.wait_ge(s_ve, cnt.val(s_ve))
        sync.wait_ge(s_se, cnt.val(s_se))
        sync.wait_ge(s_pe, cnt.val(s_pe))
        for s in (s_st, s_h):
            if cnt.val(s):
                sync.wait_ge(s, cnt.val(s))

    def vdone():
        pass

    W = {}

    with es:
        def load_pat(dst, col0, v, ncols):
            sync.dma_start(out=dst[:, col0:col0 + ncols],
                           in_=v.rearrange("(c p) -> p c", p=128),
                           allow_slow_non_contiguous=True).then_inc(s_ld, 16)
            cnt.inc(s_ld, 16)

        load_pat(pat, PG, v_g, 8); load_pat(pat, PBE, v_be, 8)
        load_pat(pat, PBK, v_bk, 8); load_pat(pat, PBV, v_bv, 8)
        load_pat(pat, PBKA, v_bka, 8)
        load_pat(pat, PMK, v_mk, 8); load_pat(pat, PMV, v_mv, 8)
        load_pat(pat, PMR, v_mr, 8)
        load_pat(patB, QB2, v_b2, 8)
        load_pat(patA1, 0, v_b1, DFC)
        load_pat(patDF, PA, v_th, DFC)
        load_pat(patDF, PB, v_qr, DFC)
        load_pat(patDF, PB1C, v_qi, DFC)
        sync.dma_start(out=la16[:, 6:7], in_=v_td[:, None]).then_inc(s_ld, 16); cnt.inc(s_ld, 16)
        sync.dma_start(out=la16[:, 7:8], in_=v_tf[:, None]).then_inc(s_ld, 16); cnt.inc(s_ld, 16)
        sync.dma_start(out=onehot8[:], in_=selrow.rearrange("o r -> r o")).then_inc(s_ld, 16)
        cnt.inc(s_ld, 16)
        sync.dma_start(out=t0bc[:], in_=bass.AP(t0c.tensor, 0, [[0, 128], [1, 1]]),
                       allow_slow_non_contiguous=True).then_inc(s_ld, 16)
        cnt.inc(s_ld, 16)

        def wload(dst, col0, srcT, cols=D):
            gp.dma_start(
                out=dst[:].rearrange("p (c e) -> p c e", e=cols)[:, col0 // cols:(col0 // cols) + srcT.shape[0] // 128, :],
                in_=srcT.rearrange("(c p) e -> p c e", p=128)).then_inc(s_gd, 16)
            cnt.inc(s_gd, 16)

        wload(arK, 0, wkT)
        wload(arK, 8 * D, wkaT)
        gp.memset(zerosb[:], 0.0)
        gp.memset(c_eps[:], LN_EPS)
        gp.memset(c_epsS[:], EPS_S)
        gp.memset(c_lnA[:], LNB_A)
        gp.memset(c_lnB[:], LNB_B)
        gp.memset(ones2d[:], 1.0)
        gp.memset(onesb[:], 1.0)
        gp.memset(carry[:], 0.0)
        gp.memset(c16[:], 0.0)
        gp.memset(lastK[:], 0.0)
        gp.memset(lastP[:], 0.0)
        gp.memset(lastA[:], 0.0)
        gp.memset(sumkw[:], 0.0)
        gp.iota(iot[:], [[1, NT]], base=0, channel_multiplier=0,
                allow_small_or_imprecise_dtypes=True)
        gp.iota(iotk[:], [[1, KT]], base=0, channel_multiplier=0,
                allow_small_or_imprecise_dtypes=True)
        def gstep(op):
            op.then_inc(s_gp, 1)
            cnt.inc(s_gp, 1)
            gp.wait_ge(s_gp, cnt.val(s_gp))

        gstep(gp.memset(ident[:], 1.0))
        gstep(gp.affine_select(ident[:], ident[:], [[1, 128]], ALU.is_equal, 0.0,
                               base=0, channel_multiplier=-1))
        gstep(gp.memset(identb[:], 1.0))
        gstep(gp.affine_select(identb[:], identb[:], [[1, 128]], ALU.is_equal, 0.0,
                               base=0, channel_multiplier=-1))
        gstep(gp.memset(selT[:], 1.0))
        gstep(gp.affine_select(selT[:], selT[:], [[1, D]], ALU.is_ge, 0.0,
                               base=0, channel_multiplier=-64))
        gstep(gp.affine_select(selT[:], selT[:], [[-1, D]], ALU.is_ge, 0.0,
                               base=63, channel_multiplier=64))
        gstep(gp.memset(selc[:], 1.0))
        for c in range(DC):
            slc = selc[:, c * 16:(c + 1) * 16]
            gstep(gp.affine_select(slc, slc, [[-64, 16]], ALU.is_ge, 0.0,
                                   base=128 * c, channel_multiplier=1))
            gstep(gp.affine_select(slc, slc, [[64, 16]], ALU.is_ge, 0.0,
                                   base=63 - 128 * c, channel_multiplier=-1))
        gp_done = cnt.val(s_gd)
        gp_done2 = cnt.val(s_gp)

        for e in (vec, sca, ten):
            e.wait_ge(s_ld, cnt.val(s_ld))
            e.wait_ge(s_gd, gp_done)
            e.wait_ge(s_gp, gp_done2)
        sca.activation(la16[:, 5:6], la16[:, 6:7], ACTF.Exp)
        sca.activation(la16[:, 4:5], la16[:, 5:6], ACTF.Exp, scale=-1.0)
        sca.activation(la16[:, 0:1], la16[:, 4:5], ACTF.Ln)
        si = sca.activation(la16[:, 2:3], la16[:, 7:8], ACTF.Exp)
        vec.wait_ge(s_se, cnt.val(s_se))
        vec.tensor_scalar(la16[:, 1:2], la16[:, 0:1], -1.0, None, ALU.mult)
        th = patDF[:, PA:PA + DFC]; qwr = patDF[:, PB:PB + DFC]
        qwi = patDF[:, PB1C:PB1C + DFC]
        thr = patA1[:, DFC:2 * DFC]
        vec.tensor_scalar(thr, th, INV2PI, MAGIC, ALU.mult, ALU.add)
        vec.tensor_scalar(thr, thr, MAGIC, None, ALU.subtract)
        vec.scalar_tensor_tensor(thr, thr, -TWOPI, th, ALU.mult, ALU.add)
        sth = tX[:, 0:DFC]; cth = tX[:, DFC:2 * DFC]
        s2 = tX[:, 2 * DFC:3 * DFC]
        vec.tensor_scalar(s2, th, 1.0, PIH, ALU.mult, ALU.add)
        vec.tensor_scalar(tX[:, 3 * DFC:4 * DFC], s2, INV2PI, MAGIC, ALU.mult, ALU.add)
        vec.tensor_scalar(tX[:, 3 * DFC:4 * DFC], tX[:, 3 * DFC:4 * DFC], MAGIC,
                          None, ALU.subtract)
        vec.scalar_tensor_tensor(s2, tX[:, 3 * DFC:4 * DFC], -TWOPI, s2,
                                 ALU.mult, ALU.add)
        vdone()
        sca.wait_ge(s_ve, cnt.val(s_ve))
        sca.activation(sth, thr, ACTF.Sin)
        si = sca.activation(cth, s2, ACTF.Sin)
        vec.wait_ge(s_se, cnt.val(s_se))
        qp = tY[:, 0:DFC]; scrD = tY[:, DFC:2 * DFC]
        vec.tensor_tensor(qp, cth, qwr, ALU.mult)
        vec.tensor_tensor(scrD, sth, qwi, ALU.mult)
        vec.tensor_tensor(qp, qp, scrD, ALU.add)
        vec.tensor_tensor(scrD, qp, qwi, ALU.mult)
        vec.tensor_scalar(scrD, scrD, 0.1, None, ALU.mult)
        vec.tensor_tensor(tY[:, 2 * DFC:3 * DFC], scrD, sth, ALU.mult)
        vec.tensor_tensor(tY[:, 3 * DFC:4 * DFC], scrD, cth, ALU.mult)
        vec.tensor_copy(patDF[:, PA:PA + DFC], tY[:, 2 * DFC:3 * DFC])
        vec.tensor_copy(patDF[:, PB:PB + DFC], tY[:, 3 * DFC:4 * DFC])
        vec.tensor_scalar(patDF[:, PB1C:PB1C + DFC], patA1[:, 0:DFC], 0.1, PIH,
                          ALU.mult, ALU.add)
        vec.tensor_scalar(patDF[:, PB1S:PB1S + DFC], patA1[:, 0:DFC], 0.1, None,
                          ALU.mult)
        vec.tensor_scalar(pat[:, PBKA:PBKA + 8], pat[:, PBKA:PBKA + 8],
                          1.0 / 32.0, None, ALU.mult)
        vdone()

        ten.wait_ge(s_ve, cnt.val(s_ve))
        for (colsrc, dstcol) in [(0, QLA), (1, QNLA), (2, QEU)]:
            for c in range(DC):
                mi = ten.matmul(ps2[:, c:c + 1], selT[:, c * 128:(c + 1) * 128],
                                la16[:, colsrc:colsrc + 1], start=True, stop=True)
            mi.then_inc(s_pe, 1); cnt.inc(s_pe, 1)
            vec.wait_ge(s_pe, cnt.val(s_pe))
            vec.tensor_copy(patB[:, dstcol:dstcol + 8], ps2[:, 0:8])
            vdone()
            ten.wait_ge(s_ve, cnt.val(s_ve))

        for c in range(DC):
            nla = patB[:, QNLA + c:QNLA + c + 1]
            la_ = patB[:, QLA + c:QLA + c + 1]
            vec.tensor_tensor(scr8[:, 0:1], nla, t0bc[:], ALU.mult)
            vec.tensor_scalar(scr8[:, 1:2], nla, 1.0, LN4, ALU.mult, ALU.add)
            vec.tensor_tensor(scr8[:, 0:1], scr8[:, 0:1], scr8[:, 1:2], ALU.add)
            vec.tensor_scalar(scr8[:, 2:3], nla, float(NT), None, ALU.mult)
            vec.tensor_scalar(svA[:, 0:KT], iotk[:, 0:KT], scr8[:, 2:3], None, ALU.mult)
            vec.tensor_scalar(binv[:, c * KT:(c + 1) * KT], svA[:, 0:KT],
                              scr8[:, 0:1], None, ALU.add)
            vec.tensor_tensor(scr8[:, 3:4], la_, t0bc[:], ALU.mult)
            vec.tensor_scalar(scr8[:, 3:4], scr8[:, 3:4], 1.0, -float(C_S), ALU.mult, ALU.add)
            vec.tensor_scalar(scr8[:, 4:5], la_, float(NT), None, ALU.mult)
            vec.tensor_scalar(svA[:, 0:KT], iotk[:, 0:KT], scr8[:, 4:5], None, ALU.mult)
            vec.tensor_scalar(bpow[:, c * KT:(c + 1) * KT], svA[:, 0:KT],
                              scr8[:, 3:4], None, ALU.add)
        vec.tensor_tensor(la16[:, 3:4], la16[:, 1:2], t0bc[0:16, :], ALU.mult)
        vec.tensor_scalar(la16[:, 6:7], la16[:, 1:2], 1.0, LN4, ALU.mult, ALU.add)
        vec.tensor_tensor(la16[:, 3:4], la16[:, 3:4], la16[:, 6:7], ALU.add)
        vec.tensor_scalar(la16[:, 6:7], la16[:, 1:2], float(NT), None, ALU.mult)
        vec.tensor_scalar(ct1[:, 0:KT], iotk[0:16, 0:KT], la16[:, 6:7], None, ALU.mult)
        vec.tensor_scalar(b16k[:, 0:KT], ct1[:, 0:KT], la16[:, 3:4], None, ALU.add)
        vec.tensor_tensor(la16[:, 3:4], la16[:, 0:1], t0bc[0:16, :], ALU.mult)
        vec.tensor_scalar(la16[:, 3:4], la16[:, 3:4], 1.0, -float(C_S), ALU.mult, ALU.add)
        vec.tensor_scalar(la16[:, 6:7], la16[:, 0:1], float(NT), None, ALU.mult)
        vec.tensor_scalar(ct1[:, 0:KT], iotk[0:16, 0:KT], la16[:, 6:7], None, ALU.mult)
        vec.tensor_scalar(b16k[:, KT:2 * KT], ct1[:, 0:KT], la16[:, 3:4], None, ALU.add)
        vec.memset(tZ[:], 0.0)
        vdone()
        sync.wait_ge(s_ve, cnt.val(s_ve))
        for i in range(4):
            sync.dma_start(out=ag_in[i], in_=tZ[0:1, 0:AG_W]).then_inc(s_st, 16)
            cnt.inc(s_st, 16)
        vec.tensor_scalar(bigk[:, 0:KT], iotk[:, 0:KT], -float(NT), BIG_C,
                          ALU.mult, ALU.add)
        vec.tensor_scalar(bigk[:, 0:KT], bigk[:, 0:KT], t0bc[:], None,
                          ALU.subtract)
        vdone()
        bar()

        evac_cnt = {"v": 0}

        def ln_stats_norm(n, y_tile, out_tile, armored=False):
            if armored:
                vec.tensor_scalar(sc[:, 0:DC * n], y_tile[:, 0:DC * n],
                                  B1C, -B1C, ALU.min, ALU.max)
                vdone()
                sca.wait_ge(s_ve, cnt.val(s_ve))
                si = sca.activation(tB16[:, 0:DC * n], sc[:, 0:DC * n], ACTF.Square)
                ten.wait_ge(s_se, cnt.val(s_se))
                for c in range(DC):
                    mi = ten.matmul(ps2[64:65, 0:n], onesb[:, 0:1],
                                    tB16[:, c * n:(c + 1) * n],
                                    start=(c == 0), stop=(c == DC - 1))
                mi.then_inc(s_pe, 1); cnt.inc(s_pe, 1)
                vec.wait_ge(s_pe, cnt.val(s_pe))
                vec.tensor_scalar(sc[:, 0:DC * n], y_tile[:, 0:DC * n],
                                  SC70, None, ALU.mult)
                vec.tensor_copy(tB16b[:, 0:DC * n], sc[:, 0:DC * n])
                vdone()
                sca.wait_ge(s_ve, cnt.val(s_ve))
                sca.wait_ge(s_pe, cnt.val(s_pe))
                si = sca.activation(tB16[:, 0:DC * n], sc[:, 0:DC * n], ACTF.Square)
            else:
                vec.tensor_copy(tB16b[:, 0:DC * n], y_tile[:, 0:DC * n])
                vdone()
                sca.wait_ge(s_ve, cnt.val(s_ve))
                si = sca.activation(tB16[:, 0:DC * n], y_tile[:, 0:DC * n], ACTF.Square)
            ten.wait_ge(s_ve, cnt.val(s_ve))
            ten.wait_ge(s_se, cnt.val(s_se))
            for c in range(DC):
                mi = ten.matmul(ps2[0:1, 0:n], onesb[:, 0:1],
                                tB16b[:, c * n:(c + 1) * n],
                                start=(c == 0), stop=(c == DC - 1))
            for c in range(DC):
                mi = ten.matmul(ps2[32:33, 0:n], onesb[:, 0:1],
                                tB16[:, c * n:(c + 1) * n],
                                start=(c == 0), stop=(c == DC - 1))
            mi.then_inc(s_pe, 1); cnt.inc(s_pe, 1)
            vec.wait_ge(s_pe, cnt.val(s_pe))
            mu = svA[0:1, 0:n]; var = svB[0:1, 0:n]
            rstd = ct1[0:1, 0:n]; c0r = ct2[0:1, 0:n]; msk = crd[0:1, 0:n]
            vec.tensor_scalar(mu, ps2[0:1, 0:n], 1.0 / D, None, ALU.mult)
            vec.tensor_scalar(var, ps2[32:33, 0:n], 1.0 / D, None, ALU.mult)
            if armored:
                aarg = sc[0:1, 0:n]; barg = sc[0:1, n:2 * n]
                mbl = sc[0:1, 2 * n:3 * n]; muc = sc[0:1, 3 * n:4 * n]
                vlo = sc[0:1, 4 * n:5 * n]; rb = tZ[0:1, 0:n]
                vec.tensor_scalar(msk, ps2[32:33, 0:n], TQ_S, None, ALU.is_lt)
                vec.tensor_tensor(c0r, mu, mu, ALU.mult)
                vec.tensor_tensor(var, var, c0r, ALU.subtract)
                vec.tensor_scalar(muc, mu, P70, B1C, ALU.mult, ALU.min)
                vec.tensor_scalar(muc, muc, -B1C, None, ALU.max)
                vec.tensor_tensor(muc, muc, muc, ALU.mult)
                vec.scalar_tensor_tensor(vlo, ps2[64:65, 0:n], 1.0 / D,
                                         muc, ALU.mult, ALU.subtract)
                vec.tensor_scalar(aarg, vlo, 1.5e19, 0.0, ALU.min, ALU.max)
                vec.tensor_scalar(barg, var, SC64M, 1e-30, ALU.mult, ALU.max)
                vec.tensor_scalar(mbl, var, TBL, None, ALU.is_gt)
                vdone()
                sca.wait_ge(s_ve, cnt.val(s_ve))
                sca.activation(rstd, aarg, ACTF.Ln, bias=c_eps[0:1, 0:1])
                sca.activation(rstd, rstd, ACTF.Exp, scale=-0.5)
                sca.activation(rb, barg, ACTF.Ln)
                si = sca.activation(rb, rb, ACTF.Exp, scale=-0.5,
                                    bias=c_lnB[0:1, 0:1])
                vec.wait_ge(s_se, cnt.val(s_se))
                vec.tensor_tensor(rb, rb, rstd, ALU.subtract)
                vec.tensor_tensor(rb, rb, mbl, ALU.mult)
                vec.tensor_tensor(rstd, rstd, rb, ALU.add)
                vec.tensor_tensor(rstd, rstd, msk, ALU.mult)
                vec.scalar_tensor_tensor(c0r, mu, -1.0, rstd, ALU.mult, ALU.mult)
                vec.tensor_scalar(c0r, c0r, C17, -C17, ALU.min, ALU.max)
                vec.tensor_scalar(c0r, c0r, P70, None, ALU.mult)
                vdone()
            else:
                vec.tensor_tensor(msk, mu, mu, ALU.mult)
                vec.tensor_tensor(var, var, msk, ALU.subtract)
                vdone()
                sca.wait_ge(s_ve, cnt.val(s_ve))
                sca.activation(rstd, var, ACTF.Ln, bias=c_eps[0:1, 0:1])
                si = sca.activation(rstd, rstd, ACTF.Exp, scale=-0.5)
                vec.wait_ge(s_se, cnt.val(s_se))
                vec.tensor_scalar(msk, var, 0.0, 0.0, ALU.mult, ALU.is_equal)
                vec.tensor_tensor(rstd, rstd, msk, ALU.mult)
                vec.scalar_tensor_tensor(c0r, mu, -1.0, rstd, ALU.mult, ALU.mult)
                vdone()
            ten.wait_ge(s_ve, cnt.val(s_ve))
            ten.matmul(ps3[:, 0:n], ones2d[0:1, :], rstd, start=True, stop=True)
            mi = ten.matmul(psBig[:, 0:n], ones2d[0:1, :], c0r, start=True, stop=True)
            mi.then_inc(s_pe, 1); cnt.inc(s_pe, 1)
            vec.wait_ge(s_pe, cnt.val(s_pe))
            for c in range(DC):
                slq = slice(c * n, (c + 1) * n)
                vec.tensor_tensor(out_tile[:, slq], y_tile[:, slq], ps3[:, 0:n], ALU.mult)
                vec.tensor_tensor(out_tile[:, slq], out_tile[:, slq], psBig[:, 0:n], ALU.add)

        for hh in range(2):
            gp.dma_start(
                out=arH[:, 0:4 * DF].rearrange("p (c f) -> p c f", f=DF),
                in_=w1[hh * 512:(hh + 1) * 512, :].rearrange("(c p) f -> p c f", p=128)
            ).then_inc(s_gd, 16)
            cnt.inc(s_gd, 16)
            gp.wait_ge(s_gd, cnt.val(s_gd))
            for q in range(8):
                gp.dma_start(
                    out=w1b_s[q].rearrange("p (kc v) -> p kc v", v=NT)[:, hh * 4:(hh + 1) * 4, :],
                    in_=arH[:, 0:4 * DF].rearrange("p (c f) -> p c f", f=DF)[:, :, q * NT:(q + 1) * NT]
                ).then_inc(s_pc, 16)
                cnt.inc(s_pc, 16)
            gp.wait_ge(s_pc, cnt.val(s_pc))
        for hh in range(4):
            gp.dma_start(
                out=arH[:, 0:8 * D].rearrange("p (c f) -> p c f", f=D),
                in_=w2[hh * 1024:(hh + 1) * 1024, :].rearrange("(c p) f -> p c f", p=128)
            ).then_inc(s_gd, 16)
            cnt.inc(s_gd, 16)
            gp.wait_ge(s_gd, cnt.val(s_gd))
            for e in range(DC):
                gp.dma_start(
                    out=w2b_s[e].rearrange("p (fc j) -> p fc j", j=128)[:, hh * 8:(hh + 1) * 8, :],
                    in_=arH[:, 0:8 * D].rearrange("p (c f) -> p c f", f=D)[:, :, e * 128:(e + 1) * 128]
                ).then_inc(s_pc, 16)
                cnt.inc(s_pc, 16)
            gp.wait_ge(s_pc, cnt.val(s_pc))


        for k in range(KT):
            vec.wait_ge(s_st, cnt.val(s_st))
            sca.wait_ge(s_st, cnt.val(s_st))
            sync.dma_start(out=sv(tX, NT, 0, DC, 0, NT),
                           in_=hv(x_in, k * NT, NT)).then_inc(s_ld, 16)
            cnt.inc(s_ld, 16)
            vec.wait_ge(s_ld, cnt.val(s_ld))
            sca.wait_ge(s_ld, cnt.val(s_ld))
            ln_stats_norm(NT, tX, tY)
            for c in range(DC):
                slq = slice(c * NT, (c + 1) * NT)
                vec.tensor_scalar(tB16[:, slq],
                                  tY[:, slq], pat[:, PG + c:PG + c + 1],
                                  pat[:, PBE + c:PBE + c + 1], ALU.mult, ALU.add)
            vdone()
            sync.wait_ge(s_ve, cnt.val(s_ve))
            sync.dma_start(out=hv(h_s, k * NT, NT),
                           in_=sv(tB16, NT, 0, DC, 0, NT)).then_inc(s_h, 16)
            cnt.inc(s_h, 16)
            ten.wait_ge(s_ve, cnt.val(s_ve))
            se_b = cnt.val(s_se)
            for e in range(DC):
                bank = psAB[e % 2]
                if e >= 2:
                    ten.wait_ge(s_se, se_b + e - 1)
                else:
                    ten.wait_ge(s_se, se_b)
                for kc in range(DC):
                    mi = ten.matmul(bank[:, 0:NT],
                                    arK[:, kc * D + e * 128: kc * D + e * 128 + 128],
                                    tB16[:, kc * NT: kc * NT + NT],
                                    start=(kc == 0), stop=(kc == DC - 1))
                mi.then_inc(s_pe, 1); cnt.inc(s_pe, 1)
                sca.wait_ge(s_pe, cnt.val(s_pe))
                si = sca.activation(tZ[:, e * NT:(e + 1) * NT], bank[:, 0:NT],
                                    ACTF.Identity, bias=pat[:, PBK + e:PBK + e + 1])
            vec.wait_ge(s_se, cnt.val(s_se))
            vec.wait_ge(s_pe, cnt.val(s_pe))
            vec.wait_ge(s_h, cnt.val(s_h))
            vec.tensor_copy(tB16[:, 0:DC * NT], tZ[:, 0:DC * NT])
            vdone()
            sync.wait_ge(s_se, cnt.val(s_se))
            sync.dma_start(out=hv(keys_s, k * NT, NT),
                           in_=sv(tZ, NT, 0, DC, 0, NT)).then_inc(s_st, 16)
            cnt.inc(s_st, 16)
            ten.wait_ge(s_ve, cnt.val(s_ve))
            se_b = cnt.val(s_se)
            for e in range(DC):
                bank = psAB[e % 2]
                if e >= 2:
                    ten.wait_ge(s_se, se_b + e - 1)
                else:
                    ten.wait_ge(s_se, se_b)
                for kc in range(DC):
                    mi = ten.matmul(bank[:, 0:NT],
                                    arK[:, (8 + kc) * D + e * 128: (8 + kc) * D + e * 128 + 128],
                                    tB16[:, kc * NT: kc * NT + NT],
                                    start=(kc == 0), stop=(kc == DC - 1))
                mi.then_inc(s_pe, 1); cnt.inc(s_pe, 1)
                sca.wait_ge(s_pe, cnt.val(s_pe))
                si = sca.activation(sc[:, e * NT:(e + 1) * NT], bank[:, 0:NT],
                                    ACTF.Exp, scale=1.0 / 32.0,
                                    bias=pat[:, PBKA + e:PBKA + e + 1])
                vec.wait_ge(s_se, cnt.val(s_se))
                vec.tensor_copy(tB16b[:, e * NT:(e + 1) * NT], sc[:, e * NT:(e + 1) * NT])
                vec.tensor_reduce(scr8[:, e:e + 1], sc[:, e * NT:(e + 1) * NT],
                                  mybir.AxisListType.X, ALU.add)
                vec.tensor_tensor(sumkw[:, e:e + 1], sumkw[:, e:e + 1],
                                  scr8[:, e:e + 1], ALU.add)
            vdone()
            sync.wait_ge(s_ve, cnt.val(s_ve))
            sync.dma_start(out=hv(kw_s, k * NT, NT),
                           in_=sv(tB16b, NT, 0, DC, 0, NT)).then_inc(s_st, 16)
            cnt.inc(s_st, 16)
            tbar()

        def allgather_carry(idx, src_cols, ccol0, with16=False):
            ten.wait_ge(s_ve, cnt.val(s_ve))
            ten.transpose(ps2[0:8, 0:128], src_cols[:], ident[:]).then_inc(s_pe, 1)
            cnt.inc(s_pe, 1)
            sca.wait_ge(s_pe, cnt.val(s_pe))
            sca.activation(hrow[:], ps2[0:8, 0:128], ACTF.Copy)
            sync.wait_ge(s_se, cnt.val(s_se))
            sync.dma_start(out=ag_in[idx][:, 0:1024].rearrange("o (r c) -> r (o c)", r=8),
                           in_=hrow[:]).then_inc(s_la, 16)
            cnt.inc(s_la, 16)
            if with16:
                sync.dma_start(out=ag_in[idx][:, 1024:1040].rearrange("o (w u) -> (o w) u", u=1),
                               in_=c16[:, 1:2]).then_inc(s_la, 16)
                cnt.inc(s_la, 16)
            gp.wait_ge(s_la, cnt.val(s_la))
            gp.collective_compute("AllGather", ALU.bypass,
                                  replica_groups=[list(range(8))],
                                  ins=[ag_in[idx].opt()],
                                  outs=[ag_out[idx].opt()]).then_inc(s_ag, 1)
            cnt.inc(s_ag, 1)
            sync.wait_ge(s_ag, cnt.val(s_ag))
            gselv = sc[0:8, 0:AG_W]
            sync.dma_start(out=gselv, in_=ag_out[idx]).then_inc(s_la, 16)
            cnt.inc(s_la, 16)
            ten.wait_ge(s_la, cnt.val(s_la))
            ten.matmul(ps2[0:1, 0:NT], onehot8[:], gselv[:, 0:NT], start=True, stop=True)
            mi = ten.matmul(ps2[32:33, 0:NT], onehot8[:], gselv[:, NT:2 * NT],
                            start=True, stop=True)
            if with16:
                mi = ten.matmul(ps2[64:65, 0:16], onehot8[:], gselv[:, 1024:1040],
                                start=True, stop=True)
            mi.then_inc(s_pe, 1); cnt.inc(s_pe, 1)
            sca.wait_ge(s_pe, cnt.val(s_pe))
            sca.activation(svA[0:1, 0:NT], ps2[0:1, 0:NT], ACTF.Copy)
            si = sca.activation(svA[32:33, 0:NT], ps2[32:33, 0:NT], ACTF.Copy)
            if with16:
                si = sca.activation(svA[64:65, 0:16], ps2[64:65, 0:16], ACTF.Copy)
            ten.wait_ge(s_se, cnt.val(s_se))
            for c in range(DC):
                rsrc = (svA[0:1, c * 128:(c + 1) * 128] if c < 4 else
                        svA[32:33, (c - 4) * 128:(c - 3) * 128])
                idk = ones2d[0:1, 0:1] if c < 4 else ones2d[32:33, 0:1]
                ten.transpose(ps3[:, 0:1], rsrc, idk).then_inc(s_pe, 1)
                cnt.inc(s_pe, 1)
                vec.wait_ge(s_pe, cnt.val(s_pe))
                vec.tensor_copy(carry[:, ccol0 + c:ccol0 + c + 1], ps3[:, 0:1])
                vdone()
                ten.wait_ge(s_ve, cnt.val(s_ve))
            if with16:
                ten.transpose(ps3[0:16, 0:1], svA[64:65, 0:16],
                              ones2d[64:65, 0:1]).then_inc(s_pe, 1)
                cnt.inc(s_pe, 1)
                vec.wait_ge(s_pe, cnt.val(s_pe))
                vec.tensor_copy(c16[:, 0:1], ps3[0:16, 0:1])
            bar()

        allgather_carry(0, sumkw, 0)
        vec.tensor_copy(lastK[:], carry[:, 0:8])
        vdone()
        bar()

        gp.wait_ge(s_pe, cnt.val(s_pe))
        wload(arK, 0, wvT)
        ten.wait_ge(s_gd, cnt.val(s_gd))
        for k in range(KT):
            vec.wait_ge(s_st, cnt.val(s_st))
            sca.wait_ge(s_st, cnt.val(s_st))
            sync.dma_start(out=sv(tX, NT, 0, DC, 0, NT),
                           in_=hv(keys_s, k * NT, NT)).then_inc(s_ld, 16)
            cnt.inc(s_ld, 16)
            sync.dma_start(out=sv(tB16, NT, 0, DC, 0, NT),
                           in_=hv(kw_s, k * NT, NT)).then_inc(s_ld, 16)
            cnt.inc(s_ld, 16)
            sync.dma_start(out=sv(arH, NT, 0, DC, 0, NT),
                           in_=hv(h_s, k * NT, NT)).then_inc(s_ld, 16)
            cnt.inc(s_ld, 16)
            ten.wait_ge(s_ld, cnt.val(s_ld))
            vp_done = []
            for e in range(DC):
                bank = psAB[e % 2]
                if e >= 2:
                    ten.wait_ge(s_se, vp_done[e - 2])
                else:
                    ten.wait_ge(s_se, cnt.val(s_se))
                for kc in range(DC):
                    mi = ten.matmul(bank[:, 0:NT],
                                    arK[:, kc * D + e * 128: kc * D + e * 128 + 128],
                                    arH[:, kc * NT: kc * NT + NT],
                                    start=(kc == 0), stop=(kc == DC - 1))
                mi.then_inc(s_pe, 1); cnt.inc(s_pe, 1)
                sca.wait_ge(s_pe, cnt.val(s_pe))
                si = sca.activation(sc[:, e * NT:(e + 1) * NT], bank[:, 0:NT],
                                    ACTF.Identity, bias=pat[:, PBV + e:PBV + e + 1])
                vp_done.append(cnt.val(s_se))
            sync.wait_ge(s_se, cnt.val(s_se))
            sync.dma_start(out=hv(vals_s, k * NT, NT),
                           in_=sv(sc, NT, 0, DC, 0, NT)).then_inc(s_st, 16)
            cnt.inc(s_st, 16)
            vec.wait_ge(s_ld, cnt.val(s_ld))
            for c in range(DC):
                slq = slice(c * NT, (c + 1) * NT)
                vec.tensor_tensor_scan(tY[:, slq], tB16[:, slq], zerosb[:, 0:NT],
                                       lastK[:, c:c + 1], ALU.add, ALU.add)
                vec.tensor_copy(lastK[:, c:c + 1], tY[:, c * NT + NT - 1:c * NT + NT])
            vdone()
            sca.wait_ge(s_ve, cnt.val(s_ve))
            sca.activation(tZ[:, 0:DC * NT], tY[:, 0:DC * NT], ACTF.Ln)
            si = sca.activation(tZ[:, 0:DC * NT], tZ[:, 0:DC * NT], ACTF.Exp, scale=-1.0)
            vec.wait_ge(s_se, cnt.val(s_se))
            for c in range(DC):
                slq = slice(c * NT, (c + 1) * NT)
                vec.tensor_tensor(tY[:, slq], tB16[:, slq], tX[:, slq], ALU.mult)
                vec.tensor_tensor(tY[:, slq], tY[:, slq], tZ[:, slq], ALU.mult)
            for c in range(DC):
                slq = slice(c * NT, (c + 1) * NT)
                vec.tensor_tensor_scan(tZ[:, slq], tY[:, slq], zerosb[:, 0:NT],
                                       lastP[:, c:c + 1], ALU.add, ALU.add)
                vec.tensor_copy(lastP[:, c:c + 1], tZ[:, c * NT + NT - 1:c * NT + NT])
            vec.tensor_copy(tB16b[:, 0:DC * NT], tZ[:, 0:DC * NT])
            vdone()
            sync.wait_ge(s_ve, cnt.val(s_ve))
            sync.dma_start(out=hv(pool_s, k * NT, NT),
                           in_=sv(tB16b, NT, 0, DC, 0, NT)).then_inc(s_st, 16)
            cnt.inc(s_st, 16)
            tbar()

        allgather_carry(1, lastP, 8)

        for k in range(KT):
            vec.wait_ge(s_st, cnt.val(s_st))
            sca.wait_ge(s_st, cnt.val(s_st))
            sync.dma_start(out=sv(tX, NT, 0, DC, 0, NT),
                           in_=hv(x_in, k * NT, NT)).then_inc(s_ld, 16)
            cnt.inc(s_ld, 16)
            sync.dma_start(out=sv(tB16b, NT, 0, DC, 0, NT),
                           in_=hv(pool_s, k * NT, NT)).then_inc(s_ld, 16)
            cnt.inc(s_ld, 16)
            sync.dma_start(out=sv(sc, NT, 0, DC, 0, NT),
                           in_=hv(vals_s, k * NT, NT)).then_inc(s_ld, 16)
            cnt.inc(s_ld, 16)
            vec.wait_ge(s_ld, cnt.val(s_ld))
            for c in range(DC):
                slq = slice(c * NT, (c + 1) * NT)
                vec.scalar_tensor_tensor(tY[:, slq], tB16b[:, slq],
                                         carry[:, 8 + c:8 + c + 1],
                                         sc[:, slq], ALU.add, ALU.mult)
                vec.tensor_tensor(tY[:, slq], tY[:, slq], tX[:, slq], ALU.add)
            vdone()
            sca.wait_ge(s_ve, cnt.val(s_ve))
            ln_stats_norm(NT, tY, tZ)
            vdone()
            sync.wait_ge(s_ve, cnt.val(s_ve))
            sync.dma_start(out=hv(x2_s, k * NT, NT),
                           in_=sv(tZ, NT, 0, DC, 0, NT)).then_inc(s_st, 16)
            cnt.inc(s_st, 16)
            tbar()

        if upto == "x2":
            _finalize(nc, sync, cnt, s_st, dbg, debug,
                      [("d_x2", x2_s, D * T), ("d_keys", keys_s, D * T)])
            return nc

        W2SL = DFC * NT
        bar()
        for k in range(KT):
            vec.wait_ge(s_st, cnt.val(s_st))
            sca.wait_ge(s_st, cnt.val(s_st))
            sync.dma_start(out=sv(tX, NT, 0, DC, 0, NT),
                           in_=hv(x2_s, k * NT, NT)).then_inc(s_ld, 16)
            cnt.inc(s_ld, 16)
            vec.wait_ge(s_ld, cnt.val(s_ld))
            vec.tensor_copy(tB16[:, 0:DC * NT], tX[:, 0:DC * NT])
            vdone()
            ten.wait_ge(s_ve, cnt.val(s_ve))
            banks4 = [psBig[:, 0:NT], psBig[:, NT:2 * NT],
                      ps2[:, 0:NT], ps3[:, 0:NT]]
            pe_q = cnt.val(s_pe)
            se_done4, ve_done4 = [], []
            for q in range(8):
                ssl = s_sl0 if q % 2 == 0 else s_sl1
                if q >= 2:
                    sync.wait_ge(s_pe, pe_q + 4 * (q - 1))
                sync.dma_start(out=arH[:, (q % 2) * DF: (q % 2) * DF + DF],
                               in_=w1b_s[q, :, :]).then_inc(ssl, 16)
                cnt.inc(ssl, 16)
                ten.wait_ge(ssl, cnt.val(ssl))
                woq = (q % 2) * DF
                for fq in range(4):
                    fc = q * 4 + fq
                    bank = banks4[fc % 4]
                    slq = slice((fc % 4) * NT, (fc % 4) * NT + NT)
                    if fc >= 4:
                        ten.wait_ge(s_se, se_done4[fc - 4])
                        ten.wait_ge(s_ve, ve_done4[fc - 4])
                    for kc in range(DC):
                        mi = ten.matmul(bank,
                                        arH[:, woq + kc * NT + fq * 128: woq + kc * NT + fq * 128 + 128],
                                        tB16[:, kc * NT: kc * NT + NT],
                                        start=(kc == 0), stop=(kc == DC - 1))
                    mi.then_inc(s_pe, 1); cnt.inc(s_pe, 1)
                    sca.wait_ge(s_pe, cnt.val(s_pe))
                    sca.activation(tY[:, slq], bank, ACTF.Sin, scale=0.1,
                                   bias=patDF[:, PB1C + fc:PB1C + fc + 1])
                    sca.activation(tZ[:, slq], bank, ACTF.Sin, scale=0.1,
                                   bias=patDF[:, PB1S + fc:PB1S + fc + 1])
                    se_done4.append(cnt.val(s_se))
                    vec.wait_ge(s_se, cnt.val(s_se))
                    vec.scalar_tensor_tensor(sc[:, slq], tY[:, slq],
                                             patDF[:, PA + fc:PA + fc + 1],
                                             bank, ALU.mult, ALU.add)
                    vec.scalar_tensor_tensor(sc[:, slq], tZ[:, slq],
                                             patDF[:, PB + fc:PB + fc + 1],
                                             sc[:, slq], ALU.mult, ALU.add)
                    vec.tensor_scalar(arK[:, fc * NT:fc * NT + NT], sc[:, slq],
                                      patA1[:, fc:fc + 1], 0.0, ALU.add, ALU.max)
                    ve_done4.append(cnt.val(s_ve))
            ten.wait_ge(s_ve, ve_done4[-1])
            se_b = cnt.val(s_se)
            pe_b = cnt.val(s_pe)
            for e in range(DC):
                ssl = s_sl0 if e % 2 == 0 else s_sl1
                if e >= 2:
                    sync.wait_ge(s_pe, pe_b + e - 1)
                else:
                    sync.wait_ge(s_pe, pe_b)
                sync.dma_start(out=arK[:, W2SL + (e % 2) * DFC * 128: W2SL + (e % 2) * DFC * 128 + DFC * 128],
                               in_=w2b_s[e, :, :]).then_inc(ssl, 16)
                cnt.inc(ssl, 16)
                ten.wait_ge(ssl, cnt.val(ssl))
                if e >= 2:
                    ten.wait_ge(s_se, se_b + e - 1)
                else:
                    ten.wait_ge(s_se, se_b)
                bank = psAB[e % 2]
                woff = W2SL + (e % 2) * DFC * 128
                for fc in range(DFC):
                    mi = ten.matmul(bank[:, 0:NT],
                                    arK[:, woff + fc * 128: woff + fc * 128 + 128],
                                    arK[:, fc * NT: fc * NT + NT],
                                    start=(fc == 0), stop=(fc == DFC - 1))
                mi.then_inc(s_pe, 1); cnt.inc(s_pe, 1)
                sca.wait_ge(s_pe, cnt.val(s_pe))
                si = sca.activation(tY[:, e * NT:(e + 1) * NT], bank[:, 0:NT],
                                    ACTF.Identity, bias=patB[:, QB2 + e:QB2 + e + 1])
            vec.wait_ge(s_se, cnt.val(s_se))
            vec.tensor_tensor(tY[:, 0:DC * NT], tY[:, 0:DC * NT], tX[:, 0:DC * NT],
                              ALU.add)
            vdone()
            ln_stats_norm(NT, tY, tZ)
            vdone()
            sync.wait_ge(s_ve, cnt.val(s_ve))
            sync.dma_start(out=hv(x3_s, k * NT, NT),
                           in_=sv(tZ, NT, 0, DC, 0, NT)).then_inc(s_st, 16)
            cnt.inc(s_st, 16)
            tbar()

        if upto == "x3":
            _finalize(nc, sync, cnt, s_st, dbg, debug,
                      [("d_x2", x2_s, D * T), ("d_x3", x3_s, D * T)])
            return nc

        for c in range(DC):
            vec.tensor_copy(scr8b[:, c:c + 1], tZ[:, c * NT + NT - 1:c * NT + NT])
        vdone()
        ten.wait_ge(s_ve, cnt.val(s_ve))
        allgather_carry(2, scr8b, 16)

        wload(arH, 0, wkeT)
        wload(arH, 8 * D, wvaT)
        wload(arH, 16 * D, wreT)
        for e in (vec, sca, ten):
            e.wait_ge(s_gd, cnt.val(s_gd))
        vec.memset(lastA[:], 0.0)
        vec.memset(maxP[:], 0.0)
        vec.memset(maxN[:], 0.0)
        vec.memset(c16[:, 1:2], 0.0)
        vec.memset(c16[:, 6:7], 0.0)
        vdone()
        bar()

        def proj_l1(wbase, ksl_tile, evac, eng=None, rbase=0):
            eng = eng or sca
            esem = eng._sem
            done = []
            for e in range(DC):
                bank = psAB[e % 2]
                if e >= 2:
                    ten.wait_ge(esem, done[e - 2])
                else:
                    ten.wait_ge(esem, cnt.val(esem))
                for kc in range(DC):
                    mi = ten.matmul(bank[:, 0:NT],
                                    arH[:, wbase + kc * D + e * 128: wbase + kc * D + e * 128 + 128],
                                    ksl_tile[:, rbase + kc * NT: rbase + kc * NT + NT],
                                    start=(kc == 0), stop=(kc == DC - 1))
                mi.then_inc(s_pe, 1); cnt.inc(s_pe, 1)
                eng.wait_ge(s_pe, cnt.val(s_pe))
                evac(e, bank)
                done.append(cnt.val(esem))

        def inv_alpha_tile(k, dst, scr_t):
            for c in range(DC):
                slq = slice(c * NT, (c + 1) * NT)
                vec.tensor_scalar(scr_t[:, slq], iot[:, 0:NT],
                                  patB[:, QNLA + c:QNLA + c + 1],
                                  binv[:, c * KT + k:c * KT + k + 1],
                                  ALU.mult, ALU.add)
            vec.tensor_scalar(scr_t[:, 0:DC * NT], scr_t[:, 0:DC * NT], EXP_CLAMP,
                              None, ALU.min)
            vdone()
            sca.wait_ge(s_ve, cnt.val(s_ve))
            si = sca.activation(dst[:, 0:DC * NT], scr_t[:, 0:DC * NT], ACTF.Exp)
            vec.wait_ge(s_se, cnt.val(s_se))

        def inv16_tile(k, dst, scr_c):
            vec.tensor_scalar(scr_c[:, 0:NT], iot[0:16, 0:NT], la16[:, 1:2],
                              b16k[:, k:k + 1], ALU.mult, ALU.add)
            vec.tensor_scalar(scr_c[:, 0:NT], scr_c[:, 0:NT], EXP_CLAMP, None, ALU.min)
            vdone()
            sca.wait_ge(s_ve, cnt.val(s_ve))
            si = sca.activation(dst[:, 0:NT], scr_c[:, 0:NT], ACTF.Exp)
            vec.wait_ge(s_se, cnt.val(s_se))

        for k in range(KT):
            vec.wait_ge(s_st, cnt.val(s_st))
            sca.wait_ge(s_st, cnt.val(s_st))
            sync.dma_start(out=sv(tX, NT, 0, DC, 0, NT),
                           in_=hv(x3_s, k * NT, NT)).then_inc(s_ld, 16)
            cnt.inc(s_ld, 16)
            if k == 0:
                sync.dma_start(out=sv(tY, NT, 0, DC, 1, NT - 1),
                               in_=hv(x3_s, 0, NT - 1)).then_inc(s_ld, 16)
                cnt.inc(s_ld, 16)
                vec.wait_ge(s_ld, cnt.val(s_ld))
                for c in range(DC):
                    vec.tensor_copy(tY[:, c * NT:c * NT + 1],
                                    carry[:, 16 + c:16 + c + 1])
            else:
                sync.dma_start(out=sv(tY, NT, 0, DC, 0, NT),
                               in_=hv(x3_s, k * NT - 1, NT)).then_inc(s_ld, 16)
                cnt.inc(s_ld, 16)
                vec.wait_ge(s_ld, cnt.val(s_ld))
            vec.tensor_tensor(tX[:, 0:DC * NT], tX[:, 0:DC * NT], tY[:, 0:DC * NT],
                              ALU.subtract)
            vdone()

            for c in range(DC):
                slq = slice(c * NT, (c + 1) * NT)
                vec.scalar_tensor_tensor(arK[:, c * NT:c * NT + NT], tX[:, slq],
                                         pat[:, PMK + c:PMK + c + 1],
                                         tY[:, slq], ALU.mult, ALU.add)
                vec.scalar_tensor_tensor(arK[:, DF + c * NT:DF + c * NT + NT],
                                         tX[:, slq],
                                         pat[:, PMV + c:PMV + c + 1],
                                         tY[:, slq], ALU.mult, ALU.add)
                vec.scalar_tensor_tensor(arK[:, 2 * DF + c * NT:2 * DF + c * NT + NT],
                                         tX[:, slq],
                                         pat[:, PMR + c:PMR + c + 1],
                                         tY[:, slq], ALU.mult, ALU.add)
            vdone()
            ten.wait_ge(s_ve, cnt.val(s_ve))

            def evac_k(e, bank):
                si = sca.activation(sc[:, e * NT:(e + 1) * NT], bank[:, 0:NT], ACTF.Exp)
            proj_l1(0, arK, evac_k, rbase=0)

            def evac_v(e, bank):
                si = sca.activation(tZ[:, e * NT:(e + 1) * NT], bank[:, 0:NT],
                                    ACTF.Identity)
            proj_l1(8 * D, arK, evac_v, rbase=DF)

            def evac_r(e, bank):
                si = sca.activation(tB16b[:, e * NT:(e + 1) * NT], bank[:, 0:NT],
                                    ACTF.Sigmoid)
            proj_l1(16 * D, arK, evac_r, rbase=2 * DF)

            ten.wait_ge(s_se, cnt.val(s_se))
            for c in range(DC):
                mi = ten.matmul(ps2[0:16, 0:NT], selc[:, c * 16:(c + 1) * 16],
                                sc[:, c * NT:(c + 1) * NT],
                                start=(c == 0), stop=(c == DC - 1))
            mi.then_inc(s_pe, 1); cnt.inc(s_pe, 1)
            vec.wait_ge(s_pe, cnt.val(s_pe))
            vec.tensor_copy(cse[:, 0:NT], ps2[0:16, 0:NT])
            vdone()
            sync.wait_ge(s_ve, cnt.val(s_ve))
            sync.dma_start(out=se_s[:, k * NT:(k + 1) * NT], in_=cse[:, 0:NT]).then_inc(s_st, 16)
            cnt.inc(s_st, 16)
            sync.wait_ge(s_se, cnt.val(s_se))
            sync.dma_start(out=hv(r_s, k * NT, NT),
                           in_=sv(tB16b, NT, 0, DC, 0, NT)).then_inc(s_st, 16)
            cnt.inc(s_st, 16)

            vec.wait_ge(s_se, cnt.val(s_se))
            vec.tensor_tensor(tZ[:, 0:DC * NT], sc[:, 0:DC * NT], tZ[:, 0:DC * NT],
                              ALU.mult)
            vdone()
            sync.wait_ge(s_ve, cnt.val(s_ve))
            sync.dma_start(out=hv(ekv_s, k * NT, NT),
                           in_=sv(tZ, NT, 0, DC, 0, NT)).then_inc(s_st, 16)
            cnt.inc(s_st, 16)

            inv_alpha_tile(k, tX, tY)
            vec.scalar_tensor_tensor(tX[:, 0:DC * NT], tZ[:, 0:DC * NT], C1,
                                     tX[:, 0:DC * NT], ALU.mult, ALU.mult)
            vec.tensor_scalar(bmt[:, 0:NT], iot[:, 0:NT], -1.0,
                              bigk[:, k:k + 1], ALU.mult, ALU.add)
            vec.wait_ge(s_pe, cnt.val(s_pe))
            u1 = psBig[:, 0:NT]
            for c in range(DC):
                slq = slice(c * NT, (c + 1) * NT)
                vec.tensor_tensor_scan(tY[:, slq], tX[:, slq], zerosb[:, 0:NT],
                                       lastA[:, c:c + 1], ALU.add, ALU.add)
                vec.tensor_copy(lastA[:, c:c + 1], tY[:, c * NT + NT - 1:c * NT + NT])
                vec.tensor_tensor(u1, tX[:, slq], tY[:, slq], ALU.max)
                vec.scalar_tensor_tensor(u1, u1, THF, bmt[:, 0:NT],
                                         ALU.is_ge, ALU.mult)
                vec.tensor_reduce(scr8[:, c:c + 1], u1, mybir.AxisListType.X, ALU.max)
                vec.tensor_tensor(u1, tX[:, slq], tY[:, slq], ALU.min)
                vec.scalar_tensor_tensor(u1, u1, -THF, bmt[:, 0:NT],
                                         ALU.is_le, ALU.mult)
                vec.tensor_reduce(scr8b[:, c:c + 1], u1, mybir.AxisListType.X, ALU.max)
            vec.tensor_tensor(maxP[:], maxP[:], scr8[:], ALU.max)
            vec.tensor_tensor(maxN[:], maxN[:], scr8b[:], ALU.max)
            vdone()
            inv16_tile(k, ct2, ct1)
            vec.scalar_tensor_tensor(ccb[:, 0:NT], cse[:, 0:NT], C1,
                                     ct2[:, 0:NT], ALU.mult, ALU.mult)
            vec.tensor_tensor_scan(ct1[:, 0:NT], ccb[:, 0:NT], zerosb[0:16, 0:NT],
                                   c16[:, 1:2], ALU.add, ALU.add)
            vec.tensor_copy(c16[:, 1:2], ct1[:, NT - 1:NT])
            vec.tensor_tensor(ct2[:, 0:NT], ccb[:, 0:NT], ct1[:, 0:NT], ALU.max)
            vec.scalar_tensor_tensor(ct2[:, 0:NT], ct2[:, 0:NT], THF,
                                     bmt[0:16, 0:NT], ALU.is_ge, ALU.mult)
            vec.tensor_reduce(c16[:, 4:5], ct2[:, 0:NT], mybir.AxisListType.X, ALU.max)
            vec.tensor_tensor(c16[:, 6:7], c16[:, 6:7], c16[:, 4:5], ALU.max)
            vdone()
            tbar()

        ten.wait_ge(s_ve, cnt.val(s_ve))
        for (src_cols, off) in ((lastA[:], 0), (maxP[:], 1024), (maxN[:], 2048)):
            ten.transpose(ps2[0:8, 0:128], src_cols, ident[:]).then_inc(s_pe, 1)
            cnt.inc(s_pe, 1)
            sca.wait_ge(s_pe, cnt.val(s_pe))
            sca.wait_ge(s_la, cnt.val(s_la))
            sca.activation(hrow[:], ps2[0:8, 0:128], ACTF.Copy)
            sync.wait_ge(s_se, cnt.val(s_se))
            sync.dma_start(out=ag_in[3][:, off:off + 1024].rearrange("o (r c) -> r (o c)", r=8),
                           in_=hrow[:]).then_inc(s_la, 16)
            cnt.inc(s_la, 16)
            ten.wait_ge(s_se, cnt.val(s_se))
        sync.dma_start(out=ag_in[3][:, 3072:3088].rearrange("o (w u) -> (o w) u", u=1),
                       in_=c16[:, 1:2]).then_inc(s_la, 16)
        cnt.inc(s_la, 16)
        sync.dma_start(out=ag_in[3][:, 3088:3104].rearrange("o (w u) -> (o w) u", u=1),
                       in_=c16[:, 6:7]).then_inc(s_la, 16)
        cnt.inc(s_la, 16)
        gp.wait_ge(s_la, cnt.val(s_la))
        gp.collective_compute("AllGather", ALU.bypass,
                              replica_groups=[list(range(8))],
                              ins=[ag_in[3].opt()],
                              outs=[ag_out[3].opt()]).then_inc(s_ag, 1)
        cnt.inc(s_ag, 1)
        sync.wait_ge(s_ag, cnt.val(s_ag))
        gselv = sc[0:8, 0:AG_W]
        sync.dma_start(out=gselv, in_=ag_out[3]).then_inc(s_la, 16)
        cnt.inc(s_la, 16)
        ten.wait_ge(s_la, cnt.val(s_la))
        for (off, ccol0) in ((0, 24), (1024, 32), (2048, 40)):
            ten.matmul(ps2[0:1, 0:NT], onehot8[:], gselv[:, off:off + NT],
                       start=True, stop=True)
            mi = ten.matmul(ps2[32:33, 0:NT], onehot8[:], gselv[:, off + NT:off + 2 * NT],
                            start=True, stop=True)
            mi.then_inc(s_pe, 1); cnt.inc(s_pe, 1)
            sca.wait_ge(s_pe, cnt.val(s_pe))
            sca.activation(svA[0:1, 0:NT], ps2[0:1, 0:NT], ACTF.Copy)
            si = sca.activation(svA[32:33, 0:NT], ps2[32:33, 0:NT], ACTF.Copy)
            ten.wait_ge(s_se, cnt.val(s_se))
            for c in range(DC):
                rsrc = (svA[0:1, c * 128:(c + 1) * 128] if c < 4 else
                        svA[32:33, (c - 4) * 128:(c - 3) * 128])
                idk = ones2d[0:1, 0:1] if c < 4 else ones2d[32:33, 0:1]
                ten.transpose(ps3[:, 0:1], rsrc, idk).then_inc(s_pe, 1)
                cnt.inc(s_pe, 1)
                vec.wait_ge(s_pe, cnt.val(s_pe))
                vec.tensor_copy(carry[:, ccol0 + c:ccol0 + c + 1], ps3[:, 0:1])
                vdone()
                ten.wait_ge(s_ve, cnt.val(s_ve))
        ten.matmul(ps2[0:1, 0:16], onehot8[:], gselv[:, 3072:3088], start=True, stop=True)
        mi = ten.matmul(ps2[32:33, 0:16], onehot8[:], gselv[:, 3088:3104],
                        start=True, stop=True)
        mi.then_inc(s_pe, 1); cnt.inc(s_pe, 1)
        sca.wait_ge(s_pe, cnt.val(s_pe))
        sca.activation(svA[0:1, 0:16], ps2[0:1, 0:16], ACTF.Copy)
        si = sca.activation(svA[32:33, 0:16], ps2[32:33, 0:16], ACTF.Copy)
        ten.wait_ge(s_se, cnt.val(s_se))
        ten.transpose(ps3[0:16, 0:1], svA[0:1, 0:16], ones2d[0:1, 0:1]).then_inc(s_pe, 1)
        cnt.inc(s_pe, 1)
        vec.wait_ge(s_pe, cnt.val(s_pe))
        vec.tensor_copy(c16[:, 0:1], ps3[0:16, 0:1])
        vdone()
        ten.wait_ge(s_ve, cnt.val(s_ve))
        ten.transpose(ps3[0:16, 0:1], svA[32:33, 0:16], ones2d[32:33, 0:1]).then_inc(s_pe, 1)
        cnt.inc(s_pe, 1)
        vec.wait_ge(s_pe, cnt.val(s_pe))
        vec.tensor_copy(c16[:, 2:3], ps3[0:16, 0:1])
        vdone()
        bar()

        if upto == "front":
            _finalize(nc, sync, cnt, s_st, dbg, debug,
                      [("d_x2", x2_s, 0), ("d_x3", x3_s, 0), ("d_ekv", ekv_s, 0)])
            return nc

        wload(arH, 0, woT)
        ten.wait_ge(s_gd, cnt.val(s_gd))
        vec.tensor_copy(lastP[:], carry[:, 24:32])
        vec.tensor_copy(chainP[:], carry[:, 32:40])
        vec.tensor_copy(chainN[:], carry[:, 40:48])
        vec.tensor_copy(c16[:, 3:4], c16[:, 0:1])
        vec.tensor_copy(c16[:, 4:5], c16[:, 2:3])
        vec.memset(c16[:, 7:8], -LN2_64)
        vdone()
        bar()
        for k in range(KT):
            vec.wait_ge(s_st, cnt.val(s_st))
            sca.wait_ge(s_st, cnt.val(s_st))
            sync.dma_start(out=cse[:, 0:NT], in_=se_s[:, k * NT:(k + 1) * NT]).then_inc(s_la, 16)
            cnt.inc(s_la, 16)
            sync.dma_start(out=sv(tZ, NT, 0, DC, 0, NT),
                           in_=hv(ekv_s, k * NT, NT)).then_inc(s_h, 16)
            cnt.inc(s_h, 16)
            sync.dma_start(out=sv(tX, NT, 0, DC, 0, NT),
                           in_=hv(x3_s, k * NT, NT)).then_inc(s_ld, 16)
            cnt.inc(s_ld, 16)
            sync.dma_start(out=sv(tB16b, NT, 0, DC, 0, NT),
                           in_=hv(r_s, k * NT, NT)).then_inc(s_ld, 16)
            cnt.inc(s_ld, 16)
            vec.wait_ge(s_la, cnt.val(s_la))
            sca.wait_ge(s_la, cnt.val(s_la))
            vec.tensor_scalar(bmt[:, 0:NT], iot[:, 0:NT], -1.0,
                              bigk[:, k:k + 1], ALU.mult, ALU.add)
            inv16_tile(k, ct2, ct1)
            vec.scalar_tensor_tensor(ccb[:, 0:NT], cse[:, 0:NT], C1,
                                     ct2[:, 0:NT], ALU.mult, ALU.mult)
            vec.tensor_tensor_scan(ct2[:, 0:NT], ccb[:, 0:NT], zerosb[0:16, 0:NT],
                                   c16[:, 3:4], ALU.add, ALU.add)
            vec.tensor_tensor(ct1[:, 0:NT], ccb[:, 0:NT], ct2[:, 0:NT], ALU.max)
            vec.scalar_tensor_tensor(ct1[:, 0:NT], ct1[:, 0:NT], THF,
                                     bmt[0:16, 0:NT], ALU.is_ge, ALU.mult)
            vec.tensor_tensor_scan(ccb[:, 0:NT], ct1[:, 0:NT], zerosb[0:16, 0:NT],
                                   c16[:, 4:5], ALU.max, ALU.max)
            vec.tensor_copy(crd[:, 1:NT], ct2[:, 0:NT - 1])
            vec.tensor_copy(crd[:, 0:1], c16[:, 3:4])
            vec.scalar_tensor_tensor(crd[:, 1:NT], ccb[:, 0:NT - 1], 0.0,
                                     crd[:, 1:NT], ALU.is_le, ALU.mult)
            vec.scalar_tensor_tensor(crd[:, 0:1], c16[:, 4:5], 0.0,
                                     crd[:, 0:1], ALU.is_le, ALU.mult)
            vec.tensor_copy(c16[:, 3:4], ct2[:, NT - 1:NT])
            vec.tensor_copy(c16[:, 4:5], ccb[:, NT - 1:NT])
            vec.tensor_scalar(ct1[:, 0:NT], iot[0:16, 0:NT], la16[:, 0:1],
                              b16k[:, KT + k:KT + k + 1], ALU.mult, ALU.add)
            vdone()
            sca.wait_ge(s_ve, cnt.val(s_ve))
            si = sca.activation(ccb[:, 0:NT], ct1[:, 0:NT], ACTF.Exp)
            vec.wait_ge(s_se, cnt.val(s_se))
            vec.scalar_tensor_tensor(crd[:, 0:NT], ccb[:, 0:NT], C_RESC,
                                     crd[:, 0:NT], ALU.mult, ALU.mult)
            vec.scalar_tensor_tensor(crd[:, 0:NT], cse[:, 0:NT], la16[:, 2:3],
                                     crd[:, 0:NT], ALU.mult, ALU.add)
            vec.tensor_scalar(crd[:, 0:NT], crd[:, 0:NT], 1e-6, None, ALU.max)
            vec.tensor_scalar(crd[:, 0:NT], crd[:, 0:NT], 3.3e38, None, ALU.min)
            vdone()
            sca.wait_ge(s_ve, cnt.val(s_ve))
            sca.activation(ct1[:, 0:NT], crd[:, 0:NT], ACTF.Ln, scale=WKV_SC * WKV_SC)
            si = sca.activation(crd[:, 0:NT], ct1[:, 0:NT], ACTF.Exp, scale=-1.0,
                                bias=c16[:, 7:8])
            vec.wait_ge(s_h, cnt.val(s_h))
            sca.wait_ge(s_h, cnt.val(s_h))
            inv_alpha_tile(k, sc, tY)
            vec.scalar_tensor_tensor(sc[:, 0:DC * NT], tZ[:, 0:DC * NT], C1,
                                     sc[:, 0:DC * NT], ALU.mult, ALU.mult)
            vec.wait_ge(s_pe, cnt.val(s_pe))

            def wkv_post(c):
                slq = slice(c * NT, (c + 1) * NT)
                vec.wait_ge(s_se, se_mark[c])
                vec.scalar_tensor_tensor(sc[:, slq], tY[:, slq], C_RESC,
                                         sc[:, slq], ALU.mult, ALU.mult)
                vec.scalar_tensor_tensor(ps2[:, 0:NT], tY[:, slq], 0.0,
                                         ps2[:, 0:NT], ALU.is_gt, ALU.mult)
                vec.scalar_tensor_tensor(sc[:, slq], ps2[:, 0:NT], -FMAX,
                                         sc[:, slq], ALU.mult, ALU.add)
                vec.scalar_tensor_tensor(sc[:, slq], tZ[:, slq],
                                         patB[:, QEU + c:QEU + c + 1],
                                         sc[:, slq], ALU.mult, ALU.add)

            se_mark = {}
            for c in range(DC):
                slq = slice(c * NT, (c + 1) * NT)
                u1 = tY[:, ((c + 1) % DC) * NT:((c + 1) % DC) * NT + NT]
                u2 = tY[:, ((c + 2) % DC) * NT:((c + 2) % DC) * NT + NT]
                u4 = tY[:, ((c + 3) % DC) * NT:((c + 3) % DC) * NT + NT]
                u3 = ps2[:, 0:NT]
                vec.tensor_tensor_scan(tY[:, slq], sc[:, slq], zerosb[:, 0:NT],
                                       lastP[:, c:c + 1], ALU.add, ALU.add)
                vec.tensor_tensor(u1, sc[:, slq], tY[:, slq], ALU.max)
                vec.scalar_tensor_tensor(u1, u1, THF, bmt[:, 0:NT],
                                         ALU.is_ge, ALU.mult)
                vec.tensor_tensor_scan(u2, u1, zerosb[:, 0:NT],
                                       chainP[:, c:c + 1], ALU.max, ALU.max)
                if c >= 1:
                    wkv_post(c - 1)
                vec.tensor_tensor(u3, sc[:, slq], tY[:, slq], ALU.min)
                vec.scalar_tensor_tensor(u3, u3, -THF, bmt[:, 0:NT],
                                         ALU.is_le, ALU.mult)
                vec.tensor_tensor_scan(u4, u3, zerosb[:, 0:NT],
                                       chainN[:, c:c + 1], ALU.max, ALU.max)
                vec.tensor_tensor(u3[:, 1:NT], u4[:, 0:NT - 1],
                                  u2[:, 0:NT - 1], ALU.is_gt)
                vec.tensor_tensor(u3[:, 0:1], chainN[:, c:c + 1],
                                  chainP[:, c:c + 1], ALU.is_gt)
                vec.tensor_tensor(u1[:, 1:NT], u2[:, 0:NT - 1],
                                  u4[:, 0:NT - 1], ALU.max)
                vec.tensor_tensor(u1[:, 0:1], chainP[:, c:c + 1],
                                  chainN[:, c:c + 1], ALU.max)
                vec.tensor_copy(chainP[:, c:c + 1], u2[:, NT - 1:NT])
                vec.tensor_copy(chainN[:, c:c + 1], u4[:, NT - 1:NT])
                vec.tensor_copy(sc[:, c * NT + 1:(c + 1) * NT],
                                tY[:, c * NT:c * NT + NT - 1])
                vec.tensor_copy(sc[:, c * NT:c * NT + 1], lastP[:, c:c + 1])
                vec.tensor_copy(lastP[:, c:c + 1], tY[:, c * NT + NT - 1:c * NT + NT])
                vec.scalar_tensor_tensor(sc[:, slq], u1, 0.0, sc[:, slq],
                                         ALU.is_le, ALU.mult)
                sca.wait_ge(s_ve, cnt.val(s_ve))
                si = sca.activation(tY[:, slq], iot[:, 0:NT],
                                    ACTF.Exp, scale=patB[:, QLA + c:QLA + c + 1],
                                    bias=bpow[:, c * KT + k:c * KT + k + 1])
                se_mark[c] = cnt.val(s_se)
            wkv_post(DC - 1)
            vdone()
            if debug:
                sync.wait_ge(s_ve, cnt.val(s_ve))
                sync.dma_start(out=hv(dbg["d_num"], k * NT, NT),
                               in_=sv(sc, NT, 0, DC, 0, NT)).then_inc(s_st, 16)
                cnt.inc(s_st, 16)
                vec.wait_ge(s_st, cnt.val(s_st))
            vec.wait_ge(s_ld, cnt.val(s_ld))
            ten.wait_ge(s_se, cnt.val(s_se))
            ten.wait_ge(s_ve, cnt.val(s_ve))
            banks4b = [ps2[:, 0:NT], ps3[:, 0:NT],
                       psBig[:, 0:NT], psBig[:, NT:2 * NT]]
            bdone = []
            for c in range(DC):
                bank = banks4b[c % 4]
                if c >= 4:
                    ten.wait_ge(s_ve, bdone[c - 4])
                mi = ten.matmul(bank, selT[:, c * 128:(c + 1) * 128],
                                crd[:, 0:NT], start=True, stop=True)
                mi.then_inc(s_pe, 1); cnt.inc(s_pe, 1)
                vec.wait_ge(s_pe, cnt.val(s_pe))
                slq = slice(c * NT, (c + 1) * NT)
                vec.scalar_tensor_tensor(sc[:, slq], sc[:, slq], WKV_SC,
                                         bank, ALU.mult, ALU.mult)
                vec.tensor_scalar(sc[:, slq], sc[:, slq], WKV_CL, -WKV_CL,
                                  ALU.min, ALU.max)
                vec.tensor_tensor(tY[:, slq], sc[:, slq], tB16b[:, slq], ALU.mult)
                vec.tensor_scalar(tY[:, slq], tY[:, slq], RW_SC, 2.656e35,
                                  ALU.mult, ALU.min)
                vec.tensor_scalar(tB16[:, slq], tY[:, slq], -2.656e35, None, ALU.max)
                bdone.append(cnt.val(s_ve))
            if debug:
                sync.wait_ge(s_ve, cnt.val(s_ve))
                sync.dma_start(out=hv(dbg["d_wkv"], k * NT, NT),
                               in_=sv(sc, NT, 0, DC, 0, NT)).then_inc(s_st, 16)
                cnt.inc(s_st, 16)
                sync.dma_start(out=dbg["d_rdn"][:, k * NT:(k + 1) * NT],
                               in_=crd[:, 0:NT]).then_inc(s_st, 16)
                cnt.inc(s_st, 16)
                vec.wait_ge(s_st, cnt.val(s_st))
                sca.wait_ge(s_st, cnt.val(s_st))
            def evac_att(e, bank):
                vi = vec.tensor_scalar(tZ[:, e * NT:(e + 1) * NT], bank[:, 0:NT],
                                       2.656e36, -2.656e36, ALU.min, ALU.max)
            proj_l1(0, tB16, evac_att, eng=vec)
            if debug:
                vdone()
                sync.wait_ge(s_ve, cnt.val(s_ve))
                sync.dma_start(out=hv(dbg["d_att"], k * NT, NT),
                               in_=sv(tZ, NT, 0, DC, 0, NT)).then_inc(s_st, 16)
                cnt.inc(s_st, 16)
                vec.wait_ge(s_st, cnt.val(s_st))
            vec.scalar_tensor_tensor(tY[:, 0:DC * NT], tZ[:, 0:DC * NT], 128.0,
                                     tX[:, 0:DC * NT], ALU.mult, ALU.add)
            vdone()
            ln_stats_norm(NT, tY, tZ, armored=True)
            vdone()
            sync.wait_ge(s_ve, cnt.val(s_ve))
            sync.dma_start(out=hv(x4_s, k * NT, NT),
                           in_=sv(tZ, NT, 0, DC, 0, NT)).then_inc(s_st, 16)
            cnt.inc(s_st, 16)
            tbar()

        if upto == "x4":
            _finalize(nc, sync, cnt, s_st, dbg, debug,
                      [("d_x2", x2_s, 0), ("d_x3", x3_s, 0), ("d_x4", x4_s, 0)])
            return nc

        for oc in range(DC):
            vec.wait_ge(s_st, cnt.val(s_st))
            sca.wait_ge(s_st, cnt.val(s_st))
            sync.wait_ge(s_ld, cnt.val(s_ld))
            sync.dma_start(out=tX[:, 0:D], in_=l1fw[oc * 128:(oc + 1) * 128, :]).then_inc(s_ld, 16)
            cnt.inc(s_ld, 16)
            vec.wait_ge(s_ld, cnt.val(s_ld))
            vec.tensor_reduce(scr8[:, 0:1], tX[:, 0:D], mybir.AxisListType.X,
                              ALU.add, apply_absolute_value=True)
            vec.tensor_scalar(scr8[:, 0:1], scr8[:, 0:1], 1.0 / D, None, ALU.mult)
            vec.tensor_scalar(scr8[:, 0:1], scr8[:, 0:1], 1e-5, None, ALU.max)
            vdone()
            sca.wait_ge(s_ve, cnt.val(s_ve))
            sca.activation(scr8[:, 1:2], scr8[:, 0:1], ACTF.Ln)
            si = sca.activation(scr8[:, 1:2], scr8[:, 1:2], ACTF.Exp, scale=-1.0)
            vec.wait_ge(s_se, cnt.val(s_se))
            vec.tensor_scalar(tY[:, 0:D], tX[:, 0:D], scr8[:, 1:2], None, ALU.mult)
            vec.tensor_scalar(tZ[:, 0:D], tY[:, 0:D], 0.5, 1.0, ALU.is_gt, ALU.mult)
            vec.tensor_scalar(tX[:, 0:D], tY[:, 0:D], -0.5, 1.0, ALU.is_lt, ALU.mult)
            vec.tensor_tensor(tZ[:, 0:D], tZ[:, 0:D], tX[:, 0:D], ALU.subtract)
            vec.tensor_scalar(tB16[:, 0:D], tZ[:, 0:D], scr8[:, 0:1], None, ALU.mult)
            vdone()
            ten.wait_ge(s_ve, cnt.val(s_ve))
            sca.wait_ge(s_h, cnt.val(s_h))
            for ic in range(DC):
                ten.transpose(psT[:, 0:128],
                              tB16[:, ic * 128:(ic + 1) * 128], identb[:]).then_inc(s_pe, 1)
                cnt.inc(s_pe, 1)
                sca.wait_ge(s_pe, cnt.val(s_pe))
                si = sca.activation(tB16b[:, ic * 128:(ic + 1) * 128], psT[:, 0:128],
                                    ACTF.Copy)
                ten.wait_ge(s_se, cnt.val(s_se))
            sync.wait_ge(s_se, cnt.val(s_se))
            sync.dma_start(out=wqT_s.rearrange("(c p) e -> p c e", p=128)[:, :, oc * 128:(oc + 1) * 128],
                           in_=tB16b[:, 0:D].rearrange("p (c e) -> p c e", e=128)).then_inc(s_h, 16)
            cnt.inc(s_h, 16)
            tbar()
        sync.dma_start(out=arH[:, 0:8 * D].rearrange("p (c e) -> p c e", e=D),
                       in_=wqT_s.rearrange("(c p) e -> p c e", p=128)).then_inc(s_ld, 16)
        cnt.inc(s_ld, 16)
        wload(arH, 8 * D, hadc)
        bar()

        for e in (vec, sca, ten):
            e.wait_ge(s_ld, cnt.val(s_ld))
        for k in range(KT):
            vec.wait_ge(s_st, cnt.val(s_st))
            sca.wait_ge(s_st, cnt.val(s_st))
            sync.dma_start(out=sv(tX, NT, 0, DC, 0, NT),
                           in_=hv(x4_s, k * NT, NT)).then_inc(s_ld, 16)
            cnt.inc(s_ld, 16)
            vec.wait_ge(s_ld, cnt.val(s_ld))
            vec.tensor_copy(tB16[:, 0:DC * NT], tX[:, 0:DC * NT])
            vdone()
            ten.wait_ge(s_ve, cnt.val(s_ve))

            def p12_quant(tg):
                slq = slice(tg * 1024, tg * 1024 + 1024)
                vec.wait_ge(s_se, se_done[tg])
                vec.tensor_reduce(scr8[:, 0:1], tY[:, slq], mybir.AxisListType.X,
                                  ALU.max, apply_absolute_value=True)
                vec.tensor_scalar(scr8[:, 0:1], scr8[:, 0:1], 1e-5, None, ALU.max)
                vec.tensor_scalar(scr8[:, 1:2], scr8[:, 0:1], 1.0 / 7.0, None, ALU.mult)
                vec.reciprocal(scr8[:, 2:3], scr8[:, 1:2])
                vec.tensor_scalar(tZ[:, slq], tY[:, slq], scr8[:, 2:3], MAGIC,
                                  ALU.mult, ALU.add)
                vec.tensor_scalar(tZ[:, slq], tZ[:, slq], MAGIC, None, ALU.subtract)
                vec.tensor_scalar(tZ[:, slq], tZ[:, slq], -8.0, 7.0, ALU.max, ALU.min)
                vec.tensor_scalar(tB16b[:, slq], tZ[:, slq], scr8[:, 1:2], None,
                                  ALU.mult)
                return cnt.val(s_ve)

            def p12_transpose(tg):
                ten.wait_ge(s_ve, q_mark[tg])
                for oc in range(DC):
                    mi = ten.transpose(psT[:, (oc % 4) * 128:(oc % 4) * 128 + 128],
                                       tB16b[:, tg * 1024 + oc * 128: tg * 1024 + oc * 128 + 128],
                                       identb[:])
                    if oc % 4 == 3:
                        mi.then_inc(s_pe, 1); cnt.inc(s_pe, 1)
                        sca.wait_ge(s_pe, cnt.val(s_pe))
                        for j in range(4):
                            ocx = oc - 3 + j
                            si = sca.activation(
                                tB16[:, ocx * NT + tg * 128: ocx * NT + tg * 128 + 128],
                                psT[:, j * 128:j * 128 + 128], ACTF.Copy)
                        ten.wait_ge(s_se, cnt.val(s_se))

            se_done = []; q_mark = {}
            for tg in range(NT // 128):
                if tg >= 1:
                    p12_transpose(tg - 1)
                    ten.wait_ge(s_se, se_done[tg - 1])
                else:
                    ten.wait_ge(s_se, cnt.val(s_se))
                for nh in range(2):
                    bank = psAB[nh]
                    for kc in range(DC):
                        mi = ten.matmul(bank[:, 0:NT],
                                        tB16[:, kc * NT + tg * 128: kc * NT + tg * 128 + 128],
                                        arH[:, (8 + kc) * D + nh * NT: (8 + kc) * D + nh * NT + NT],
                                        start=(kc == 0), stop=(kc == DC - 1))
                mi.then_inc(s_pe, 1); cnt.inc(s_pe, 1)
                sca.wait_ge(s_pe, cnt.val(s_pe))
                sca.activation(tY[:, tg * 1024:tg * 1024 + NT], ps0[:, 0:NT], ACTF.Copy)
                si = sca.activation(tY[:, tg * 1024 + NT:tg * 1024 + 2 * NT],
                                    ps1[:, 0:NT], ACTF.Copy)
                se_done.append(cnt.val(s_se))
                q_mark[tg] = p12_quant(tg)
            p12_transpose(NT // 128 - 1)
            def evac_f(e, bank):
                si = sca.activation(tZ[:, e * NT:(e + 1) * NT], bank[:, 0:NT],
                                    ACTF.Identity)
            se_b = cnt.val(s_se)
            for e in range(DC):
                bank = psAB[e % 2]
                if e >= 2:
                    ten.wait_ge(s_se, se_b + e - 1)
                else:
                    ten.wait_ge(s_se, se_b)
                for kc in range(DC):
                    mi = ten.matmul(bank[:, 0:NT],
                                    arH[:, kc * D + e * 128: kc * D + e * 128 + 128],
                                    tB16[:, kc * NT: kc * NT + NT],
                                    start=(kc == 0), stop=(kc == DC - 1))
                mi.then_inc(s_pe, 1); cnt.inc(s_pe, 1)
                sca.wait_ge(s_pe, cnt.val(s_pe))
                evac_f(e, bank)
            vec.wait_ge(s_se, cnt.val(s_se))
            vec.tensor_tensor(tY[:, 0:DC * NT], tZ[:, 0:DC * NT], tX[:, 0:DC * NT],
                              ALU.add)
            vdone()
            ln_stats_norm(NT, tY, tZ)
            vdone()
            sync.wait_ge(s_ve, cnt.val(s_ve))
            sync.dma_start(out=hv(out_t, k * NT, NT),
                           in_=sv(tZ, NT, 0, DC, 0, NT)).then_inc(s_st, 16)
            cnt.inc(s_st, 16)
            tbar()

        _finalize(nc, sync, cnt, s_st, dbg, debug,
                  [("d_x2", x2_s, 0), ("d_x3", x3_s, 0), ("d_x4", x4_s, 0)])
    return nc


def _finalize(nc, sync, cnt, s_st, dbg, debug, dumps):
    if debug:
        for nm, src, _ in dumps:
            if nm in dbg:
                sync.dma_start(out=dbg[nm], in_=src).then_inc(s_st, 16)
                cnt.inc(s_st, 16)
    sync.wait_ge(s_st, cnt.val(s_st))
    nc.all_engine_barrier()


def _hadamard_const():
    h = np.array([[1.0]], dtype=np.float64)
    while h.shape[0] < D:
        h = np.block([[h, h], [h, -h]])
    return np.ascontiguousarray((h / np.sqrt(D)).astype(np.float32))


def _shard_inputs(inputs):
    f = np.float32
    xs = np.asarray(inputs["L0_attn_hidden_states"], f)
    tr = lambda a: np.ascontiguousarray(np.asarray(a, f).T)
    cc = lambda a: np.ascontiguousarray(np.asarray(a, f))
    common = dict(
        wkT=tr(inputs["L0_attn_Wk"]), wvT=tr(inputs["L0_attn_Wv"]),
        wkaT=tr(inputs["L0_attn_Wka"]),
        w1=cc(inputs["L0_ffn_W1"]), w2=cc(inputs["L0_ffn_W2"]),
        wkeT=tr(inputs["L1_attn_W_key"]), wvaT=tr(inputs["L1_attn_W_value"]),
        wreT=tr(inputs["L1_attn_W_receptance"]), woT=tr(inputs["L1_attn_W_output"]),
        l1fw=cc(inputs["L1_ffn_weight"]),
        bk=cc(inputs["L0_attn_bk"]), bv=cc(inputs["L0_attn_bv"]),
        bka=cc(inputs["L0_attn_bka"]),
        gamma=cc(inputs["L0_attn_gamma"]), beta=cc(inputs["L0_attn_beta"]),
        b1=cc(inputs["L0_ffn_b1"]), b2=cc(inputs["L0_ffn_b2"]),
        theta=cc(inputs["L0_ffn_quantum_phases"]),
        qwr=cc(inputs["L0_ffn_quantum_weights_real"]),
        qwi=cc(inputs["L0_ffn_quantum_weights_imag"]),
        mixk=cc(np.asarray(inputs["L1_attn_mix_k"], f).ravel()),
        mixv=cc(np.asarray(inputs["L1_attn_mix_v"], f).ravel()),
        mixr=cc(np.asarray(inputs["L1_attn_mix_r"], f).ravel()),
        td=cc(inputs["L1_attn_time_decay"]), tf=cc(inputs["L1_attn_time_first"]),
        hadc=_hadamard_const(),
    )
    in_maps = []
    for c in range(8):
        b, half = c // 2, c % 2
        sel = np.zeros((1, 8), f)
        if half == 1:
            sel[0, c - 1] = 1.0
        m = dict(common)
        m["x_in"] = np.ascontiguousarray(xs[b, half * T:(half + 1) * T, :].T)
        m["selrow"] = sel
        m["t0c"] = np.array([[float(half * T)]], f)
        in_maps.append(m)
    return in_maps


def kernel(**inputs):
    in_maps = _shard_inputs(inputs)
    nc = build_program(debug=False, upto="out")
    res = run_bass_kernel_spmd(nc, in_maps, core_ids=list(range(8)))
    out = np.empty((B, S, D), np.float32)
    for c in range(8):
        b, half = c // 2, c % 2
        out[b, half * T:(half + 1) * T, :] = res.results[c]["out"].T
    return out
